# version 01
# baseline (speedup 1.0000x reference)
import numpy as np
import ml_dtypes
import concourse.bass as bass
import concourse.mybir as mybir
from concourse.bass_utils import run_bass_kernel_spmd

F32 = mybir.dt.float32
BF16 = mybir.dt.bfloat16
AF = mybir.ActivationFunctionType
ALU = mybir.AluOpType
AX = mybir.AxisListType.X

NCORES = 8
T = 16384
TC = T // NCORES
C = 1024
CCH = C // 128
NI = 3072
ICH = NI // 128
NTT = TC // 512
WH = TC + 2
MAGIC = float(np.float32(1.5 * 2 ** 23))
EPS = 1e-5
BF = ml_dtypes.bfloat16
ENGS = ("sync", "tensor", "vector", "scalar", "gpsimd")


class Sem:
    def __init__(self, nc, name):
        self.h = nc.alloc_semaphore(name)
        self.n = 0
        self.name = name


_CHAIN_SKIP = {"dma_start", "wait_ge", "wait_op", "sem_inc", "then_inc",
               "dma_start_transpose", "indirect_dma_start", "engine_nop",
               "value_load", "reg_mov", "nop"}


class Chained:

    def __init__(self, eng, sem):
        self._e = eng
        self._s = sem

    def __getattr__(self, k):
        a = getattr(self._e, k)
        if not callable(a) or k in _CHAIN_SKIP or k.startswith("_"):
            return a

        def wrap(*args, **kw):
            r = a(*args, **kw)
            if hasattr(r, "then_inc"):
                self._s.n += 1
                r.then_inc(self._s.h, 1)
                self._e.wait_ge(self._s.h, self._s.n)
            return r
        return wrap


class Prog:

    def __init__(self, nc):
        self.nc = nc
        self.steps = {e: [] for e in ENGS}
        self.chains = {e: Sem(nc, f"chain_{e}") for e in ("vector", "scalar", "gpsimd")}

    def sem(self, name):
        return Sem(self.nc, name)

    def on(self, eng, fn, wait=(), inc=None, k=1):
        for s, v in wait:
            assert v <= s.n, f"{eng}: wait {s.name}>={v} but only {s.n} reserved"
        if inc is not None:
            inc.n += k
        self.steps[eng].append((fn, tuple(wait), inc, k))

    def emit(self):
        def mk(ename):
            chain = self.chains.get(ename)

            def body(eng):
                ceng = Chained(eng, chain) if chain is not None else eng
                for fn, waits, inc, k in self.steps[ename]:
                    for s, v in waits:
                        eng.wait_ge(s.h, v)
                    ins = fn(ceng)
                    if inc is not None:
                        if chain is not None and k != 16:
                            eng.sem_inc(inc.h, k)
                        else:
                            assert ins is not None, ename
                            ins.then_inc(inc.h, k)
            return body

        with self.nc.Block() as block:
            block.sync(mk("sync"))
            block.tensor(mk("tensor"))
            block.vector(mk("vector"))
            block.scalar(mk("scalar"))
            block.gpsimd(mk("gpsimd"))


class Arena:

    def __init__(self, nc, name, nbytes):
        self.t = nc.alloc_sbuf_tensor(name, [128, nbytes], mybir.dt.uint8)
        self.off = 0
        self.nbytes = nbytes

    def carve(self, shape, dtype):
        esz = mybir.dt.size(dtype)
        n = int(np.prod(shape)) * esz
        off = (self.off + 511) // 512 * 512
        assert off + n <= self.nbytes, f"arena overflow {off}+{n} > {self.nbytes}"
        self.off = off + n
        ap = self.t.ap()[:, off:off + n].bitcast(dtype)
        if len(shape) > 1:
            dims = " ".join(f"d{i}" for i in range(len(shape)))
            kw = {f"d{i}": s for i, s in enumerate(shape)}
            ap = ap.rearrange(f"p ({dims}) -> p {dims}", **kw)
        return ap

    def reset(self, off=0):
        self.off = off


def exp_poly(eng, out_ap, x_ap, tmp_ap):
    eng.tensor_scalar(out=tmp_ap, in0=x_ap, scalar1=1.0 / 6.0, scalar2=0.5,
                      op0=ALU.mult, op1=ALU.add)
    eng.tensor_tensor(out=tmp_ap, in0=tmp_ap, in1=x_ap, op=ALU.mult)
    eng.tensor_scalar(out=tmp_ap, in0=tmp_ap, scalar1=1.0, scalar2=None, op0=ALU.add)
    eng.tensor_tensor(out=tmp_ap, in0=tmp_ap, in1=x_ap, op=ALU.mult)
    return eng.tensor_scalar(out=out_ap, in0=tmp_ap, scalar1=1.0, scalar2=None,
                             op0=ALU.add)


def emit_absmax_export(P, col_ap, row_sb, scr_dram, out_dram, wait, bb, vv, xx):
    P.on("gpsimd", lambda e: e.dma_start(out=scr_dram, in_=col_ap),
         wait=wait, inc=bb, k=16)
    B1 = bb.n
    P.on("gpsimd", lambda e: e.dma_start(out=row_sb[0:1, 0:128],
                                         in_=scr_dram.unsqueeze(0)),
         wait=[(bb, B1)], inc=bb, k=16)
    B2 = bb.n
    P.on("vector", lambda e: e.reduce_max(out=row_sb[0:1, 128:129],
                                          in_=row_sb[0:1, 0:128], axis=AX),
         wait=[(bb, B2)], inc=vv)
    V1 = vv.n
    P.on("gpsimd", lambda e: e.dma_start(out=out_dram, in_=row_sb[0:1, 128:129]),
         wait=[(vv, V1)], inc=xx, k=16)


def cvec_ap(dram_ap):
    return dram_ap.rearrange("(a p) -> p a", p=128)


def fm(dram_ap):
    return dram_ap.rearrange("(a p) t -> p a t", p=128)



def build_la():
    nc = bass.Bass("TRN2", target_bir_lowering=False)
    xT = nc.dram_tensor("xT", [C, WH], F32, kind="ExternalInput").ap()
    mixk = nc.dram_tensor("mixk", [128, CCH], F32, kind="ExternalInput").ap()
    mixv = nc.dram_tensor("mixv", [128, CCH], F32, kind="ExternalInput").ap()
    mixr = nc.dram_tensor("mixr", [128, CCH], F32, kind="ExternalInput").ap()
    cvec = nc.dram_tensor("cvec", [128, CCH], F32, kind="ExternalInput").ap()
    wmod = nc.dram_tensor("wmod", [C, 6 * C], F32, kind="ExternalInput").ap()
    maskLR = nc.dram_tensor("maskLR", [128, 2], F32, kind="ExternalInput").ap()
    eye = nc.dram_tensor("eye", [64, 64], F32, kind="ExternalInput").ap()
    wup_sh = nc.dram_tensor("wup_sh", [NI // 8, C], F32, kind="ExternalInput").ap()
    wgate_sh = nc.dram_tensor("wgate_sh", [NI // 8, C], F32, kind="ExternalInput").ap()
    wdown_sh = nc.dram_tensor("wdown_sh", [C // 8, NI], F32, kind="ExternalInput").ap()

    h_o = nc.dram_tensor("h_o", [C, WH], BF16, kind="ExternalOutput").ap()
    u_o = nc.dram_tensor("u_o", [3, C // 2, TC], BF16, kind="ExternalOutput").ap()
    mod_o = nc.dram_tensor("mod_o", [128, 48], F32, kind="ExternalOutput").ap()
    wupd_o = nc.dram_tensor("wupd_o", [NI // 8, C], BF16, kind="ExternalOutput").ap()
    wgated_o = nc.dram_tensor("wgated_o", [NI // 8, C], BF16, kind="ExternalOutput").ap()
    wdownd_o = nc.dram_tensor("wdownd_o", [C // 8, NI], BF16, kind="ExternalOutput").ap()

    modscr = nc.dram_tensor("modscr", [48, 128], F32, kind="Internal").ap()
    rowscr = nc.dram_tensor("rowscr", [2, WH], F32, kind="Internal").ap()

    h_sb = nc.alloc_sbuf_tensor("h_sb", [128, CCH, WH], BF16)
    mask_sb = nc.alloc_sbuf_tensor("mask_sb", [128, 2], BF16)
    maskf_sb = nc.alloc_sbuf_tensor("maskf_sb", [128, 2], F32)
    mix_sb = nc.alloc_sbuf_tensor("mix_sb", [128, 3, CCH], F32)
    scf_sb = nc.alloc_sbuf_tensor("scf_sb", [128, CCH], F32)
    scf2_sb = nc.alloc_sbuf_tensor("scf2_sb", [128, CCH, 1], F32)
    tmp8_sb = nc.alloc_sbuf_tensor("tmp8_sb", [128, CCH], F32)
    sc_sb = nc.alloc_sbuf_tensor("sc_sb", [128, CCH, 1], BF16)
    m48_sb = nc.alloc_sbuf_tensor("m48_sb", [48, 128], F32)
    mod_sb = nc.alloc_sbuf_tensor("mod_sb", [128, 48], F32)
    A_sb = nc.alloc_sbuf_tensor("A_sb", [128, CCH], F32)
    eye_sb = nc.alloc_sbuf_tensor("eye_sb", [64, 64], F32)
    evrow_sb = nc.alloc_sbuf_tensor("evrow_sb", [1, 512], F32)
    xb_sb = nc.alloc_sbuf_tensor("xb_sb", [128, CCH, 512], BF16)
    x2b_sb = nc.alloc_sbuf_tensor("x2b_sb", [128, CCH, 512], BF16)
    stat_sb = nc.alloc_sbuf_tensor("stat_sb", [1, 2, WH], F32)
    brow_sb = nc.alloc_sbuf_tensor("brow_sb", [128, 2, WH], F32)
    ht1_sb = nc.alloc_sbuf_tensor("ht1_sb", [128, WH], F32)
    ht2_sb = nc.alloc_sbuf_tensor("ht2_sb", [128, WH], F32)
    wm_sb = nc.alloc_sbuf_tensor("wm_sb", [128, 2, 512], F32)
    wmb_sb = nc.alloc_sbuf_tensor("wmb_sb", [128, 2, 512], BF16)
    dqw_sb = nc.alloc_sbuf_tensor("dqw_sb", [128, 2, 1024], F32)
    dqo_sb = nc.alloc_sbuf_tensor("dqo_sb", [128, 2, 1024], BF16)
    dqi_sb = nc.alloc_sbuf_tensor("dqi_sb", [128, 32], F32)
    dqs_sb = nc.alloc_sbuf_tensor("dqs_sb", [128, 32], F32)

    Z1 = Arena(nc, "zone1", 68 * 1024)
    x_sb = Z1.carve([CCH, WH], F32)
    Z1.reset()
    u_sb = Z1.carve([3, 4, TC], BF16)
    dscr_sb = Z1.carve([TC], BF16)

    ps_mod = [nc.alloc_psum_tensor(f"ps_mod{i}", [1, 512], F32) for i in range(2)]
    ps_modB = [nc.alloc_psum_tensor(f"ps_modB{i}", [1, 256], F32) for i in range(2)]
    ps_sum = nc.alloc_psum_tensor("ps_sum", [1, 512], F32)
    ps_ssq = nc.alloc_psum_tensor("ps_ssq", [1, 512], F32)
    ps_tr = nc.alloc_psum_tensor("ps_tr", [64, 48], F32)

    P = Prog(nc)
    ld = P.sem("ld")
    wld = [P.sem(f"wld{i}") for i in range(2)]
    cst = P.sem("cst")
    mmod = P.sem("mmod")
    vmd = P.sem("vmd")
    bb = P.sem("bb")
    ss = P.sem("ss")
    vv = P.sem("vv")
    rr = P.sem("rr")
    hh = P.sem("hh")
    mst = P.sem("mst")
    sst = P.sem("sst")
    hrd = P.sem("hrd")
    urd = P.sem("urd")
    xx = P.sem("xx")
    dql = [P.sem(f"dql{i}") for i in range(2)]
    xdq = P.sem("xdq")
    dqp = P.sem("dqp")
    dqv = P.sem("dqv")

    P.on("sync", lambda e: e.dma_start(out=x_sb, in_=fm(xT)), inc=ld, k=16)
    P.on("sync", lambda e: e.dma_start(out=maskf_sb[:], in_=maskLR), inc=ld, k=16)
    for i, t in enumerate((mixk, mixv, mixr)):
        P.on("sync", lambda e, i=i, t=t: e.dma_start(out=mix_sb[:, i, :], in_=t),
             inc=ld, k=16)
    P.on("sync", lambda e: e.dma_start(out=scf_sb[:], in_=cvec), inc=ld, k=16)
    P.on("sync", lambda e: e.dma_start(out=eye_sb[:], in_=eye), inc=ld, k=16)
    LD_X = LD_MASK = LD_MIX = LD_C = LD_EYE = 112

    def fsilu(e):
        coef = [1.0 / 3628800, -1.0 / 362880, 1.0 / 40320, -1.0 / 5040,
                1.0 / 720, -1.0 / 120, 1.0 / 24, -1.0 / 6, 0.5, -1.0, 1.0]
        e.tensor_scalar(out=tmp8_sb[:], in0=scf_sb[:], scalar1=coef[0],
                        scalar2=coef[1], op0=ALU.mult, op1=ALU.add)
        for k in range(2, len(coef)):
            e.tensor_tensor(out=tmp8_sb[:], in0=tmp8_sb[:], in1=scf_sb[:], op=ALU.mult)
            e.tensor_scalar(out=tmp8_sb[:], in0=tmp8_sb[:], scalar1=coef[k],
                            scalar2=None, op0=ALU.add)
        e.tensor_scalar(out=tmp8_sb[:], in0=tmp8_sb[:], scalar1=1.0, scalar2=None,
                        op0=ALU.add)
        e.reciprocal(out=tmp8_sb[:], in_=tmp8_sb[:])
        e.tensor_mul(out=scf2_sb[:, :, 0], in0=scf_sb[:], in1=tmp8_sb[:])
        return e.tensor_copy(out=sc_sb[:, :, 0], in_=scf2_sb[:, :, 0])
    P.on("vector", fsilu, wait=[(ld, LD_C)], inc=ss)
    SS_SILU = ss.n

    for it in range(96):
        j, hc = divmod(it, CCH)
        w = [(cst, it - 1), (mmod, it - 1)] if it >= 2 else []
        P.on("sync", lambda e, j=j, hc=hc, it=it: e.dma_start(
            out=wm_sb[:, it % 2, :],
            in_=wmod.rearrange("(a p) j -> p a j", p=128)[:, hc, j * 512:(j + 1) * 512]),
            wait=w, inc=wld[it % 2], k=16)
        wc = [(wld[it % 2], 16 * (it // 2 + 1))]
        if it >= 2:
            wc.append((mmod, it - 1))
        P.on("gpsimd", lambda e, it=it: e.tensor_copy(
            out=wmb_sb[:, it % 2, :], in_=wm_sb[:, it % 2, :]), wait=wc, inc=cst)
        wm = [(cst, it + 1), (ss, SS_SILU)]
        if hc == 0 and j >= 2:
            wm.append((vmd, j - 1))
        if j in (4, 5, 10, 11):
            def fmodf(e, j=j, hc=hc, it=it):
                e.matmul(out=ps_mod[j % 2][0:1, 0:256], lhsT=scf2_sb[:, hc, :],
                         rhs=wm_sb[:, it % 2, 0:256],
                         start=(hc == 0), stop=(hc == CCH - 1))
                return e.matmul(out=ps_modB[j % 2][0:1, :], lhsT=scf2_sb[:, hc, :],
                                rhs=wm_sb[:, it % 2, 256:512],
                                start=(hc == 0), stop=(hc == CCH - 1))
            P.on("tensor", fmodf, wait=wm, inc=mmod)
        else:
            P.on("tensor", lambda e, j=j, hc=hc, it=it: e.matmul(
                out=ps_mod[j % 2][:], lhsT=sc_sb[:, hc, :], rhs=wmb_sb[:, it % 2, :],
                start=(hc == 0), stop=(hc == CCH - 1)), wait=wm, inc=mmod)
        if hc == CCH - 1:
            wv = [(mmod, 8 * (j + 1))] + ([(bb, 16 * j)] if j > 0 else [])
            if j in (4, 5, 10, 11):
                def fev(e, j=j):
                    e.tensor_copy(out=evrow_sb[0:1, 0:256], in_=ps_mod[j % 2][0:1, 0:256])
                    return e.tensor_copy(out=evrow_sb[0:1, 256:512],
                                         in_=ps_modB[j % 2][0:1, :])
                P.on("vector", fev, wait=wv, inc=vmd)
            else:
                P.on("vector", lambda e, j=j: e.tensor_copy(out=evrow_sb[0:1, :],
                                                            in_=ps_mod[j % 2][:]),
                     wait=wv, inc=vmd)
            P.on("gpsimd", lambda e, j=j: e.dma_start(
                out=modscr.rearrange("a b -> (a b)").unsqueeze(0)[:, j * 512:(j + 1) * 512],
                in_=evrow_sb[0:1, :]), wait=[(vmd, j + 1)], inc=bb, k=16)
    BB_MOD = bb.n
    P.on("gpsimd", lambda e: e.dma_start(out=m48_sb[:], in_=modscr),
         wait=[(bb, BB_MOD)], inc=bb, k=16)
    BB_M48 = bb.n
    P.on("tensor", lambda e: e.transpose(out=ps_tr[:], in_=m48_sb[:, 0:64],
                                         identity=eye_sb[0:48, 0:48]),
         wait=[(bb, BB_M48), (ld, LD_EYE)], inc=mmod)
    P.on("vector", lambda e: e.tensor_copy(out=mod_sb[0:64, :], in_=ps_tr[0:64, :]),
         wait=[(mmod, 97)], inc=vmd)
    P.on("tensor", lambda e: e.transpose(out=ps_tr[:], in_=m48_sb[:, 64:128],
                                         identity=eye_sb[0:48, 0:48]),
         wait=[(vmd, 13)], inc=mmod)
    P.on("vector", lambda e: e.tensor_copy(out=mod_sb[64:128, :], in_=ps_tr[0:64, :]),
         wait=[(mmod, 98)], inc=vmd)
    P.on("vector", lambda e: e.tensor_scalar(out=A_sb[:], in0=mod_sb[:, 8:16],
                                             scalar1=1.0, scalar2=None, op0=ALU.add),
         wait=[(vmd, 14)], inc=vmd)
    VMD_DONE = vmd.n
    P.on("vector", lambda e: e.tensor_copy(out=mask_sb[:], in_=maskf_sb[:]),
         wait=[(ld, LD_MASK)], inc=vv)
    VV_MASK = vv.n

    onesb_sb = nc.alloc_sbuf_tensor("onesb_sb", [128, 1], BF16)
    eps_sb = nc.alloc_sbuf_tensor("eps_sb", [128, 1], F32)
    P.on("vector", lambda e: e.memset(onesb_sb[:], 1.0), inc=vv)
    P.on("vector", lambda e: e.memset(eps_sb[:], EPS), inc=vv)
    VV_ONES = vv.n
    tiles = [(st * 512, 512) for st in range(4)] + [(2048, 2)]
    VV_CAST = {}
    SS_SQ = {}
    for st, (t0, tw) in enumerate(tiles):
        wv = [(ld, LD_X)]
        if st > 0:
            wv += [(ss, SS_SQ[st - 1]), (mst, st)]
        P.on("vector", lambda e, t0=t0, tw=tw: e.tensor_copy(
            out=xb_sb[:, :, 0:tw], in_=x_sb[:, :, t0:t0 + tw]), wait=wv, inc=vv)
        VV_CAST[st] = vv.n
        ws = [(vv, VV_CAST[st])]
        if st > 0:
            ws.append((mst, st))
        P.on("scalar", lambda e, tw=tw: e.activation(
            out=x2b_sb[:, :, 0:tw], in_=xb_sb[:, :, 0:tw], func=AF.Square),
            wait=ws, inc=ss)
        SS_SQ[st] = ss.n
        def fstat(e, tw=tw):
            for a in range(CCH):
                e.matmul(out=ps_sum[0:1, 0:tw], lhsT=onesb_sb[:], rhs=xb_sb[:, a, 0:tw],
                         start=(a == 0), stop=(a == CCH - 1))
            ins = None
            for a in range(CCH):
                ins = e.matmul(out=ps_ssq[0:1, 0:tw], lhsT=onesb_sb[:],
                               rhs=x2b_sb[:, a, 0:tw], start=(a == 0), stop=(a == CCH - 1))
            return ins
        wt = [(vv, VV_CAST[st]), (ss, SS_SQ[st]), (vv, VV_ONES)]
        if st > 0:
            wt.append((sst, 2 * st))
        P.on("tensor", fstat, wait=wt, inc=mst)
        P.on("scalar", lambda e, t0=t0, tw=tw: e.activation(
            out=stat_sb[0:1, 0, t0:t0 + tw], in_=ps_sum[0:1, 0:tw], func=AF.Copy),
            wait=[(mst, st + 1)], inc=sst)
        P.on("scalar", lambda e, t0=t0, tw=tw: e.activation(
            out=stat_sb[0:1, 1, t0:t0 + tw], in_=ps_ssq[0:1, 0:tw], func=AF.Copy),
            wait=[], inc=sst)
    assert mst.n == 5 and sst.n == 10

    P.on("scalar", lambda e: e.activation(out=stat_sb[0:1, 0, :], in_=stat_sb[0:1, 0, :],
                                          func=AF.Copy, scale=1.0 / C),
         wait=[(sst, 10)], inc=ss)
    P.on("scalar", lambda e: e.activation(out=ht1_sb[0:1, :], in_=stat_sb[0:1, 0, :],
                                          func=AF.Square), wait=[], inc=ss)
    SS_MSQ = ss.n
    P.on("vector", lambda e: e.scalar_tensor_tensor(
        out=stat_sb[0:1, 1, :], in0=stat_sb[0:1, 1, :], scalar=1.0 / C,
        in1=ht1_sb[0:1, :], op0=ALU.mult, op1=ALU.subtract),
        wait=[(ss, SS_MSQ)], inc=rr)
    RR_VAR = rr.n
    P.on("scalar", lambda e: e.activation(out=stat_sb[0:1, 1, :], in_=stat_sb[0:1, 1, :],
                                          func=AF.Ln, bias=eps_sb[0:1, :]),
         wait=[(rr, RR_VAR), (vv, VV_ONES)], inc=ss)
    P.on("scalar", lambda e: e.activation(out=stat_sb[0:1, 1, :], in_=stat_sb[0:1, 1, :],
                                          func=AF.Exp, scale=-0.5), wait=[], inc=ss)
    SS_RSTD = ss.n
    P.on("gpsimd", lambda e: e.tensor_mul(out=ht1_sb[0:1, :], in0=stat_sb[0:1, 0, :],
                                          in1=stat_sb[0:1, 1, :]),
         wait=[(ss, SS_RSTD)], inc=rr)
    RR_MH = rr.n
    P.on("gpsimd", lambda e: e.dma_start(out=rowscr[0:1, :], in_=stat_sb[0:1, 1, :]),
         wait=[], inc=bb, k=16)
    P.on("gpsimd", lambda e: e.dma_start(out=rowscr[1:2, :], in_=ht1_sb[0:1, :]),
         wait=[], inc=bb, k=16)
    BB_W = bb.n
    P.on("gpsimd", lambda e: e.dma_start(
        out=brow_sb[:, 0, :], in_=rowscr[0:1, :].partition_broadcast(128).squeeze(1)),
        wait=[(bb, BB_W)], inc=bb, k=16)
    P.on("gpsimd", lambda e: e.dma_start(
        out=brow_sb[:, 1, :], in_=rowscr[1:2, :].partition_broadcast(128).squeeze(1)),
        wait=[], inc=bb, k=16)
    BB_ROWS = bb.n

    VV_HT1 = {}
    for c in range(CCH):
        wv = [(bb, BB_ROWS)] if c == 0 else [(hh, c)]
        P.on("vector", lambda e, c=c: e.tensor_mul(
            out=ht1_sb[:], in0=x_sb[:, c, :], in1=brow_sb[:, 0, :]), wait=wv, inc=vv)
        VV_HT1[c] = vv.n
        P.on("gpsimd", lambda e: e.tensor_sub(out=ht2_sb[:], in0=ht1_sb[:],
                                              in1=brow_sb[:, 1, :]),
             wait=[(vv, VV_HT1[c]), (rr, RR_MH)], inc=hh)
        def fh(e, c=c):
            e.tensor_scalar(out=h_sb[:, c, :], in0=ht2_sb[:],
                            scalar1=A_sb[:, c:c + 1], scalar2=mod_sb[:, c:c + 1],
                            op0=ALU.mult, op1=ALU.add)
            e.tensor_mul(out=h_sb[:, c, 0:1], in0=h_sb[:, c, 0:1], in1=mask_sb[:, 0:1])
            return e.tensor_mul(out=h_sb[:, c, WH - 1:WH], in0=h_sb[:, c, WH - 1:WH],
                                in1=mask_sb[:, 1:2])
        P.on("vector", fh, wait=[(hh, c + 1), (vmd, VMD_DONE), (vv, VV_MASK)], inc=hrd)
    assert hrd.n == CCH

    P.on("gpsimd", lambda e: e.dma_start(out=fm(h_o), in_=h_sb[:]),
         wait=[(hrd, CCH)], inc=xx, k=16)
    P.on("gpsimd", lambda e: e.dma_start(out=mod_o, in_=mod_sb[:]),
         wait=[(vmd, VMD_DONE)], inc=xx, k=16)

    for c in range(4):
        hp = h_sb[:, c, 0:TC] if c < 2 else h_sb[:, c, 2:WH]
        wd = [(hrd, CCH)] + ([(urd, 4 * c)] if c > 0 else [])
        P.on("vector", lambda e, c=c, hp=hp: e.tensor_sub(
            out=dscr_sb, in0=h_sb[:, c, 1:TC + 1], in1=hp), wait=wd, inc=urd)
        U_D = urd.n
        for ip in range(3):
            eng = "vector"
            P.on(eng, lambda e, c=c, ip=ip, hp=hp: e.scalar_tensor_tensor(
                out=u_sb[:, ip, c, :], in0=dscr_sb, scalar=mix_sb[:, ip, c:c + 1],
                in1=hp, op0=ALU.mult, op1=ALU.add),
                wait=[(urd, U_D), (ld, LD_MIX)], inc=urd)
    assert urd.n == 16
    P.on("gpsimd", lambda e: e.dma_start(
        out=u_o.rearrange("i (a p) t -> p i a t", p=128), in_=u_sb),
        wait=[(urd, 16)], inc=xx, k=16)

    for t9 in range(9):
        w = [(dqp, t9 - 1)] if t9 >= 2 else []
        if t9 < 3:
            src = wup_sh.rearrange("(a p) c -> p a c", p=128)[:, t9, :]
        elif t9 < 6:
            src = wgate_sh.rearrange("(a p) c -> p a c", p=128)[:, t9 - 3, :]
        else:
            src = wdown_sh[:, (t9 - 6) * 1024:(t9 - 5) * 1024]
        P.on("sync", lambda e, src=src, t9=t9: e.dma_start(
            out=dqw_sb[:, t9 % 2, :], in_=src), wait=w, inc=dql[t9 % 2], k=16)
        def fdq1(e, t9=t9):
            e.tensor_reduce(out=dqi_sb[:],
                            in_=dqw_sb[:, t9 % 2, :].rearrange("p (b k) -> p b k", k=32),
                            axis=AX, op=ALU.max, apply_absolute_value=True)
            e.tensor_scalar(out=dqs_sb[:], in0=dqi_sb[:],
                            scalar1=1.0 / 7.0, scalar2=1e-12,
                            op0=ALU.mult, op1=ALU.max)
            return e.reciprocal(out=dqi_sb[:], in_=dqs_sb[:])
        wq1 = [(dql[t9 % 2], 16 * (t9 // 2 + 1))] + ([(dqp, t9)] if t9 >= 1 else [])
        P.on("vector", fdq1, wait=wq1, inc=dqv)
        def fdq2(e, t9=t9):
            b1 = dqi_sb.ap().unsqueeze(2).broadcast_to([128, 32, 32])
            w3 = dqw_sb[:, t9 % 2, :].rearrange("p (b k) -> p b k", k=32)
            e.tensor_tensor(out=w3, in0=w3, in1=b1, op=ALU.mult)
            e.tensor_scalar(out=dqw_sb[:, t9 % 2, :], in0=dqw_sb[:, t9 % 2, :],
                            scalar1=MAGIC, scalar2=None, op0=ALU.add)
            e.tensor_scalar(out=dqw_sb[:, t9 % 2, :], in0=dqw_sb[:, t9 % 2, :],
                            scalar1=MAGIC, scalar2=None, op0=ALU.subtract)
            b2 = dqs_sb.ap().unsqueeze(2).broadcast_to([128, 32, 32])
            return e.tensor_tensor(
                out=dqo_sb[:, t9 % 2, :].rearrange("p (b k) -> p b k", k=32),
                in0=w3, in1=b2, op=ALU.mult)
        P.on("vector", fdq2,
             wait=[(dqv, t9 + 1)] + ([(xdq, 16 * (t9 - 1))] if t9 >= 2 else []),
             inc=dqp)
        if t9 < 3:
            P.on("gpsimd", lambda e, t9=t9: e.dma_start(
                out=wupd_o.rearrange("(a p) c -> p a c", p=128)[:, t9, :],
                in_=dqo_sb[:, t9 % 2, :]), wait=[(dqp, t9 + 1)], inc=xdq, k=16)
        elif t9 < 6:
            P.on("gpsimd", lambda e, t9=t9: e.dma_start(
                out=wgated_o.rearrange("(a p) c -> p a c", p=128)[:, t9 - 3, :],
                in_=dqo_sb[:, t9 % 2, :]), wait=[(dqp, t9 + 1)], inc=xdq, k=16)
        else:
            P.on("gpsimd", lambda e, t9=t9: e.dma_start(
                out=wdownd_o[:, (t9 - 6) * 1024:(t9 - 5) * 1024],
                in_=dqo_sb[:, t9 % 2, :]), wait=[(dqp, t9 + 1)], inc=xdq, k=16)

    P.on("sync", lambda e: None, wait=[(xx, xx.n), (xdq, xdq.n)])
    P.emit()
    return nc



def build_lb():
    nc = bass.Bass("TRN2", target_bir_lowering=False)
    h_i = nc.dram_tensor("h_i", [C, WH], BF16, kind="ExternalInput").ap()
    u_i = nc.dram_tensor("u_i", [3, C // 2, TC], BF16, kind="ExternalInput").ap()
    wkT = nc.dram_tensor("wkT", [C, C], F32, kind="ExternalInput").ap()
    wvT = nc.dram_tensor("wvT", [C, C], F32, kind="ExternalInput").ap()
    wrT = nc.dram_tensor("wrT", [C, C], F32, kind="ExternalInput").ap()
    sdec = nc.dram_tensor("sdec", [128, CCH], F32, kind="ExternalInput").ap()

    ek_o = nc.dram_tensor("ek_o", [C, TC], F32, kind="ExternalOutput").ap()
    ekv_o = nc.dram_tensor("ekv_o", [C, TC], F32, kind="ExternalOutput").ap()
    sr_o = nc.dram_tensor("sr_o", [C, TC], F32, kind="ExternalOutput").ap()
    afin_o = nc.dram_tensor("afin_o", [128, CCH], F32, kind="ExternalOutput").ap()
    bfin_o = nc.dram_tensor("bfin_o", [128, CCH], F32, kind="ExternalOutput").ap()
    dbg_o = nc.dram_tensor("dbg_o", [128, 2, TC], F32, kind="ExternalOutput").ap()

    h_sb = nc.alloc_sbuf_tensor("h_sb", [128, CCH, WH], BF16)
    u_sb = nc.alloc_sbuf_tensor("u_sb", [128, 3, 4, TC], BF16)
    wsf_sb = nc.alloc_sbuf_tensor("wsf_sb", [128, 2, 3, CCH, 128], F32)
    wsb_sb = nc.alloc_sbuf_tensor("wsb_sb", [128, 2, 3, CCH, 128], BF16)
    sd_sb = nc.alloc_sbuf_tensor("sd_sb", [128, CCH], F32)
    ew_sb = nc.alloc_sbuf_tensor("ew_sb", [128, CCH], F32)
    tmp8_sb = nc.alloc_sbuf_tensor("tmp8_sb", [128, CCH], F32)
    ek_sb = nc.alloc_sbuf_tensor("ek_sb", [128, TC], F32)
    v_sb = nc.alloc_sbuf_tensor("v_sb", [128, TC], F32)
    ekv_sb = nc.alloc_sbuf_tensor("ekv_sb", [128, TC], F32)
    sr_sb = nc.alloc_sbuf_tensor("sr_sb", [128, TC], F32)
    ewt_sb = nc.alloc_sbuf_tensor("ewt_sb", [128, TC], F32)
    asc_sb = nc.alloc_sbuf_tensor("asc_sb", [128, TC], F32)
    bsc_sb = nc.alloc_sbuf_tensor("bsc_sb", [128, TC], F32)
    afin_sb = nc.alloc_sbuf_tensor("afin_sb", [128, CCH], F32)
    bfin_sb = nc.alloc_sbuf_tensor("bfin_sb", [128, CCH], F32)

    ps_kvr = [nc.alloc_psum_tensor(f"ps_kvr{i}", [128, 512], F32) for i in range(2)]

    P = Prog(nc)
    ld = P.sem("ld")
    awl = [P.sem(f"awl{i}") for i in range(2)]
    wcs = P.sem("wcs")
    vv = P.sem("vv")
    mkv = P.sem("mkv")
    evc = P.sem("evc")
    vkv = P.sem("vkv")
    xx = P.sem("xx")
    xek = P.sem("xek")
    xekv = P.sem("xekv")
    xsr = P.sem("xsr")

    P.on("sync", lambda e: e.dma_start(out=h_sb[:], in_=fm(h_i)), inc=ld, k=16)
    P.on("sync", lambda e: e.dma_start(
        out=u_sb[:], in_=u_i.rearrange("i (a p) t -> p i a t", p=128)), inc=ld, k=16)
    P.on("sync", lambda e: e.dma_start(out=sd_sb[:], in_=sdec), inc=ld, k=16)
    LD_H = LD_U = LD_SD = 48

    def few(e):
        e.tensor_scalar(out=ew_sb[:], in0=sd_sb[:], scalar1=1.0 / T, scalar2=None,
                        op0=ALU.mult)
        return exp_poly(e, ew_sb[:], ew_sb[:], tmp8_sb[:])
    P.on("vector", few, wait=[(ld, LD_SD)], inc=vv)
    VV_EW = vv.n

    def emit_wslice(oc):
        for i, t in enumerate((wkT, wvT, wrT)):
            w = []
            if oc >= 2:
                w += [(wcs, 3 * (oc - 1)), (mkv, 12 * (oc - 1))]
            P.on("sync", lambda e, t=t, oc=oc, i=i: e.dma_start(
                out=wsf_sb[:, oc % 2, i, :, :],
                in_=fm(t)[:, :, oc * 128:(oc + 1) * 128]), wait=w, inc=awl[oc % 2], k=16)
        wc = [(awl[oc % 2], 48 * (oc // 2 + 1))]
        if oc >= 2:
            wc.append((mkv, 12 * (oc - 1)))
        def fwc(e, oc=oc):
            ins = None
            for i in range(3):
                ins = e.tensor_copy(out=wsb_sb[:, oc % 2, i, :, :],
                                    in_=wsf_sb[:, oc % 2, i, :, :])
            return ins
        P.on("vector", fwc, wait=wc, inc=wcs, k=3)

    emit_wslice(0)
    emit_wslice(1)

    for oc in range(CCH):
        for ip in range(3):
            for tt in range(NTT):
                g = oc * 12 + ip * 4 + tt
                w = [(ld, LD_U), (wcs, 3 * (oc + 1))]
                if g >= 2:
                    w.append((evc, g - 1))
                def fkvr(e, oc=oc, ip=ip, tt=tt, g=g):
                    bank = ps_kvr[g % 2]
                    ins = None
                    for cin in range(CCH):
                        if cin < 4:
                            rhs = u_sb[:, ip, cin, tt * 512:(tt + 1) * 512]
                        else:
                            rhs = h_sb[:, cin, 1 + tt * 512:1 + (tt + 1) * 512]
                        ins = e.matmul(out=bank[:], lhsT=wsb_sb[:, oc % 2, ip, cin, :],
                                       rhs=rhs, start=(cin == 0), stop=(cin == CCH - 1))
                    return ins
                P.on("tensor", fkvr, wait=w, inc=mkv)
                we = [(mkv, g + 1)]
                if oc > 0 and tt == 0:
                    we.append(((xek, xekv, xsr)[ip], 16 * oc))
                    if ip <= 1:
                        we.append((vkv, 6 * oc))
                def fev(e, ip=ip, tt=tt, g=g):
                    bank = ps_kvr[g % 2]
                    sl = slice(tt * 512, (tt + 1) * 512)
                    if ip == 0:
                        return e.activation(out=ek_sb[:, sl], in_=bank[:], func=AF.Exp)
                    if ip == 1:
                        return e.activation(out=v_sb[:, sl], in_=bank[:], func=AF.Copy)
                    return e.activation(out=sr_sb[:, sl], in_=bank[:], func=AF.Sigmoid)
                P.on("scalar", fev, wait=we, inc=evc)
        if oc + 2 < CCH:
            emit_wslice(oc + 2)
        def fewt(e, oc=oc):
            e.memset(ewt_sb[:], 1.0)
            return e.tensor_scalar(out=ewt_sb[:], in0=ewt_sb[:],
                                   scalar1=ew_sb[:, oc:oc + 1], scalar2=None,
                                   op0=ALU.mult)
        P.on("vector", fewt, wait=[(vv, VV_EW), (evc, oc * 12 + 4)], inc=vkv)
        P.on("vector", lambda e: e.tensor_mul(out=ekv_sb[:], in0=ek_sb[:], in1=v_sb[:]),
             wait=[(evc, oc * 12 + 8)], inc=vkv)
        P.on("vector", lambda e: e.tensor_tensor_scan(
            out=asc_sb[:], data0=ewt_sb[:], data1=ekv_sb[:], initial=0.0,
            op0=ALU.mult, op1=ALU.add), wait=[], inc=vkv)
        P.on("vector", lambda e, oc=oc: e.tensor_copy(
            out=afin_sb[:, oc:oc + 1], in_=asc_sb[:, TC - 1:TC]), wait=[], inc=vkv)
        P.on("vector", lambda e: e.tensor_tensor_scan(
            out=bsc_sb[:], data0=ewt_sb[:], data1=ek_sb[:], initial=0.0,
            op0=ALU.mult, op1=ALU.add), wait=[], inc=vkv)
        P.on("vector", lambda e, oc=oc: e.tensor_copy(
            out=bfin_sb[:, oc:oc + 1], in_=bsc_sb[:, TC - 1:TC]), wait=[], inc=vkv)
        P.on("gpsimd", lambda e, oc=oc: e.dma_start(
            out=fm(ek_o)[:, oc, :], in_=ek_sb[:]), wait=[(vkv, 6 * oc + 5)], inc=xek, k=16)
        P.on("gpsimd", lambda e, oc=oc: e.dma_start(
            out=fm(ekv_o)[:, oc, :], in_=ekv_sb[:]), wait=[(vkv, 6 * oc + 3)], inc=xekv, k=16)
        P.on("gpsimd", lambda e, oc=oc: e.dma_start(
            out=fm(sr_o)[:, oc, :], in_=sr_sb[:]), wait=[(evc, oc * 12 + 12)], inc=xsr, k=16)
    assert mkv.n == 96 and evc.n == 96 and vkv.n == 48

    P.on("gpsimd", lambda e: e.dma_start(out=afin_o, in_=afin_sb[:]),
         wait=[(vkv, 48)], inc=xx, k=16)
    P.on("gpsimd", lambda e: e.dma_start(out=bfin_o, in_=bfin_sb[:]),
         wait=[(vkv, 48)], inc=xx, k=16)
    P.on("gpsimd", lambda e: e.dma_start(out=dbg_o[:, 0, :], in_=ewt_sb[:]),
         wait=[(vkv, 48)], inc=xx, k=16)
    P.on("gpsimd", lambda e: e.dma_start(out=dbg_o[:, 1, :], in_=asc_sb[:]),
         wait=[(vkv, 48)], inc=xx, k=16)
    P.on("sync", lambda e: None, wait=[(xx, xx.n), (xek, xek.n), (xekv, xekv.n),
                                       (xsr, xsr.n)])
    P.emit()
    return nc



def build_l2():
    nc = bass.Bass("TRN2", target_bir_lowering=False)
    xT = nc.dram_tensor("xT", [C, WH], F32, kind="ExternalInput").ap()
    ek_i = nc.dram_tensor("ek_i", [C, TC], F32, kind="ExternalInput").ap()
    ekv_i = nc.dram_tensor("ekv_i", [C, TC], F32, kind="ExternalInput").ap()
    sr_i = nc.dram_tensor("sr_i", [C, TC], F32, kind="ExternalInput").ap()
    af_all = nc.dram_tensor("af_all", [128, CCH, 8], F32, kind="ExternalInput").ap()
    bf_all = nc.dram_tensor("bf_all", [128, CCH, 8], F32, kind="ExternalInput").ap()
    pw = nc.dram_tensor("pw", [128, 8], F32, kind="ExternalInput").ap()
    pmask = nc.dram_tensor("pmask", [128, 8], F32, kind="ExternalInput").ap()
    woT = nc.dram_tensor("woT", [C, C], F32, kind="ExternalInput").ap()
    mod_i = nc.dram_tensor("mod_i", [128, 48], F32, kind="ExternalInput").ap()
    sdec = nc.dram_tensor("sdec", [128, CCH], F32, kind="ExternalInput").ap()
    sfirst = nc.dram_tensor("sfirst", [128, CCH], F32, kind="ExternalInput").ap()

    x2_o = nc.dram_tensor("x2_o", [C, TC], F32, kind="ExternalOutput").ap()
    h2_o = nc.dram_tensor("h2_o", [C, TC], F32, kind="ExternalOutput").ap()
    mh2_o = nc.dram_tensor("mh2_o", [1, 1], F32, kind="ExternalOutput").ap()
    ydbg_o = nc.dram_tensor("ydbg_o", [C, TC], BF16, kind="ExternalOutput").ap()

    rowscr = nc.dram_tensor("rowscr", [2, 512], F32, kind="Internal").ap()
    colscr = nc.dram_tensor("colscr", [128, 1], F32, kind="Internal").ap()

    y_sb = nc.alloc_sbuf_tensor("y_sb", [128, CCH, TC], BF16)
    wob_sb = nc.alloc_sbuf_tensor("wob_sb", [128, CCH, C], BF16)
    a2_sb = nc.alloc_sbuf_tensor("a2_sb", [128, TC + 1], F32)
    b2_sb = nc.alloc_sbuf_tensor("b2_sb", [128, TC + 1], F32)
    num_sb = nc.alloc_sbuf_tensor("num_sb", [128, TC], F32)
    den_sb = nc.alloc_sbuf_tensor("den_sb", [128, TC], F32)
    ewt_sb = nc.alloc_sbuf_tensor("ewt_sb", [128, TC], F32)
    af_sb = nc.alloc_sbuf_tensor("af_sb", [128, 2, CCH, 8], F32)
    d3_sb = nc.alloc_sbuf_tensor("d3_sb", [128, CCH, 8], F32)
    t3_sb = nc.alloc_sbuf_tensor("t3_sb", [128, CCH, 8], F32)
    ain_sb = nc.alloc_sbuf_tensor("ain_sb", [128, 2, CCH], F32)
    pw_sb = nc.alloc_sbuf_tensor("pw_sb", [128, 2, 8], F32)
    sd_sb = nc.alloc_sbuf_tensor("sd_sb", [128, CCH], F32)
    sf_sb = nc.alloc_sbuf_tensor("sf_sb", [128, CCH], F32)
    ew_sb = nc.alloc_sbuf_tensor("ew_sb", [128, CCH], F32)
    eu_sb = nc.alloc_sbuf_tensor("eu_sb", [128, CCH], F32)
    tmp8_sb = nc.alloc_sbuf_tensor("tmp8_sb", [128, CCH], F32)
    mod_sb = nc.alloc_sbuf_tensor("mod_sb", [128, 48], F32)
    ga_sb = nc.alloc_sbuf_tensor("ga_sb", [128, CCH], F32)
    gat_sb = nc.alloc_sbuf_tensor("gat_sb", [128, 2, CCH], F32)
    B2_sb = nc.alloc_sbuf_tensor("B2_sb", [128, CCH], F32)
    run_sb = nc.alloc_sbuf_tensor("run_sb", [128, 2], F32)
    mrow_sb = nc.alloc_sbuf_tensor("mrow_sb", [1, 130], F32)
    stat2_sb = nc.alloc_sbuf_tensor("stat2_sb", [1, 2, 512], F32)
    brow2_sb = nc.alloc_sbuf_tensor("brow2_sb", [128, 2, 512], F32)
    onesb_sb = nc.alloc_sbuf_tensor("onesb_sb", [128, 1], BF16)
    eps_sb = nc.alloc_sbuf_tensor("eps_sb", [1, 1], F32)
    ZW = Arena(nc, "zoneW", 33 * 1024)
    wof_sb = ZW.carve([CCH, C], F32)
    ZW.reset()
    x2c_sb = ZW.carve([CCH, 512], F32)
    h2t_sb = ZW.carve([CCH, 512], F32)
    ZS = Arena(nc, "zoneS", 50 * 1024)
    ek2_sb = ZS.carve([2, TC], F32)
    ekv2_sb = ZS.carve([2, TC], F32)
    sr2_sb = ZS.carve([2, TC], F32)
    ZS.reset()
    xb2_sb = ZS.carve([CCH, 512], BF16)
    x2b2_sb = ZS.carve([CCH, 512], BF16)
    xt_sb = ZS.carve([2, 512], F32)

    ps_o = [nc.alloc_psum_tensor(f"ps_o{i}", [128, 512], F32) for i in range(2)]
    ps_sum = nc.alloc_psum_tensor("ps_sum", [1, 512], F32)
    ps_ssq = nc.alloc_psum_tensor("ps_ssq", [1, 512], F32)

    P = Prog(nc)
    ld = P.sem("ld")
    awo = P.sem("awo")
    cek = P.sem("cek")
    cekv = P.sem("cekv")
    csr = P.sem("csr")
    xld = [P.sem(f"xld{i}") for i in range(2)]
    vv = P.sem("vv")
    sph = P.sem("sph")
    vph = P.sem("vph")
    mm = P.sem("mm")
    evc = P.sem("evc")
    mst = P.sem("mst")
    sst = P.sem("sst")
    rr = P.sem("rr")
    bb = P.sem("bb")
    hx = P.sem("hx")
    xx = P.sem("xx")

    P.on("sync", lambda e: e.dma_start(out=af_sb[:, 0], in_=af_all), inc=ld, k=16)
    P.on("sync", lambda e: e.dma_start(out=af_sb[:, 1], in_=bf_all), inc=ld, k=16)
    P.on("sync", lambda e: e.dma_start(out=pw_sb[:, 0], in_=pw), inc=ld, k=16)
    P.on("sync", lambda e: e.dma_start(out=pw_sb[:, 1], in_=pmask), inc=ld, k=16)
    P.on("sync", lambda e: e.dma_start(out=mod_sb[:], in_=mod_i), inc=ld, k=16)
    P.on("sync", lambda e: e.dma_start(out=sd_sb[:], in_=sdec), inc=ld, k=16)
    P.on("sync", lambda e: e.dma_start(out=sf_sb[:], in_=sfirst), inc=ld, k=16)
    LD_SMALL = ld.n
    P.on("sync", lambda e: e.dma_start(out=wof_sb, in_=fm(woT)), inc=awo, k=16)

    def fsetup(e):
        e.memset(onesb_sb[:], 1.0)
        e.memset(eps_sb[:], EPS)
        e.tensor_scalar(out=ew_sb[:], in0=sd_sb[:], scalar1=1.0 / T, scalar2=None,
                        op0=ALU.mult)
        exp_poly(e, ew_sb[:], ew_sb[:], tmp8_sb[:])
        e.tensor_scalar(out=eu_sb[:], in0=sf_sb[:], scalar1=1.0 / T, scalar2=None,
                        op0=ALU.mult)
        exp_poly(e, eu_sb[:], eu_sb[:], tmp8_sb[:])
        e.tensor_scalar(out=B2_sb[:], in0=mod_sb[:, 32:40], scalar1=1.0, scalar2=None,
                        op0=ALU.add)
        return e.tensor_tensor(
            out=t3_sb[:], in0=sd_sb.ap().unsqueeze(2).broadcast_to([128, CCH, 8]),
            in1=pw_sb.ap()[:, 0, :].unsqueeze(1).broadcast_to([128, CCH, 8]),
            op=ALU.mult)
    P.on("vector", fsetup, wait=[(ld, LD_SMALL)], inc=vv)
    VV_T3 = vv.n
    P.on("scalar", lambda e: e.activation(out=d3_sb[:], in_=t3_sb[:], func=AF.Exp),
         wait=[(vv, VV_T3)], inc=sph)
    SPH_D3 = sph.n
    P.on("scalar", lambda e: e.activation(out=gat_sb[:, 0], in_=mod_sb[:, 16:24],
                                          func=AF.Abs), wait=[(ld, LD_SMALL)], inc=sph)
    P.on("scalar", lambda e: e.sign(out=gat_sb[:, 1], in_=mod_sb[:, 16:24]),
         wait=[], inc=sph)
    SPH_GA = sph.n

    def fprefix(e):
        e.tensor_tensor(out=d3_sb[:], in0=d3_sb[:],
                        in1=pw_sb.ap()[:, 1, :].unsqueeze(1).broadcast_to([128, CCH, 8]),
                        op=ALU.mult)
        e.tensor_tensor(out=t3_sb[:], in0=af_sb[:, 0], in1=d3_sb[:], op=ALU.mult)
        e.tensor_reduce(out=ain_sb[:, 0], in_=t3_sb[:], axis=AX, op=ALU.add)
        e.tensor_tensor(out=t3_sb[:], in0=af_sb[:, 1], in1=d3_sb[:], op=ALU.mult)
        e.tensor_reduce(out=ain_sb[:, 1], in_=t3_sb[:], axis=AX, op=ALU.add)
        e.tensor_scalar(out=gat_sb[:, 0], in0=gat_sb[:, 0], scalar1=0.25,
                        scalar2=None, op0=ALU.max)
        return e.tensor_tensor(out=ga_sb[:], in0=gat_sb[:, 1], in1=gat_sb[:, 0],
                               op=ALU.mult)
    P.on("vector", fprefix, wait=[(sph, SPH_GA)], inc=vv)
    VV_PREFIX = vv.n

    def fwoc(e):
        ins = None
        for a in range(CCH):
            ins = e.tensor_copy(out=wob_sb[:, a, :], in_=wof_sb[:, a, :])
        return ins
    P.on("vector", fwoc, wait=[(awo, 16)], inc=vv)
    VV_WO = vv.n

    for oc in range(CCH):
        for i, (t, buf, cs) in enumerate(((ek_i, ek2_sb, cek), (ekv_i, ekv2_sb, cekv),
                                          (sr_i, sr2_sb, csr))):
            w = [(vph, 2 * (oc - 1))] if oc >= 2 else []
            P.on("sync", lambda e, t=t, buf=buf, oc=oc: e.dma_start(
                out=buf[:, oc % 2, :], in_=fm(t)[:, oc, :]), wait=w, inc=cs, k=16)
        def fscan(e, oc=oc):
            sl = oc % 2
            e.memset(ewt_sb[:], 1.0)
            e.tensor_scalar(out=ewt_sb[:], in0=ewt_sb[:], scalar1=ew_sb[:, oc:oc + 1],
                            scalar2=None, op0=ALU.mult)
            e.tensor_copy(out=a2_sb[:, 0:1], in_=ain_sb[:, 0, oc:oc + 1])
            e.tensor_copy(out=b2_sb[:, 0:1], in_=ain_sb[:, 1, oc:oc + 1])
            e.tensor_tensor_scan(out=a2_sb[:, 1:TC + 1], data0=ewt_sb[:],
                                 data1=ekv2_sb[:, sl, :],
                                 initial=ain_sb[:, 0, oc:oc + 1],
                                 op0=ALU.mult, op1=ALU.add)
            e.tensor_tensor_scan(out=b2_sb[:, 1:TC + 1], data0=ewt_sb[:],
                                 data1=ek2_sb[:, sl, :],
                                 initial=ain_sb[:, 1, oc:oc + 1],
                                 op0=ALU.mult, op1=ALU.add)
            e.scalar_tensor_tensor(out=num_sb[:], in0=ekv2_sb[:, sl, :],
                                   scalar=eu_sb[:, oc:oc + 1], in1=a2_sb[:, 0:TC],
                                   op0=ALU.mult, op1=ALU.add)
            return e.scalar_tensor_tensor(out=den_sb[:], in0=ek2_sb[:, sl, :],
                                          scalar=eu_sb[:, oc:oc + 1], in1=b2_sb[:, 0:TC],
                                          op0=ALU.mult, op1=ALU.add)
        wsc = [(cek, 16 * (oc + 1)), (cekv, 16 * (oc + 1)), (vv, VV_PREFIX)]
        if oc >= 1:
            wsc.append((sph, SPH_GA + oc))
        P.on("vector", fscan, wait=wsc, inc=vph)
        def frden(e):
            e.activation(out=den_sb[:], in_=den_sb[:], func=AF.Ln)
            return e.activation(out=den_sb[:], in_=den_sb[:], func=AF.Exp, scale=-1.0)
        P.on("scalar", frden, wait=[(vph, 2 * oc + 1)], inc=sph)
        def fy(e, oc=oc):
            e.tensor_mul(out=num_sb[:], in0=num_sb[:], in1=den_sb[:])
            return e.tensor_mul(out=y_sb[:, oc, :], in0=sr2_sb[:, oc % 2, :],
                                in1=num_sb[:])
        P.on("vector", fy, wait=[(sph, SPH_GA + oc + 1), (csr, 16 * (oc + 1))], inc=vph)
    assert vph.n == 16
    P.on("gpsimd", lambda e: e.dma_start(out=fm(ydbg_o), in_=y_sb[:]),
         wait=[(vph, 16)], inc=xx, k=16)

    BBR_LIST = []
    for tt in range(NTT):
        for oc2 in range(CCH):
            g = tt * CCH + oc2
            w = [(evc, g - 1)] if g >= 2 else [(vph, 16)]
            P.on("sync", lambda e, tt=tt, oc2=oc2, g=g: e.dma_start(
                out=xt_sb[:, g % 2, :], in_=fm(xT)[:, oc2, 1 + tt * 512:1 + (tt + 1) * 512]),
                wait=w, inc=xld[g % 2], k=16)
            wm_ = [(vph, 16), (vv, VV_WO), (xld[g % 2], 16 * (g // 2 + 1))]
            if g >= 2:
                wm_.append((evc, g - 1))
            def fomm(e, tt=tt, oc2=oc2, g=g):
                bank = ps_o[g % 2]
                ins = None
                for cin in range(CCH):
                    ins = e.matmul(out=bank[:], lhsT=wob_sb[:, cin, oc2 * 128:(oc2 + 1) * 128],
                                   rhs=y_sb[:, cin, tt * 512:(tt + 1) * 512],
                                   start=(cin == 0), stop=(cin == CCH - 1))
                return ins
            P.on("tensor", fomm, wait=wm_, inc=mm)
            wev = [(mm, g + 1)]
            if tt >= 1 and oc2 == 0:
                wev += [(hx, tt * CCH), (xx, 32 * (tt - 1) + 32)]
            P.on("vector", lambda e, oc2=oc2, g=g: e.scalar_tensor_tensor(
                out=x2c_sb[:, oc2, :], in0=ps_o[g % 2][:], scalar=ga_sb[:, oc2:oc2 + 1],
                in1=xt_sb[:, g % 2, :], op0=ALU.mult, op1=ALU.add),
                wait=wev, inc=evc)
        P.on("gpsimd", lambda e, tt=tt: e.dma_start(
            out=fm(x2_o)[:, :, tt * 512:(tt + 1) * 512], in_=x2c_sb),
            wait=[(evc, (tt + 1) * CCH)], inc=xx, k=16)
        def fcast(e):
            return e.tensor_copy(out=xb2_sb, in_=x2c_sb)
        wc_ = [(evc, (tt + 1) * CCH)] + ([(mst, tt)] if tt > 0 else [])
        P.on("vector", fcast, wait=wc_, inc=vv)
        VV_C = vv.n
        wsq = [(vv, VV_C)] + ([(mst, tt)] if tt > 0 else [])
        P.on("scalar", lambda e: e.activation(out=x2b2_sb, in_=xb2_sb, func=AF.Square),
             wait=wsq, inc=sph)
        SPH_SQ = sph.n
        def fstat(e):
            for a in range(CCH):
                e.matmul(out=ps_sum[0:1, :], lhsT=onesb_sb[:], rhs=xb2_sb[:, a, :],
                         start=(a == 0), stop=(a == CCH - 1))
            ins = None
            for a in range(CCH):
                ins = e.matmul(out=ps_ssq[0:1, :], lhsT=onesb_sb[:], rhs=x2b2_sb[:, a, :],
                               start=(a == 0), stop=(a == CCH - 1))
            return ins
        wst = [(vv, VV_C), (sph, SPH_SQ)]
        if tt > 0:
            wst.append((sst, 4 * tt))
        P.on("tensor", fstat, wait=wst, inc=mst)
        def frows(e, tt=tt):
            e.activation(out=stat2_sb[0:1, 0, :], in_=ps_sum[0:1, :], func=AF.Copy,
                         scale=1.0 / C)
            e.activation(out=stat2_sb[0:1, 1, :], in_=ps_ssq[0:1, :], func=AF.Copy,
                         scale=1.0 / C)
            e.activation(out=brow2_sb[0:1, 0, :], in_=stat2_sb[0:1, 0, :], func=AF.Square)
            return e.activation(out=brow2_sb[0:1, 0, :], in_=brow2_sb[0:1, 0, :],
                                func=AF.Copy, scale=-1.0)
        wfr = [(mst, tt + 1)]
        if tt > 0:
            wfr += [(bb, BBR_LIST[tt - 1]), (hx, tt * CCH)]
        P.on("scalar", frows, wait=wfr, inc=sst)
        def frows2(e):
            return e.tensor_add(out=stat2_sb[0:1, 1, :], in0=stat2_sb[0:1, 1, :],
                                in1=brow2_sb[0:1, 0, :])
        P.on("vector", frows2, wait=[(sst, 4 * tt + 1)], inc=sst)
        def frows3(e):
            e.activation(out=stat2_sb[0:1, 1, :], in_=stat2_sb[0:1, 1, :], func=AF.Ln,
                         bias=eps_sb[0:1, :])
            return e.activation(out=stat2_sb[0:1, 1, :], in_=stat2_sb[0:1, 1, :],
                                func=AF.Exp, scale=-0.5)
        P.on("scalar", frows3, wait=[(sst, 4 * tt + 2)], inc=sst)
        def frows4(e):
            return e.tensor_mul(out=stat2_sb[0:1, 0, :], in0=stat2_sb[0:1, 0, :],
                                in1=stat2_sb[0:1, 1, :])
        P.on("vector", frows4, wait=[(sst, 4 * tt + 3)], inc=sst)
        def fbounce1(e, tt=tt):
            e.dma_start(out=rowscr[0:1, :], in_=stat2_sb[0:1, 1, :]).then_inc(bb.h, 16)
            return e.dma_start(out=rowscr[1:2, :], in_=stat2_sb[0:1, 0, :])
        bb.n += 16
        P.on("gpsimd", fbounce1, wait=[(sst, 4 * tt + 4)], inc=bb, k=16)
        BBW = bb.n
        def fbounce2(e, tt=tt):
            e.dma_start(out=brow2_sb[:, 0, :],
                        in_=rowscr[0:1, :].partition_broadcast(128).squeeze(1)).then_inc(bb.h, 16)
            return e.dma_start(out=brow2_sb[:, 1, :],
                               in_=rowscr[1:2, :].partition_broadcast(128).squeeze(1))
        bb.n += 16
        P.on("gpsimd", fbounce2, wait=[(bb, BBW)], inc=bb, k=16)
        BBR = bb.n
        BBR_LIST.append(BBR)
        for oc2 in range(CCH):
            def fh2(e, oc2=oc2, tt=tt):
                e.tensor_mul(out=h2t_sb[:, oc2, :], in0=x2c_sb[:, oc2, :],
                             in1=brow2_sb[:, 0, :])
                e.tensor_sub(out=h2t_sb[:, oc2, :], in0=h2t_sb[:, oc2, :],
                             in1=brow2_sb[:, 1, :])
                e.tensor_scalar(out=h2t_sb[:, oc2, :], in0=h2t_sb[:, oc2, :],
                                scalar1=B2_sb[:, oc2:oc2 + 1],
                                scalar2=mod_sb[:, 24 + oc2:25 + oc2],
                                op0=ALU.mult, op1=ALU.add)
                e.tensor_reduce(out=run_sb[:, 1:2], in_=h2t_sb[:, oc2, :], axis=AX,
                                op=ALU.max, apply_absolute_value=True)
                return e.tensor_max(out=run_sb[:, 0:1], in0=run_sb[:, 0:1],
                                    in1=run_sb[:, 1:2])
            wv = [(bb, BBR)]
            if tt >= 1 and oc2 == 0:
                wv.append((xx, 32 * tt))
            if tt == 0 and oc2 == 0:
                def finit(e):
                    return e.memset(run_sb[:], 0.0)
                P.on("vector", finit, wait=[], inc=vv)
            P.on("vector", fh2, wait=wv, inc=hx)
        P.on("gpsimd", lambda e, tt=tt: e.dma_start(
            out=fm(h2_o)[:, :, tt * 512:(tt + 1) * 512], in_=h2t_sb),
            wait=[(hx, (tt + 1) * CCH)], inc=xx, k=16)
    assert mm.n == 32 and evc.n == 32

    emit_absmax_export(P, run_sb[:, 0:1], mrow_sb, colscr.rearrange("a b -> (a b)"),
                       mh2_o, [(hx, 4 * CCH)], bb, vv, xx)

    P.on("sync", lambda e: None, wait=[(xx, xx.n)])
    P.emit()
    return nc



def emit_scale_prep(P, maxa_sb, srow_sb, scr2, scol_sb, icol_sb, qmax, wait, vv, bb):
    def fs(e):
        e.reduce_max(out=srow_sb[0:1, 0:1], in_=maxa_sb[0:1, :], axis=AX)
        e.tensor_scalar(out=srow_sb[0:1, 1:2], in0=srow_sb[0:1, 0:1],
                        scalar1=1.0 / qmax, scalar2=1e-12, op0=ALU.mult, op1=ALU.max)
        return e.reciprocal(out=srow_sb[0:1, 2:3], in_=srow_sb[0:1, 1:2])
    P.on("vector", fs, wait=wait, inc=vv)
    V1 = vv.n
    P.on("gpsimd", lambda e: e.dma_start(out=scr2.unsqueeze(0), in_=srow_sb[0:1, 1:3]),
         wait=[(vv, V1)], inc=bb, k=16)
    B1 = bb.n
    P.on("gpsimd", lambda e: e.dma_start(
        out=scol_sb[:], in_=scr2[0:1].unsqueeze(0).partition_broadcast(128).squeeze(1)),
        wait=[(bb, B1)], inc=bb, k=16)
    P.on("gpsimd", lambda e: e.dma_start(
        out=icol_sb[:], in_=scr2[1:2].unsqueeze(0).partition_broadcast(128).squeeze(1)),
        wait=[], inc=bb, k=16)
    return bb.n



def build_l3():
    nc = bass.Bass("TRN2", target_bir_lowering=False)
    h2_i = nc.dram_tensor("h2_i", [C, TC], F32, kind="ExternalInput").ap()
    mh2a = nc.dram_tensor("mh2a", [1, 8], F32, kind="ExternalInput").ap()
    wupT_d = nc.dram_tensor("wupT_d", [C, NI], BF16, kind="ExternalInput").ap()
    wgateT_d = nc.dram_tensor("wgateT_d", [C, NI], BF16, kind="ExternalInput").ap()
    gate_o = nc.dram_tensor("gate_o", [NI, TC], F32, kind="ExternalOutput").ap()
    up_o = nc.dram_tensor("up_o", [NI, TC], F32, kind="ExternalOutput").ap()
    mg_o = nc.dram_tensor("mg_o", [1, 1], F32, kind="ExternalOutput").ap()
    mu_o = nc.dram_tensor("mu_o", [1, 1], F32, kind="ExternalOutput").ap()
    scr2 = nc.dram_tensor("scr2", [2], F32, kind="Internal").ap()
    colscr = nc.dram_tensor("colscr", [128], F32, kind="Internal").ap()

    wg_sb = nc.alloc_sbuf_tensor("wg_sb", [128, CCH, NI], BF16)
    wu_sb = nc.alloc_sbuf_tensor("wu_sb", [128, CCH, NI], BF16)
    h2q_sb = nc.alloc_sbuf_tensor("h2q_sb", [128, CCH, TC], BF16)
    h2s_sb = nc.alloc_sbuf_tensor("h2s_sb", [128, 2, CCH, 512], F32)
    ev_sb = nc.alloc_sbuf_tensor("ev_sb", [128, 4, 512], F32)
    maxa_sb = nc.alloc_sbuf_tensor("maxa_sb", [1, 8], F32)
    srow_sb = nc.alloc_sbuf_tensor("srow_sb", [1, 4], F32)
    scol_sb = nc.alloc_sbuf_tensor("scol_sb", [128, 1], F32)
    icol_sb = nc.alloc_sbuf_tensor("icol_sb", [128, 1], F32)
    rung_sb = nc.alloc_sbuf_tensor("rung_sb", [128, 2], F32)
    runu_sb = nc.alloc_sbuf_tensor("runu_sb", [128, 2], F32)
    mrow_sb = nc.alloc_sbuf_tensor("mrow_sb", [1, 130], F32)

    ps = [nc.alloc_psum_tensor(f"ps{i}", [128, 512], F32) for i in range(2)]

    P = Prog(nc)
    ld = P.sem("ld")
    wld = P.sem("wld")
    wld2 = P.sem("wld2")
    hld = [P.sem(f"hld{i}") for i in range(2)]
    xsl = [P.sem(f"xsl{i}") for i in range(2)]
    vv = P.sem("vv")
    ss = P.sem("ss")
    qq = P.sem("qq")
    mm = P.sem("mm")
    evc = P.sem("evc")
    bb = P.sem("bb")
    xx = P.sem("xx")

    P.on("sync", lambda e: e.dma_start(out=maxa_sb[0:1, :], in_=mh2a), inc=ld, k=16)
    P.on("sync", lambda e: e.dma_start(out=wg_sb[:], in_=fm(wgateT_d)), inc=wld, k=16)
    P.on("sync", lambda e: e.dma_start(out=wu_sb[:], in_=fm(wupT_d)), inc=wld2, k=16)
    BB_SC = emit_scale_prep(P, maxa_sb, srow_sb, scr2, scol_sb, icol_sb,
                            32767.0, [(ld, 16)], vv, bb)
    P.on("vector", lambda e: e.memset(rung_sb[:], 0.0), inc=vv)
    P.on("vector", lambda e: e.memset(runu_sb[:], 0.0), inc=vv)
    VV_MAXBASE = vv.n

    for tt in range(NTT):
        w = [(qq, 2 * (tt - 1))] if tt >= 2 else []
        P.on("sync", lambda e, tt=tt: e.dma_start(
            out=h2s_sb[:, tt % 2], in_=fm(h2_i)[:, :, tt * 512:(tt + 1) * 512]),
            wait=w, inc=hld[tt % 2], k=16)
        def fqdq(e, tt=tt):
            e.tensor_scalar(out=h2s_sb[:, tt % 2], in0=h2s_sb[:, tt % 2],
                            scalar1=icol_sb[:], scalar2=MAGIC,
                            op0=ALU.mult, op1=ALU.add)
            return e.tensor_scalar(out=h2q_sb[:, :, tt * 512:(tt + 1) * 512],
                                   in0=h2s_sb[:, tt % 2], scalar1=MAGIC,
                                   scalar2=scol_sb[:], op0=ALU.subtract, op1=ALU.mult)
        P.on("vector", fqdq, wait=[(hld[tt % 2], 16 * (tt // 2 + 1)), (bb, BB_SC)], inc=qq, k=2)
    QQ_ALL = qq.n

    NG = 0
    for wi, (wsb, out_d, run) in enumerate(((wg_sb, gate_o, rung_sb),
                                            (wu_sb, up_o, runu_sb))):
        for ic in range(ICH):
            for tt in range(NTT):
                g = NG
                NG += 1
                wm_ = [(qq, QQ_ALL), ((wld, 16) if wi == 0 else (wld2, 16))]
                if g >= 2:
                    wm_.append((evc, g - 1))
                def fmm(e, wsb=wsb, ic=ic, tt=tt, g=g):
                    bank = ps[g % 2]
                    ins = None
                    for hc in range(CCH):
                        ins = e.matmul(out=bank[:],
                                       lhsT=wsb[:, hc, ic * 128:(ic + 1) * 128],
                                       rhs=h2q_sb[:, hc, tt * 512:(tt + 1) * 512],
                                       start=(hc == 0), stop=(hc == CCH - 1))
                    return ins
                P.on("tensor", fmm, wait=wm_, inc=mm)
                eng = "scalar"
                sl = 2 * (g % 2) + 0
                wev = [(mm, g + 1)]
                if g >= 2:
                    wev.append((xsl[g % 2], 16 * ((g - 2) // 2 + 1)))
                    wev.append((vv, VV_MAXBASE + g - 1))
                P.on(eng, lambda e, g=g, sl=sl: e.activation(
                    out=ev_sb[:, sl], in_=ps[g % 2][:], func=AF.Copy),
                    wait=wev, inc=evc)
                def fmax(e, g=g, sl=sl, run=run):
                    e.tensor_reduce(out=run[:, 1:2], in_=ev_sb[:, sl], axis=AX,
                                    op=ALU.max, apply_absolute_value=True)
                    return e.tensor_max(out=run[:, 0:1], in0=run[:, 0:1],
                                        in1=run[:, 1:2])
                P.on("vector", fmax, wait=[(evc, g + 1)], inc=vv)
                P.on("gpsimd", lambda e, out_d=out_d, ic=ic, tt=tt, g=g, sl=sl: e.dma_start(
                    out=out_d.rearrange("(a p) t -> p a t", p=128)[:, ic,
                                                                  tt * 512:(tt + 1) * 512],
                    in_=ev_sb[:, sl]), wait=[(evc, g + 1)], inc=xsl[g % 2], k=16)
    assert NG == 192

    emit_absmax_export(P, rung_sb[:, 0:1], mrow_sb, colscr, mg_o,
                       [(vv, vv.n)], bb, vv, xx)
    emit_absmax_export(P, runu_sb[:, 0:1], mrow_sb, colscr, mu_o,
                       [(vv, vv.n)], bb, vv, xx)
    P.on("sync", lambda e: None,
         wait=[(xx, xx.n), (xsl[0], xsl[0].n), (xsl[1], xsl[1].n)])
    P.emit()
    return nc



def build_l4():
    nc = bass.Bass("TRN2", target_bir_lowering=False)
    gate_i = nc.dram_tensor("gate_i", [NI, TC], F32, kind="ExternalInput").ap()
    mga = nc.dram_tensor("mga", [1, 8], F32, kind="ExternalInput").ap()
    g2_o = nc.dram_tensor("g2_o", [NI, TC], F32, kind="ExternalOutput").ap()
    mg2_o = nc.dram_tensor("mg2_o", [1, 1], F32, kind="ExternalOutput").ap()
    scr2 = nc.dram_tensor("scr2", [2], F32, kind="Internal").ap()
    colscr = nc.dram_tensor("colscr", [128], F32, kind="Internal").ap()

    gt_sb = nc.alloc_sbuf_tensor("gt_sb", [128, 2, 512], F32)
    gq_sb = nc.alloc_sbuf_tensor("gq_sb", [128, 512], F32)
    sg_sb = nc.alloc_sbuf_tensor("sg_sb", [128, 512], F32)
    g2_sb = nc.alloc_sbuf_tensor("g2_sb", [128, 2, 512], F32)
    maxa_sb = nc.alloc_sbuf_tensor("maxa_sb", [1, 8], F32)
    srow_sb = nc.alloc_sbuf_tensor("srow_sb", [1, 4], F32)
    scol_sb = nc.alloc_sbuf_tensor("scol_sb", [128, 1], F32)
    icol_sb = nc.alloc_sbuf_tensor("icol_sb", [128, 1], F32)
    run_sb = nc.alloc_sbuf_tensor("run_sb", [128, 2], F32)
    mrow_sb = nc.alloc_sbuf_tensor("mrow_sb", [1, 130], F32)

    P = Prog(nc)
    ld = P.sem("ld")
    gld = [P.sem(f"gld{i}") for i in range(2)]
    xsl = [P.sem(f"xsl{i}") for i in range(2)]
    vv = P.sem("vv")
    ss = P.sem("ss")
    pp = P.sem("pp")
    bb = P.sem("bb")
    xx = P.sem("xx")

    P.on("sync", lambda e: e.dma_start(out=maxa_sb[0:1, :], in_=mga), inc=ld, k=16)
    BB_SC = emit_scale_prep(P, maxa_sb, srow_sb, scr2, scol_sb, icol_sb,
                            32767.0, [(ld, 16)], vv, bb)
    P.on("vector", lambda e: e.memset(run_sb[:], 0.0), inc=vv)

    NTILE = ICH * NTT
    for g in range(NTILE):
        ic, tt = divmod(g, NTT)
        w = [(pp, g - 1)] if g >= 2 else []
        P.on("sync", lambda e, ic=ic, tt=tt, g=g: e.dma_start(
            out=gt_sb[:, g % 2, :],
            in_=gate_i.rearrange("(a p) t -> p a t", p=128)[:, ic,
                                                            tt * 512:(tt + 1) * 512]),
            wait=w, inc=gld[g % 2], k=16)
        P.on("scalar", lambda e, g=g: e.activation(
            out=gq_sb[:], in_=gt_sb[:, g % 2, :], func=AF.Copy, scale=icol_sb[:],
            bias=MAGIC), wait=[(gld[g % 2], 16 * (g // 2 + 1)), (bb, BB_SC), (pp, g)], inc=ss)
        SSQ = ss.n
        P.on("vector", lambda e: e.tensor_scalar(
            out=gq_sb[:], in0=gq_sb[:], scalar1=MAGIC, scalar2=scol_sb[:],
            op0=ALU.subtract, op1=ALU.mult), wait=[(ss, SSQ)], inc=vv)
        VVQ = vv.n
        P.on("scalar", lambda e: e.activation(out=sg_sb[:], in_=gq_sb[:],
                                              func=AF.Sigmoid),
             wait=[(vv, VVQ)], inc=ss)
        SSS = ss.n
        def fsq(e):
            e.tensor_scalar(out=sg_sb[:], in0=sg_sb[:], scalar1=65536.0, scalar2=MAGIC,
                            op0=ALU.mult, op1=ALU.add)
            e.tensor_scalar(out=sg_sb[:], in0=sg_sb[:], scalar1=MAGIC, scalar2=65535.0,
                            op0=ALU.subtract, op1=ALU.min)
            return e.tensor_scalar(out=sg_sb[:], in0=sg_sb[:], scalar1=1.0 / 65536.0,
                                   scalar2=None, op0=ALU.mult)
        P.on("vector", fsq, wait=[(ss, SSS)], inc=vv)
        VVS = vv.n
        def fg2(e, g=g):
            e.tensor_mul(out=g2_sb[:, g % 2, :], in0=gq_sb[:], in1=sg_sb[:])
            e.tensor_reduce(out=run_sb[:, 1:2], in_=g2_sb[:, g % 2, :], axis=AX,
                            op=ALU.max, apply_absolute_value=True)
            return e.tensor_max(out=run_sb[:, 0:1], in0=run_sb[:, 0:1],
                                in1=run_sb[:, 1:2])
        wg2 = [(vv, VVS)]
        if g >= 2:
            wg2.append((xsl[g % 2], 16 * ((g - 2) // 2 + 1)))
        P.on("vector", fg2, wait=wg2, inc=pp)
        P.on("gpsimd", lambda e, ic=ic, tt=tt, g=g: e.dma_start(
            out=g2_o.rearrange("(a p) t -> p a t", p=128)[:, ic, tt * 512:(tt + 1) * 512],
            in_=g2_sb[:, g % 2, :]), wait=[(pp, g + 1)], inc=xsl[g % 2], k=16)

    emit_absmax_export(P, run_sb[:, 0:1], mrow_sb, colscr, mg2_o,
                       [(pp, NTILE)], bb, vv, xx)
    P.on("sync", lambda e: None,
         wait=[(xx, xx.n), (xsl[0], xsl[0].n), (xsl[1], xsl[1].n)])
    P.emit()
    return nc



def build_l5():
    nc = bass.Bass("TRN2", target_bir_lowering=False)
    g2_i = nc.dram_tensor("g2_i", [NI, TC], F32, kind="ExternalInput").ap()
    up_i = nc.dram_tensor("up_i", [NI, TC], F32, kind="ExternalInput").ap()
    mg2a = nc.dram_tensor("mg2a", [1, 8], F32, kind="ExternalInput").ap()
    mua = nc.dram_tensor("mua", [1, 8], F32, kind="ExternalInput").ap()
    op_o = nc.dram_tensor("op_o", [NI, TC], F32, kind="ExternalOutput").ap()
    mo_o = nc.dram_tensor("mo_o", [1, 1], F32, kind="ExternalOutput").ap()
    scr2 = nc.dram_tensor("scr2", [2], F32, kind="Internal").ap()
    scr2b = nc.dram_tensor("scr2b", [2], F32, kind="Internal").ap()
    colscr = nc.dram_tensor("colscr", [128], F32, kind="Internal").ap()

    g2t_sb = nc.alloc_sbuf_tensor("g2t_sb", [128, 2, 512], F32)
    upt_sb = nc.alloc_sbuf_tensor("upt_sb", [128, 2, 512], F32)
    q1_sb = nc.alloc_sbuf_tensor("q1_sb", [128, 512], F32)
    q2_sb = nc.alloc_sbuf_tensor("q2_sb", [128, 512], F32)
    op_sb = nc.alloc_sbuf_tensor("op_sb", [128, 2, 512], F32)
    maxa_sb = nc.alloc_sbuf_tensor("maxa_sb", [1, 2, 8], F32)
    srow_sb = nc.alloc_sbuf_tensor("srow_sb", [1, 4], F32)
    srow2_sb = nc.alloc_sbuf_tensor("srow2_sb", [1, 4], F32)
    scol_sb = nc.alloc_sbuf_tensor("scol_sb", [128, 1], F32)
    icol_sb = nc.alloc_sbuf_tensor("icol_sb", [128, 1], F32)
    scol2_sb = nc.alloc_sbuf_tensor("scol2_sb", [128, 1], F32)
    icol2_sb = nc.alloc_sbuf_tensor("icol2_sb", [128, 1], F32)
    run_sb = nc.alloc_sbuf_tensor("run_sb", [128, 2], F32)
    mrow_sb = nc.alloc_sbuf_tensor("mrow_sb", [1, 130], F32)

    P = Prog(nc)
    ld = P.sem("ld")
    gld = [P.sem(f"gld{i}") for i in range(2)]
    uld = [P.sem(f"uld{i}") for i in range(2)]
    xsl = [P.sem(f"xsl{i}") for i in range(2)]
    vv = P.sem("vv")
    ss = P.sem("ss")
    pp = P.sem("pp")
    bb = P.sem("bb")
    xx = P.sem("xx")

    P.on("sync", lambda e: e.dma_start(out=maxa_sb[:, 0, :], in_=mg2a), inc=ld, k=16)
    P.on("sync", lambda e: e.dma_start(out=maxa_sb[:, 1, :], in_=mua), inc=ld, k=16)
    B1 = emit_scale_prep(P, maxa_sb[:, 0, :], srow_sb, scr2, scol_sb, icol_sb,
                         32767.0, [(ld, 16)], vv, bb)
    B2 = emit_scale_prep(P, maxa_sb[:, 1, :], srow2_sb, scr2b, scol2_sb, icol2_sb,
                         32767.0, [(ld, 32)], vv, bb)
    P.on("vector", lambda e: e.memset(run_sb[:], 0.0), inc=vv)

    NTILE = ICH * NTT
    for g in range(NTILE):
        ic, tt = divmod(g, NTT)
        w = [(pp, g - 1)] if g >= 2 else []
        src = lambda t: t.rearrange("(a p) t -> p a t", p=128)
        P.on("sync", lambda e, ic=ic, tt=tt, g=g: e.dma_start(
            out=g2t_sb[:, g % 2, :],
            in_=g2_i.rearrange("(a p) t -> p a t", p=128)[:, ic, tt * 512:(tt + 1) * 512]),
            wait=w, inc=gld[g % 2], k=16)
        P.on("sync", lambda e, ic=ic, tt=tt, g=g: e.dma_start(
            out=upt_sb[:, g % 2, :],
            in_=up_i.rearrange("(a p) t -> p a t", p=128)[:, ic, tt * 512:(tt + 1) * 512]),
            wait=([(pp, g - 1)] if g >= 2 else []), inc=uld[g % 2], k=16)
        P.on("scalar", lambda e, g=g: e.activation(
            out=q1_sb[:], in_=g2t_sb[:, g % 2, :], func=AF.Copy, scale=icol_sb[:],
            bias=MAGIC), wait=[(gld[g % 2], 16 * (g // 2 + 1)), (bb, B2), (pp, g)], inc=ss)
        P.on("scalar", lambda e, g=g: e.activation(
            out=q2_sb[:], in_=upt_sb[:, g % 2, :], func=AF.Copy, scale=icol2_sb[:],
            bias=MAGIC), wait=[(uld[g % 2], 16 * (g // 2 + 1))], inc=ss)
        S2 = ss.n
        def fq(e):
            e.tensor_scalar(out=q1_sb[:], in0=q1_sb[:], scalar1=MAGIC, scalar2=scol_sb[:],
                            op0=ALU.subtract, op1=ALU.mult)
            return e.tensor_scalar(out=q2_sb[:], in0=q2_sb[:], scalar1=MAGIC,
                                   scalar2=scol2_sb[:], op0=ALU.subtract, op1=ALU.mult)
        P.on("vector", fq, wait=[(ss, S2)], inc=vv)
        VQ = vv.n
        def fop(e, g=g):
            e.tensor_mul(out=op_sb[:, g % 2, :], in0=q1_sb[:], in1=q2_sb[:])
            e.tensor_reduce(out=run_sb[:, 1:2], in_=op_sb[:, g % 2, :], axis=AX,
                            op=ALU.max, apply_absolute_value=True)
            return e.tensor_max(out=run_sb[:, 0:1], in0=run_sb[:, 0:1],
                                in1=run_sb[:, 1:2])
        wo = [(vv, VQ)]
        if g >= 2:
            wo.append((xsl[g % 2], 16 * ((g - 2) // 2 + 1)))
        P.on("vector", fop, wait=wo, inc=pp)
        P.on("gpsimd", lambda e, ic=ic, tt=tt, g=g: e.dma_start(
            out=op_o.rearrange("(a p) t -> p a t", p=128)[:, ic, tt * 512:(tt + 1) * 512],
            in_=op_sb[:, g % 2, :]), wait=[(pp, g + 1)], inc=xsl[g % 2], k=16)

    emit_absmax_export(P, run_sb[:, 0:1], mrow_sb, colscr, mo_o,
                       [(pp, NTILE)], bb, vv, xx)
    P.on("sync", lambda e: None,
         wait=[(xx, xx.n), (xsl[0], xsl[0].n), (xsl[1], xsl[1].n)])
    P.emit()
    return nc



def build_l6():
    nc = bass.Bass("TRN2", target_bir_lowering=False)
    op_i = nc.dram_tensor("op_i", [NI, TC], F32, kind="ExternalInput").ap()
    moa = nc.dram_tensor("moa", [1, 8], F32, kind="ExternalInput").ap()
    x2_i = nc.dram_tensor("x2_i", [C, TC], F32, kind="ExternalInput").ap()
    wdT_d = nc.dram_tensor("wdT_d", [NI, C], BF16, kind="ExternalInput").ap()
    mod_i = nc.dram_tensor("mod_i", [128, 48], F32, kind="ExternalInput").ap()
    out_o = nc.dram_tensor("out_o", [C, TC], F32, kind="ExternalOutput").ap()
    scr2 = nc.dram_tensor("scr2", [2], F32, kind="Internal").ap()

    wd_sb = nc.alloc_sbuf_tensor("wd_sb", [128, ICH, C], BF16)
    ot_sb = nc.alloc_sbuf_tensor("ot_sb", [128, 4, 512], F32)
    ob_sb = nc.alloc_sbuf_tensor("ob_sb", [128, 4, 512], BF16)
    oq_sb = nc.alloc_sbuf_tensor("oq_sb", [128, 512], F32)
    x2t_sb = nc.alloc_sbuf_tensor("x2t_sb", [128, 2, CCH, 512], F32)
    os_sb = nc.alloc_sbuf_tensor("os_sb", [128, 2, CCH, 512], F32)
    mod_sb = nc.alloc_sbuf_tensor("mod_sb", [128, 48], F32)
    gb_sb = nc.alloc_sbuf_tensor("gb_sb", [128, 2, CCH], F32)
    maxa_sb = nc.alloc_sbuf_tensor("maxa_sb", [1, 8], F32)
    srow_sb = nc.alloc_sbuf_tensor("srow_sb", [1, 4], F32)
    scol_sb = nc.alloc_sbuf_tensor("scol_sb", [128, 1], F32)
    icol_sb = nc.alloc_sbuf_tensor("icol_sb", [128, 1], F32)

    ps = [nc.alloc_psum_tensor(f"ps{i}", [128, 512], F32) for i in range(CCH)]

    P = Prog(nc)
    ld = P.sem("ld")
    wld = P.sem("wld")
    old_ = [P.sem(f"old{i}") for i in range(4)]
    xld = [P.sem(f"xld{i}") for i in range(2)]
    xos = [P.sem(f"xos{i}") for i in range(2)]
    vv = P.sem("vv")
    ss = P.sem("ss")
    qrd = P.sem("qrd")
    mm = P.sem("mm")
    evc = P.sem("evc")
    bb = P.sem("bb")
    xx = P.sem("xx")

    P.on("sync", lambda e: e.dma_start(out=maxa_sb[0:1, :], in_=moa), inc=ld, k=16)
    P.on("sync", lambda e: e.dma_start(out=mod_sb[:], in_=mod_i), inc=ld, k=16)
    P.on("sync", lambda e: e.dma_start(
        out=wd_sb[:], in_=wdT_d.rearrange("(a p) o -> p a o", p=128)), inc=wld, k=16)
    BB_SC = emit_scale_prep(P, maxa_sb, srow_sb, scr2, scol_sb, icol_sb,
                            32767.0, [(ld, 16)], vv, bb)
    P.on("scalar", lambda e: e.activation(out=gb_sb[:, 0], in_=mod_sb[:, 40:48],
                                          func=AF.Abs), wait=[(ld, 32)], inc=ss)
    P.on("scalar", lambda e: e.sign(out=gb_sb[:, 1], in_=mod_sb[:, 40:48]),
         wait=[], inc=ss)
    SS_GB = ss.n
    def fgb(e):
        e.tensor_scalar(out=gb_sb[:, 0], in0=gb_sb[:, 0], scalar1=0.25, scalar2=None,
                        op0=ALU.max)
        return e.tensor_mul(out=gb_sb[:, 0], in0=gb_sb[:, 1], in1=gb_sb[:, 0])
    P.on("vector", fgb, wait=[(ss, SS_GB)], inc=vv)
    VV_GB = vv.n

    for tt in range(NTT):
        w = [(evc, (tt - 1) * CCH)] if tt >= 2 else []
        P.on("sync", lambda e, tt=tt: e.dma_start(
            out=x2t_sb[:, tt % 2], in_=fm(x2_i)[:, :, tt * 512:(tt + 1) * 512]),
            wait=w, inc=xld[tt % 2], k=16)
        for ic in range(ICH):
            g = tt * ICH + ic
            w = [(qrd, g - 3)] if g >= 4 else []
            P.on("scalar", lambda e, ic=ic, tt=tt, g=g: e.dma_start(
                out=ot_sb[:, g % 4, :],
                in_=op_i.rearrange("(a p) t -> p a t", p=128)[:, ic,
                                                              tt * 512:(tt + 1) * 512]),
                wait=w, inc=old_[g % 4], k=16)
            def fq(e, g=g):
                e.tensor_scalar(out=oq_sb[:], in0=ot_sb[:, g % 4, :],
                                scalar1=icol_sb[:], scalar2=MAGIC,
                                op0=ALU.mult, op1=ALU.add)
                return e.tensor_scalar(out=ob_sb[:, g % 4, :], in0=oq_sb[:],
                                       scalar1=MAGIC, scalar2=scol_sb[:],
                                       op0=ALU.subtract, op1=ALU.mult)
            P.on("vector", fq,
                 wait=[(old_[g % 4], 16 * (g // 4 + 1)), (bb, BB_SC)]
                      + ([(mm, (g - 3) * CCH)] if g >= 4 else []), inc=qrd)
            def fmm(e, ic=ic, tt=tt, g=g):
                ins = None
                for oc2 in range(CCH):
                    ins = e.matmul(out=ps[oc2][:],
                                   lhsT=wd_sb[:, ic, oc2 * 128:(oc2 + 1) * 128],
                                   rhs=ob_sb[:, g % 4, :],
                                   start=(ic == 0), stop=(ic == ICH - 1))
                return ins
            wm_ = [(qrd, g + 1), (wld, 16)]
            if ic == 0 and tt > 0:
                wm_.append((evc, tt * CCH))
            P.on("tensor", fmm, wait=wm_, inc=mm, k=CCH)
        for oc2 in range(CCH):
            P.on("vector", lambda e, oc2=oc2, tt=tt: e.scalar_tensor_tensor(
                out=os_sb[:, tt % 2, oc2, :], in0=ps[oc2][:],
                scalar=gb_sb[:, 0, oc2:oc2 + 1], in1=x2t_sb[:, tt % 2, oc2, :],
                op0=ALU.mult, op1=ALU.add),
                wait=[(mm, (tt + 1) * ICH * CCH), (xld[tt % 2], 16 * (tt // 2 + 1)), (vv, VV_GB)]
                     + ([(xos[tt % 2], 16 * (tt // 2))] if tt >= 2 else []),
                inc=evc)
        P.on("gpsimd", lambda e, tt=tt: e.dma_start(
            out=fm(out_o)[:, :, tt * 512:(tt + 1) * 512], in_=os_sb[:, tt % 2]),
            wait=[(evc, (tt + 1) * CCH)], inc=xos[tt % 2], k=16)

    P.on("sync", lambda e: None,
         wait=[(xos[0], xos[0].n), (xos[1], xos[1].n)])
    P.emit()
    return nc



def _t(a):
    return np.ascontiguousarray(np.asarray(a).T)


def _cols(v):
    return np.ascontiguousarray(np.asarray(v).reshape(CCH, 128).T)


def la_maps(inp):
    x = np.asarray(inp["L0_attn_x"])[0]
    xT = _t(x)
    eye = np.eye(64, dtype=np.float32)
    maps = []
    for r in range(NCORES):
        lo, hi = r * TC, (r + 1) * TC
        xs = np.zeros((C, WH), np.float32)
        xs[:, 1:TC + 1] = xT[:, lo:hi]
        if r > 0:
            xs[:, 0] = xT[:, lo - 1]
        if r < NCORES - 1:
            xs[:, TC + 1] = xT[:, hi]
        maskLR = np.zeros((128, 2), np.float32)
        maskLR[:, 0] = 1.0 if r > 0 else 0.0
        maskLR[:, 1] = 1.0 if r < NCORES - 1 else 0.0
        maps.append({
            "xT": xs,
            "mixk": _cols(inp["L0_attn_mix_k"]),
            "mixv": _cols(inp["L0_attn_mix_v"]),
            "mixr": _cols(inp["L0_attn_mix_r"]),
            "cvec": _cols(np.asarray(inp["L0_c"])[0]),
            "wmod": np.asarray(inp["L0_wmod"]),
            "maskLR": maskLR,
            "eye": eye,
            "wup_sh": np.asarray(inp["L0_ffn_w_up"])[r * 384:(r + 1) * 384],
            "wgate_sh": np.asarray(inp["L0_ffn_w_gate"])[r * 384:(r + 1) * 384],
            "wdown_sh": np.asarray(inp["L0_ffn_w_down"])[r * 128:(r + 1) * 128],
        })
    return maps


def lb_maps(inp, ra):
    wk, wv, wr = _t(inp["L0_attn_Wk"]), _t(inp["L0_attn_Wv"]), _t(inp["L0_attn_Wr"])
    sd = _cols(inp["L0_attn_spatial_decay"])
    return [{
        "h_i": ra[r]["h_o"], "u_i": ra[r]["u_o"],
        "wkT": wk, "wvT": wv, "wrT": wr, "sdec": sd,
    } for r in range(NCORES)]


def l2_maps(inp, ra, rb):
    af_all = np.stack([rb[j]["afin_o"] for j in range(NCORES)], axis=2)
    bf_all = np.stack([rb[j]["bfin_o"] for j in range(NCORES)], axis=2)
    af_all = np.ascontiguousarray(af_all, np.float32)
    bf_all = np.ascontiguousarray(bf_all, np.float32)
    x = np.asarray(inp["L0_attn_x"])[0]
    xT = _t(x)
    wo = _t(inp["L0_attn_Wo"])
    sd = _cols(inp["L0_attn_spatial_decay"])
    sf = _cols(inp["L0_attn_spatial_first"])
    maps = []
    for r in range(NCORES):
        lo, hi = r * TC, (r + 1) * TC
        xs = np.zeros((C, WH), np.float32)
        xs[:, 1:TC + 1] = xT[:, lo:hi]
        pwv = np.zeros((128, 8), np.float32)
        pmk = np.zeros((128, 8), np.float32)
        for j in range(NCORES):
            if j < r:
                pwv[:, j] = (r - 1 - j) * (float(TC) / T)
                pmk[:, j] = 1.0
        maps.append({
            "xT": xs,
            "ek_i": rb[r]["ek_o"], "ekv_i": rb[r]["ekv_o"], "sr_i": rb[r]["sr_o"],
            "af_all": af_all, "bf_all": bf_all,
            "pw": pwv, "pmask": pmk,
            "woT": wo, "mod_i": ra[r]["mod_o"],
            "sdec": sd, "sfirst": sf,
        })
    return maps




def l3_maps(inp, ra, r2):
    wup_d = np.concatenate([ra[j]["wupd_o"] for j in range(NCORES)], axis=0)
    wgate_d = np.concatenate([ra[j]["wgated_o"] for j in range(NCORES)], axis=0)
    wupT = np.ascontiguousarray(wup_d.T)
    wgateT = np.ascontiguousarray(wgate_d.T)
    mh2a = np.ascontiguousarray(
        np.concatenate([r2[j]["mh2_o"] for j in range(NCORES)], axis=1))
    return [{
        "h2_i": r2[r]["h2_o"], "mh2a": mh2a,
        "wupT_d": wupT, "wgateT_d": wgateT,
    } for r in range(NCORES)]


def l4_maps(r3):
    mga = np.ascontiguousarray(
        np.concatenate([r3[j]["mg_o"] for j in range(NCORES)], axis=1))
    return [{"gate_i": r3[r]["gate_o"], "mga": mga} for r in range(NCORES)]


def l5_maps(r3, r4):
    mg2a = np.ascontiguousarray(
        np.concatenate([r4[j]["mg2_o"] for j in range(NCORES)], axis=1))
    mua = np.ascontiguousarray(
        np.concatenate([r3[j]["mu_o"] for j in range(NCORES)], axis=1))
    return [{"g2_i": r4[r]["g2_o"], "up_i": r3[r]["up_o"],
             "mg2a": mg2a, "mua": mua} for r in range(NCORES)]


def l6_maps(inp, ra, r2, r5):
    wdown_d = np.concatenate([ra[j]["wdownd_o"] for j in range(NCORES)], axis=0)
    wdT = np.ascontiguousarray(wdown_d.T)
    moa = np.ascontiguousarray(
        np.concatenate([r5[j]["mo_o"] for j in range(NCORES)], axis=1))
    return [{"op_i": r5[r]["op_o"], "moa": moa, "x2_i": r2[r]["x2_o"],
             "wdT_d": wdT, "mod_i": ra[r]["mod_o"]} for r in range(NCORES)]


def run_all(inp):
    cores = list(range(NCORES))
    ra = run_bass_kernel_spmd(build_la(), la_maps(inp), core_ids=cores).results
    rb = run_bass_kernel_spmd(build_lb(), lb_maps(inp, ra), core_ids=cores).results
    r2 = run_bass_kernel_spmd(build_l2(), l2_maps(inp, ra, rb), core_ids=cores).results
    r3 = run_bass_kernel_spmd(build_l3(), l3_maps(inp, ra, r2), core_ids=cores).results
    r4 = run_bass_kernel_spmd(build_l4(), l4_maps(r3), core_ids=cores).results
    r5 = run_bass_kernel_spmd(build_l5(), l5_maps(r3, r4), core_ids=cores).results
    r6 = run_bass_kernel_spmd(build_l6(), l6_maps(inp, ra, r2, r5), core_ids=cores).results
    return ra, rb, r2, r3, r4, r5, r6


def kernel(**inp):
    ra, rb, r2, r3, r4, r5, r6 = run_all(inp)
    out = np.concatenate(
        [np.asarray(r6[r]["out_o"]).T for r in range(NCORES)], axis=0)
    return np.ascontiguousarray(out).reshape(1, T, C)


# version 02
# speedup vs baseline: 1.3302x; 1.3302x over previous
import numpy as np
import ml_dtypes
import concourse.bass as bass
import concourse.mybir as mybir
from concourse.bass_utils import run_bass_kernel_spmd

F32 = mybir.dt.float32
BF16 = mybir.dt.bfloat16
AF = mybir.ActivationFunctionType
ALU = mybir.AluOpType
AX = mybir.AxisListType.X

NCORES = 8
T = 16384
TC = T // NCORES
C = 1024
CCH = C // 128
NI = 3072
ICH = NI // 128
NTT = TC // 512
WH = TC + 2
MAGIC = float(np.float32(1.5 * 2 ** 23))
EPS = 1e-5
BF = ml_dtypes.bfloat16
ENGS = ("sync", "tensor", "vector", "scalar", "gpsimd")


class Sem:
    def __init__(self, nc, name):
        self.h = nc.alloc_semaphore(name)
        self.n = 0
        self.name = name


_CHAIN_SKIP = {"dma_start", "wait_ge", "wait_op", "sem_inc", "then_inc",
               "dma_start_transpose", "indirect_dma_start", "engine_nop",
               "value_load", "reg_mov", "nop"}


class Chained:

    def __init__(self, eng, sem):
        self._e = eng
        self._s = sem

    def __getattr__(self, k):
        a = getattr(self._e, k)
        if not callable(a) or k in _CHAIN_SKIP or k.startswith("_"):
            return a

        def wrap(*args, **kw):
            r = a(*args, **kw)
            if hasattr(r, "then_inc"):
                self._s.n += 1
                r.then_inc(self._s.h, 1)
                self._e.wait_ge(self._s.h, self._s.n)
            return r
        return wrap


class Prog:

    def __init__(self, nc):
        self.nc = nc
        self.steps = {e: [] for e in ENGS}
        self.chains = {e: Sem(nc, f"chain_{e}") for e in ("vector", "scalar", "gpsimd")}

    def sem(self, name):
        return Sem(self.nc, name)

    def on(self, eng, fn, wait=(), inc=None, k=1):
        for s, v in wait:
            assert v <= s.n, f"{eng}: wait {s.name}>={v} but only {s.n} reserved"
        if inc is not None:
            inc.n += k
        self.steps[eng].append((fn, tuple(wait), inc, k))

    def emit(self):
        def mk(ename):
            chain = self.chains.get(ename)

            def body(eng):
                ceng = Chained(eng, chain) if chain is not None else eng
                for fn, waits, inc, k in self.steps[ename]:
                    for s, v in waits:
                        eng.wait_ge(s.h, v)
                    ins = fn(ceng)
                    if inc is not None:
                        if chain is not None and k != 16:
                            eng.sem_inc(inc.h, k)
                        else:
                            assert ins is not None, ename
                            ins.then_inc(inc.h, k)
            return body

        with self.nc.Block() as block:
            block.sync(mk("sync"))
            block.tensor(mk("tensor"))
            block.vector(mk("vector"))
            block.scalar(mk("scalar"))
            block.gpsimd(mk("gpsimd"))


class Arena:

    def __init__(self, nc, name, nbytes):
        self.t = nc.alloc_sbuf_tensor(name, [128, nbytes], mybir.dt.uint8)
        self.off = 0
        self.nbytes = nbytes

    def carve(self, shape, dtype):
        esz = mybir.dt.size(dtype)
        n = int(np.prod(shape)) * esz
        off = (self.off + 511) // 512 * 512
        assert off + n <= self.nbytes, f"arena overflow {off}+{n} > {self.nbytes}"
        self.off = off + n
        ap = self.t.ap()[:, off:off + n].bitcast(dtype)
        if len(shape) > 1:
            dims = " ".join(f"d{i}" for i in range(len(shape)))
            kw = {f"d{i}": s for i, s in enumerate(shape)}
            ap = ap.rearrange(f"p ({dims}) -> p {dims}", **kw)
        return ap

    def reset(self, off=0):
        self.off = off


def exp_poly(eng, out_ap, x_ap, tmp_ap):
    eng.tensor_scalar(out=tmp_ap, in0=x_ap, scalar1=1.0 / 6.0, scalar2=0.5,
                      op0=ALU.mult, op1=ALU.add)
    eng.tensor_tensor(out=tmp_ap, in0=tmp_ap, in1=x_ap, op=ALU.mult)
    eng.tensor_scalar(out=tmp_ap, in0=tmp_ap, scalar1=1.0, scalar2=None, op0=ALU.add)
    eng.tensor_tensor(out=tmp_ap, in0=tmp_ap, in1=x_ap, op=ALU.mult)
    return eng.tensor_scalar(out=out_ap, in0=tmp_ap, scalar1=1.0, scalar2=None,
                             op0=ALU.add)


def emit_absmax_export(P, col_ap, row_sb, scr_dram, out_dram, wait, bb, vv, xx):
    P.on("gpsimd", lambda e: e.dma_start(out=scr_dram, in_=col_ap),
         wait=wait, inc=bb, k=16)
    B1 = bb.n
    P.on("gpsimd", lambda e: e.dma_start(out=row_sb[0:1, 0:128],
                                         in_=scr_dram.unsqueeze(0)),
         wait=[(bb, B1)], inc=bb, k=16)
    B2 = bb.n
    P.on("vector", lambda e: e.reduce_max(out=row_sb[0:1, 128:129],
                                          in_=row_sb[0:1, 0:128], axis=AX),
         wait=[(bb, B2)], inc=vv)
    V1 = vv.n
    P.on("gpsimd", lambda e: e.dma_start(out=out_dram, in_=row_sb[0:1, 128:129]),
         wait=[(vv, V1)], inc=xx, k=16)


def cvec_ap(dram_ap):
    return dram_ap.rearrange("(a p) -> p a", p=128)


def fm(dram_ap):
    return dram_ap.rearrange("(a p) t -> p a t", p=128)



def build_la():
    nc = bass.Bass("TRN2", target_bir_lowering=False)
    xT = nc.dram_tensor("xT", [C, WH], F32, kind="ExternalInput").ap()
    mixk = nc.dram_tensor("mixk", [128, CCH], F32, kind="ExternalInput").ap()
    mixv = nc.dram_tensor("mixv", [128, CCH], F32, kind="ExternalInput").ap()
    mixr = nc.dram_tensor("mixr", [128, CCH], F32, kind="ExternalInput").ap()
    cvec = nc.dram_tensor("cvec", [128, CCH], F32, kind="ExternalInput").ap()
    wmod = nc.dram_tensor("wmod", [C, 6 * C], F32, kind="ExternalInput").ap()
    maskLR = nc.dram_tensor("maskLR", [128, 2], F32, kind="ExternalInput").ap()
    eye = nc.dram_tensor("eye", [64, 64], F32, kind="ExternalInput").ap()
    wup_sh = nc.dram_tensor("wup_sh", [NI // 8, C], F32, kind="ExternalInput").ap()
    wgate_sh = nc.dram_tensor("wgate_sh", [NI // 8, C], F32, kind="ExternalInput").ap()
    wdown_sh = nc.dram_tensor("wdown_sh", [C // 8, NI], F32, kind="ExternalInput").ap()

    h_o = nc.dram_tensor("h_o", [C, WH], BF16, kind="ExternalOutput").ap()
    u_o = nc.dram_tensor("u_o", [3, C // 2, TC], BF16, kind="ExternalOutput").ap()
    mod_o = nc.dram_tensor("mod_o", [128, 48], F32, kind="ExternalOutput").ap()
    wupd_o = nc.dram_tensor("wupd_o", [NI // 8, C], BF16, kind="ExternalOutput").ap()
    wgated_o = nc.dram_tensor("wgated_o", [NI // 8, C], BF16, kind="ExternalOutput").ap()
    wdownd_o = nc.dram_tensor("wdownd_o", [C // 8, NI], BF16, kind="ExternalOutput").ap()

    modscr = nc.dram_tensor("modscr", [48, 128], F32, kind="Internal").ap()
    rowscr = nc.dram_tensor("rowscr", [2, WH], F32, kind="Internal").ap()

    h_sb = nc.alloc_sbuf_tensor("h_sb", [128, CCH, WH], BF16)
    mask_sb = nc.alloc_sbuf_tensor("mask_sb", [128, 2], BF16)
    maskf_sb = nc.alloc_sbuf_tensor("maskf_sb", [128, 2], F32)
    mix_sb = nc.alloc_sbuf_tensor("mix_sb", [128, 3, CCH], F32)
    scf_sb = nc.alloc_sbuf_tensor("scf_sb", [128, CCH], F32)
    scf2_sb = nc.alloc_sbuf_tensor("scf2_sb", [128, CCH, 1], F32)
    tmp8_sb = nc.alloc_sbuf_tensor("tmp8_sb", [128, CCH], F32)
    sc_sb = nc.alloc_sbuf_tensor("sc_sb", [128, CCH, 1], BF16)
    m48_sb = nc.alloc_sbuf_tensor("m48_sb", [48, 128], F32)
    mod_sb = nc.alloc_sbuf_tensor("mod_sb", [128, 48], F32)
    A_sb = nc.alloc_sbuf_tensor("A_sb", [128, CCH], F32)
    eye_sb = nc.alloc_sbuf_tensor("eye_sb", [64, 64], F32)
    evrow_sb = nc.alloc_sbuf_tensor("evrow_sb", [1, 512], F32)
    xb_sb = nc.alloc_sbuf_tensor("xb_sb", [128, CCH, 512], BF16)
    x2b_sb = nc.alloc_sbuf_tensor("x2b_sb", [128, CCH, 512], BF16)
    stat_sb = nc.alloc_sbuf_tensor("stat_sb", [1, 2, WH], F32)
    brow_sb = nc.alloc_sbuf_tensor("brow_sb", [128, 2, WH], F32)
    ht1_sb = nc.alloc_sbuf_tensor("ht1_sb", [128, WH], F32)
    ht2_sb = nc.alloc_sbuf_tensor("ht2_sb", [128, WH], F32)
    wm_sb = nc.alloc_sbuf_tensor("wm_sb", [128, 2, 512], F32)
    wmb_sb = nc.alloc_sbuf_tensor("wmb_sb", [128, 2, 512], BF16)
    dqw_sb = nc.alloc_sbuf_tensor("dqw_sb", [128, 2, 1024], F32)
    dqo_sb = nc.alloc_sbuf_tensor("dqo_sb", [128, 2, 1024], BF16)
    dqi_sb = nc.alloc_sbuf_tensor("dqi_sb", [128, 32], F32)
    dqs_sb = nc.alloc_sbuf_tensor("dqs_sb", [128, 32], F32)

    Z1 = Arena(nc, "zone1", 68 * 1024)
    x_sb = Z1.carve([CCH, WH], F32)
    Z1.reset()
    u_sb = Z1.carve([3, 4, TC], BF16)
    dscr_sb = Z1.carve([TC], BF16)

    ps_mod = [nc.alloc_psum_tensor(f"ps_mod{i}", [1, 512], F32) for i in range(2)]
    ps_modB = [nc.alloc_psum_tensor(f"ps_modB{i}", [1, 256], F32) for i in range(2)]
    ps_sum = nc.alloc_psum_tensor("ps_sum", [1, 512], F32)
    ps_ssq = nc.alloc_psum_tensor("ps_ssq", [1, 512], F32)
    ps_tr = nc.alloc_psum_tensor("ps_tr", [64, 48], F32)

    P = Prog(nc)
    ld = P.sem("ld")
    wld = [P.sem(f"wld{i}") for i in range(2)]
    cst = P.sem("cst")
    mmod = P.sem("mmod")
    vmd = P.sem("vmd")
    bb = P.sem("bb")
    ss = P.sem("ss")
    vv = P.sem("vv")
    rr = P.sem("rr")
    hh = P.sem("hh")
    mst = P.sem("mst")
    sst = P.sem("sst")
    hrd = P.sem("hrd")
    urd = P.sem("urd")
    xx = P.sem("xx")
    dql = [P.sem(f"dql{i}") for i in range(2)]
    xdq = P.sem("xdq")
    dqp = P.sem("dqp")
    dqv = P.sem("dqv")

    P.on("sync", lambda e: e.dma_start(out=x_sb, in_=fm(xT)), inc=ld, k=16)
    P.on("sync", lambda e: e.dma_start(out=maskf_sb[:], in_=maskLR), inc=ld, k=16)
    for i, t in enumerate((mixk, mixv, mixr)):
        P.on("sync", lambda e, i=i, t=t: e.dma_start(out=mix_sb[:, i, :], in_=t),
             inc=ld, k=16)
    P.on("sync", lambda e: e.dma_start(out=scf_sb[:], in_=cvec), inc=ld, k=16)
    P.on("sync", lambda e: e.dma_start(out=eye_sb[:], in_=eye), inc=ld, k=16)
    LD_X = LD_MASK = LD_MIX = LD_C = LD_EYE = 112

    def fsilu(e):
        coef = [1.0 / 3628800, -1.0 / 362880, 1.0 / 40320, -1.0 / 5040,
                1.0 / 720, -1.0 / 120, 1.0 / 24, -1.0 / 6, 0.5, -1.0, 1.0]
        e.tensor_scalar(out=tmp8_sb[:], in0=scf_sb[:], scalar1=coef[0],
                        scalar2=coef[1], op0=ALU.mult, op1=ALU.add)
        for k in range(2, len(coef)):
            e.tensor_tensor(out=tmp8_sb[:], in0=tmp8_sb[:], in1=scf_sb[:], op=ALU.mult)
            e.tensor_scalar(out=tmp8_sb[:], in0=tmp8_sb[:], scalar1=coef[k],
                            scalar2=None, op0=ALU.add)
        e.tensor_scalar(out=tmp8_sb[:], in0=tmp8_sb[:], scalar1=1.0, scalar2=None,
                        op0=ALU.add)
        e.reciprocal(out=tmp8_sb[:], in_=tmp8_sb[:])
        e.tensor_mul(out=scf2_sb[:, :, 0], in0=scf_sb[:], in1=tmp8_sb[:])
        return e.tensor_copy(out=sc_sb[:, :, 0], in_=scf2_sb[:, :, 0])
    P.on("vector", fsilu, wait=[(ld, LD_C)], inc=ss)
    SS_SILU = ss.n

    for it in range(96):
        j, hc = divmod(it, CCH)
        w = [(cst, it - 1), (mmod, it - 1)] if it >= 2 else []
        P.on("sync", lambda e, j=j, hc=hc, it=it: e.dma_start(
            out=wm_sb[:, it % 2, :],
            in_=wmod.rearrange("(a p) j -> p a j", p=128)[:, hc, j * 512:(j + 1) * 512]),
            wait=w, inc=wld[it % 2], k=16)
        wc = [(wld[it % 2], 16 * (it // 2 + 1))]
        if it >= 2:
            wc.append((mmod, it - 1))
        P.on("gpsimd", lambda e, it=it: e.tensor_copy(
            out=wmb_sb[:, it % 2, :], in_=wm_sb[:, it % 2, :]), wait=wc, inc=cst)
        wm = [(cst, it + 1), (ss, SS_SILU)]
        if hc == 0 and j >= 2:
            wm.append((vmd, j - 1))
        if j in (4, 5, 10, 11):
            def fmodf(e, j=j, hc=hc, it=it):
                e.matmul(out=ps_mod[j % 2][0:1, 0:256], lhsT=scf2_sb[:, hc, :],
                         rhs=wm_sb[:, it % 2, 0:256],
                         start=(hc == 0), stop=(hc == CCH - 1))
                return e.matmul(out=ps_modB[j % 2][0:1, :], lhsT=scf2_sb[:, hc, :],
                                rhs=wm_sb[:, it % 2, 256:512],
                                start=(hc == 0), stop=(hc == CCH - 1))
            P.on("tensor", fmodf, wait=wm, inc=mmod)
        else:
            P.on("tensor", lambda e, j=j, hc=hc, it=it: e.matmul(
                out=ps_mod[j % 2][:], lhsT=sc_sb[:, hc, :], rhs=wmb_sb[:, it % 2, :],
                start=(hc == 0), stop=(hc == CCH - 1)), wait=wm, inc=mmod)
        if hc == CCH - 1:
            wv = [(mmod, 8 * (j + 1))] + ([(bb, 16 * j)] if j > 0 else [])
            if j in (4, 5, 10, 11):
                def fev(e, j=j):
                    e.tensor_copy(out=evrow_sb[0:1, 0:256], in_=ps_mod[j % 2][0:1, 0:256])
                    return e.tensor_copy(out=evrow_sb[0:1, 256:512],
                                         in_=ps_modB[j % 2][0:1, :])
                P.on("vector", fev, wait=wv, inc=vmd)
            else:
                P.on("vector", lambda e, j=j: e.tensor_copy(out=evrow_sb[0:1, :],
                                                            in_=ps_mod[j % 2][:]),
                     wait=wv, inc=vmd)
            P.on("gpsimd", lambda e, j=j: e.dma_start(
                out=modscr.rearrange("a b -> (a b)").unsqueeze(0)[:, j * 512:(j + 1) * 512],
                in_=evrow_sb[0:1, :]), wait=[(vmd, j + 1)], inc=bb, k=16)
    BB_MOD = bb.n
    P.on("gpsimd", lambda e: e.dma_start(out=m48_sb[:], in_=modscr),
         wait=[(bb, BB_MOD)], inc=bb, k=16)
    BB_M48 = bb.n
    P.on("tensor", lambda e: e.transpose(out=ps_tr[:], in_=m48_sb[:, 0:64],
                                         identity=eye_sb[0:48, 0:48]),
         wait=[(bb, BB_M48), (ld, LD_EYE)], inc=mmod)
    P.on("vector", lambda e: e.tensor_copy(out=mod_sb[0:64, :], in_=ps_tr[0:64, :]),
         wait=[(mmod, 97)], inc=vmd)
    P.on("tensor", lambda e: e.transpose(out=ps_tr[:], in_=m48_sb[:, 64:128],
                                         identity=eye_sb[0:48, 0:48]),
         wait=[(vmd, 13)], inc=mmod)
    P.on("vector", lambda e: e.tensor_copy(out=mod_sb[64:128, :], in_=ps_tr[0:64, :]),
         wait=[(mmod, 98)], inc=vmd)
    P.on("vector", lambda e: e.tensor_scalar(out=A_sb[:], in0=mod_sb[:, 8:16],
                                             scalar1=1.0, scalar2=None, op0=ALU.add),
         wait=[(vmd, 14)], inc=vmd)
    VMD_DONE = vmd.n
    P.on("vector", lambda e: e.tensor_copy(out=mask_sb[:], in_=maskf_sb[:]),
         wait=[(ld, LD_MASK)], inc=vv)
    VV_MASK = vv.n

    onesb_sb = nc.alloc_sbuf_tensor("onesb_sb", [128, 1], BF16)
    eps_sb = nc.alloc_sbuf_tensor("eps_sb", [128, 1], F32)
    P.on("vector", lambda e: e.memset(onesb_sb[:], 1.0), inc=vv)
    P.on("vector", lambda e: e.memset(eps_sb[:], EPS), inc=vv)
    VV_ONES = vv.n
    tiles = [(st * 512, 512) for st in range(4)] + [(2048, 2)]
    VV_CAST = {}
    SS_SQ = {}
    for st, (t0, tw) in enumerate(tiles):
        wv = [(ld, LD_X)]
        if st > 0:
            wv += [(ss, SS_SQ[st - 1]), (mst, st)]
        P.on("vector", lambda e, t0=t0, tw=tw: e.tensor_copy(
            out=xb_sb[:, :, 0:tw], in_=x_sb[:, :, t0:t0 + tw]), wait=wv, inc=vv)
        VV_CAST[st] = vv.n
        ws = [(vv, VV_CAST[st])]
        if st > 0:
            ws.append((mst, st))
        P.on("scalar", lambda e, tw=tw: e.activation(
            out=x2b_sb[:, :, 0:tw], in_=xb_sb[:, :, 0:tw], func=AF.Square),
            wait=ws, inc=ss)
        SS_SQ[st] = ss.n
        def fstat(e, tw=tw):
            for a in range(CCH):
                e.matmul(out=ps_sum[0:1, 0:tw], lhsT=onesb_sb[:], rhs=xb_sb[:, a, 0:tw],
                         start=(a == 0), stop=(a == CCH - 1))
            ins = None
            for a in range(CCH):
                ins = e.matmul(out=ps_ssq[0:1, 0:tw], lhsT=onesb_sb[:],
                               rhs=x2b_sb[:, a, 0:tw], start=(a == 0), stop=(a == CCH - 1))
            return ins
        wt = [(vv, VV_CAST[st]), (ss, SS_SQ[st]), (vv, VV_ONES)]
        if st > 0:
            wt.append((sst, 2 * st))
        P.on("tensor", fstat, wait=wt, inc=mst)
        P.on("scalar", lambda e, t0=t0, tw=tw: e.activation(
            out=stat_sb[0:1, 0, t0:t0 + tw], in_=ps_sum[0:1, 0:tw], func=AF.Copy),
            wait=[(mst, st + 1)], inc=sst)
        P.on("scalar", lambda e, t0=t0, tw=tw: e.activation(
            out=stat_sb[0:1, 1, t0:t0 + tw], in_=ps_ssq[0:1, 0:tw], func=AF.Copy),
            wait=[], inc=sst)
    assert mst.n == 5 and sst.n == 10

    P.on("scalar", lambda e: e.activation(out=stat_sb[0:1, 0, :], in_=stat_sb[0:1, 0, :],
                                          func=AF.Copy, scale=1.0 / C),
         wait=[(sst, 10)], inc=ss)
    P.on("scalar", lambda e: e.activation(out=ht1_sb[0:1, :], in_=stat_sb[0:1, 0, :],
                                          func=AF.Square), wait=[], inc=ss)
    SS_MSQ = ss.n
    P.on("vector", lambda e: e.scalar_tensor_tensor(
        out=stat_sb[0:1, 1, :], in0=stat_sb[0:1, 1, :], scalar=1.0 / C,
        in1=ht1_sb[0:1, :], op0=ALU.mult, op1=ALU.subtract),
        wait=[(ss, SS_MSQ)], inc=rr)
    RR_VAR = rr.n
    P.on("scalar", lambda e: e.activation(out=stat_sb[0:1, 1, :], in_=stat_sb[0:1, 1, :],
                                          func=AF.Ln, bias=eps_sb[0:1, :]),
         wait=[(rr, RR_VAR), (vv, VV_ONES)], inc=ss)
    P.on("scalar", lambda e: e.activation(out=stat_sb[0:1, 1, :], in_=stat_sb[0:1, 1, :],
                                          func=AF.Exp, scale=-0.5), wait=[], inc=ss)
    SS_RSTD = ss.n
    P.on("gpsimd", lambda e: e.tensor_mul(out=ht1_sb[0:1, :], in0=stat_sb[0:1, 0, :],
                                          in1=stat_sb[0:1, 1, :]),
         wait=[(ss, SS_RSTD)], inc=rr)
    RR_MH = rr.n
    P.on("gpsimd", lambda e: e.dma_start(out=rowscr[0:1, :], in_=stat_sb[0:1, 1, :]),
         wait=[], inc=bb, k=16)
    P.on("gpsimd", lambda e: e.dma_start(out=rowscr[1:2, :], in_=ht1_sb[0:1, :]),
         wait=[], inc=bb, k=16)
    BB_W = bb.n
    P.on("gpsimd", lambda e: e.dma_start(
        out=brow_sb[:, 0, :], in_=rowscr[0:1, :].partition_broadcast(128).squeeze(1)),
        wait=[(bb, BB_W)], inc=bb, k=16)
    P.on("gpsimd", lambda e: e.dma_start(
        out=brow_sb[:, 1, :], in_=rowscr[1:2, :].partition_broadcast(128).squeeze(1)),
        wait=[], inc=bb, k=16)
    BB_ROWS = bb.n

    VV_HT1 = {}
    for c in range(CCH):
        wv = [(bb, BB_ROWS)] if c == 0 else [(hh, c)]
        P.on("vector", lambda e, c=c: e.tensor_mul(
            out=ht1_sb[:], in0=x_sb[:, c, :], in1=brow_sb[:, 0, :]), wait=wv, inc=vv)
        VV_HT1[c] = vv.n
        P.on("gpsimd", lambda e: e.tensor_sub(out=ht2_sb[:], in0=ht1_sb[:],
                                              in1=brow_sb[:, 1, :]),
             wait=[(vv, VV_HT1[c]), (rr, RR_MH)], inc=hh)
        def fh(e, c=c):
            e.tensor_scalar(out=h_sb[:, c, :], in0=ht2_sb[:],
                            scalar1=A_sb[:, c:c + 1], scalar2=mod_sb[:, c:c + 1],
                            op0=ALU.mult, op1=ALU.add)
            e.tensor_mul(out=h_sb[:, c, 0:1], in0=h_sb[:, c, 0:1], in1=mask_sb[:, 0:1])
            return e.tensor_mul(out=h_sb[:, c, WH - 1:WH], in0=h_sb[:, c, WH - 1:WH],
                                in1=mask_sb[:, 1:2])
        P.on("vector", fh, wait=[(hh, c + 1), (vmd, VMD_DONE), (vv, VV_MASK)], inc=hrd)
    assert hrd.n == CCH

    P.on("gpsimd", lambda e: e.dma_start(out=fm(h_o), in_=h_sb[:]),
         wait=[(hrd, CCH)], inc=xx, k=16)
    P.on("gpsimd", lambda e: e.dma_start(out=mod_o, in_=mod_sb[:]),
         wait=[(vmd, VMD_DONE)], inc=xx, k=16)

    for c in range(4):
        hp = h_sb[:, c, 0:TC] if c < 2 else h_sb[:, c, 2:WH]
        wd = [(hrd, CCH)] + ([(urd, 4 * c)] if c > 0 else [])
        P.on("vector", lambda e, c=c, hp=hp: e.tensor_sub(
            out=dscr_sb, in0=h_sb[:, c, 1:TC + 1], in1=hp), wait=wd, inc=urd)
        U_D = urd.n
        for ip in range(3):
            eng = "vector"
            P.on(eng, lambda e, c=c, ip=ip, hp=hp: e.scalar_tensor_tensor(
                out=u_sb[:, ip, c, :], in0=dscr_sb, scalar=mix_sb[:, ip, c:c + 1],
                in1=hp, op0=ALU.mult, op1=ALU.add),
                wait=[(urd, U_D), (ld, LD_MIX)], inc=urd)
    assert urd.n == 16
    P.on("gpsimd", lambda e: e.dma_start(
        out=u_o.rearrange("i (a p) t -> p i a t", p=128), in_=u_sb),
        wait=[(urd, 16)], inc=xx, k=16)

    for t9 in range(9):
        w = [(dqp, t9 - 1)] if t9 >= 2 else []
        if t9 < 3:
            src = wup_sh.rearrange("(a p) c -> p a c", p=128)[:, t9, :]
        elif t9 < 6:
            src = wgate_sh.rearrange("(a p) c -> p a c", p=128)[:, t9 - 3, :]
        else:
            src = wdown_sh[:, (t9 - 6) * 1024:(t9 - 5) * 1024]
        P.on("sync", lambda e, src=src, t9=t9: e.dma_start(
            out=dqw_sb[:, t9 % 2, :], in_=src), wait=w, inc=dql[t9 % 2], k=16)
        def fdq1(e, t9=t9):
            e.tensor_reduce(out=dqi_sb[:],
                            in_=dqw_sb[:, t9 % 2, :].rearrange("p (b k) -> p b k", k=32),
                            axis=AX, op=ALU.max, apply_absolute_value=True)
            e.tensor_scalar(out=dqs_sb[:], in0=dqi_sb[:],
                            scalar1=1.0 / 7.0, scalar2=1e-12,
                            op0=ALU.mult, op1=ALU.max)
            return e.reciprocal(out=dqi_sb[:], in_=dqs_sb[:])
        wq1 = [(dql[t9 % 2], 16 * (t9 // 2 + 1))] + ([(dqp, t9)] if t9 >= 1 else [])
        P.on("vector", fdq1, wait=wq1, inc=dqv)
        def fdq2(e, t9=t9):
            b1 = dqi_sb.ap().unsqueeze(2).broadcast_to([128, 32, 32])
            w3 = dqw_sb[:, t9 % 2, :].rearrange("p (b k) -> p b k", k=32)
            e.tensor_tensor(out=w3, in0=w3, in1=b1, op=ALU.mult)
            e.tensor_scalar(out=dqw_sb[:, t9 % 2, :], in0=dqw_sb[:, t9 % 2, :],
                            scalar1=MAGIC, scalar2=None, op0=ALU.add)
            e.tensor_scalar(out=dqw_sb[:, t9 % 2, :], in0=dqw_sb[:, t9 % 2, :],
                            scalar1=MAGIC, scalar2=None, op0=ALU.subtract)
            b2 = dqs_sb.ap().unsqueeze(2).broadcast_to([128, 32, 32])
            return e.tensor_tensor(
                out=dqo_sb[:, t9 % 2, :].rearrange("p (b k) -> p b k", k=32),
                in0=w3, in1=b2, op=ALU.mult)
        P.on("vector", fdq2,
             wait=[(dqv, t9 + 1)] + ([(xdq, 16 * (t9 - 1))] if t9 >= 2 else []),
             inc=dqp)
        if t9 < 3:
            P.on("gpsimd", lambda e, t9=t9: e.dma_start(
                out=wupd_o.rearrange("(a p) c -> p a c", p=128)[:, t9, :],
                in_=dqo_sb[:, t9 % 2, :]), wait=[(dqp, t9 + 1)], inc=xdq, k=16)
        elif t9 < 6:
            P.on("gpsimd", lambda e, t9=t9: e.dma_start(
                out=wgated_o.rearrange("(a p) c -> p a c", p=128)[:, t9 - 3, :],
                in_=dqo_sb[:, t9 % 2, :]), wait=[(dqp, t9 + 1)], inc=xdq, k=16)
        else:
            P.on("gpsimd", lambda e, t9=t9: e.dma_start(
                out=wdownd_o[:, (t9 - 6) * 1024:(t9 - 5) * 1024],
                in_=dqo_sb[:, t9 % 2, :]), wait=[(dqp, t9 + 1)], inc=xdq, k=16)

    P.on("sync", lambda e: None, wait=[(xx, xx.n), (xdq, xdq.n)])
    P.emit()
    return nc



def build_lb():
    nc = bass.Bass("TRN2", target_bir_lowering=False)
    h_i = nc.dram_tensor("h_i", [C, WH], BF16, kind="ExternalInput").ap()
    u_i = nc.dram_tensor("u_i", [3, C // 2, TC], BF16, kind="ExternalInput").ap()
    wkT = nc.dram_tensor("wkT", [C, C], F32, kind="ExternalInput").ap()
    wvT = nc.dram_tensor("wvT", [C, C], F32, kind="ExternalInput").ap()
    wrT = nc.dram_tensor("wrT", [C, C], F32, kind="ExternalInput").ap()
    sdec = nc.dram_tensor("sdec", [128, CCH], F32, kind="ExternalInput").ap()

    ek_o = nc.dram_tensor("ek_o", [C, TC], F32, kind="ExternalOutput").ap()
    ekv_o = nc.dram_tensor("ekv_o", [C, TC], F32, kind="ExternalOutput").ap()
    sr_o = nc.dram_tensor("sr_o", [C, TC], F32, kind="ExternalOutput").ap()
    afin_o = nc.dram_tensor("afin_o", [128, CCH], F32, kind="ExternalOutput").ap()
    bfin_o = nc.dram_tensor("bfin_o", [128, CCH], F32, kind="ExternalOutput").ap()

    h_sb = nc.alloc_sbuf_tensor("h_sb", [128, CCH, WH], BF16)
    u_sb = nc.alloc_sbuf_tensor("u_sb", [128, 3, 4, TC], BF16)
    wsf_sb = nc.alloc_sbuf_tensor("wsf_sb", [128, 2, 3, CCH, 128], F32)
    wsb_sb = nc.alloc_sbuf_tensor("wsb_sb", [128, 2, 3, CCH, 128], BF16)
    sd_sb = nc.alloc_sbuf_tensor("sd_sb", [128, CCH], F32)
    ew_sb = nc.alloc_sbuf_tensor("ew_sb", [128, CCH], F32)
    tmp8_sb = nc.alloc_sbuf_tensor("tmp8_sb", [128, CCH], F32)
    ek_sb = nc.alloc_sbuf_tensor("ek_sb", [128, TC], F32)
    v_sb = nc.alloc_sbuf_tensor("v_sb", [128, TC], F32)
    ekv_sb = nc.alloc_sbuf_tensor("ekv_sb", [128, TC], F32)
    sr_sb = nc.alloc_sbuf_tensor("sr_sb", [128, TC], F32)
    ewt_sb = nc.alloc_sbuf_tensor("ewt_sb", [128, TC], F32)
    asc_sb = nc.alloc_sbuf_tensor("asc_sb", [128, TC], F32)
    bsc_sb = nc.alloc_sbuf_tensor("bsc_sb", [128, TC], F32)
    afin_sb = nc.alloc_sbuf_tensor("afin_sb", [128, CCH], F32)
    bfin_sb = nc.alloc_sbuf_tensor("bfin_sb", [128, CCH], F32)

    ps_kvr = [nc.alloc_psum_tensor(f"ps_kvr{i}", [128, 512], F32) for i in range(4)]

    P = Prog(nc)
    ld = P.sem("ld")
    awl = [P.sem(f"awl{i}") for i in range(2)]
    wcs = P.sem("wcs")
    vv = P.sem("vv")
    mkv = P.sem("mkv")
    evc = P.sem("evc")
    vkv = P.sem("vkv")
    xx = P.sem("xx")
    xek = P.sem("xek")
    xekv = P.sem("xekv")
    xsr = P.sem("xsr")

    P.on("sync", lambda e: e.dma_start(out=h_sb[:], in_=fm(h_i)), inc=ld, k=16)
    P.on("sync", lambda e: e.dma_start(
        out=u_sb[:], in_=u_i.rearrange("i (a p) t -> p i a t", p=128)), inc=ld, k=16)
    P.on("sync", lambda e: e.dma_start(out=sd_sb[:], in_=sdec), inc=ld, k=16)
    LD_H = LD_U = LD_SD = 48

    def few(e):
        e.tensor_scalar(out=ew_sb[:], in0=sd_sb[:], scalar1=1.0 / T, scalar2=None,
                        op0=ALU.mult)
        return exp_poly(e, ew_sb[:], ew_sb[:], tmp8_sb[:])
    P.on("vector", few, wait=[(ld, LD_SD)], inc=vv)
    VV_EW = vv.n

    def emit_wslice(oc):
        for i, t in enumerate((wkT, wvT, wrT)):
            w = []
            if oc >= 2:
                w += [(wcs, 3 * (oc - 1)), (mkv, 12 * (oc - 1))]
            P.on("sync", lambda e, t=t, oc=oc, i=i: e.dma_start(
                out=wsf_sb[:, oc % 2, i, :, :],
                in_=fm(t)[:, :, oc * 128:(oc + 1) * 128]), wait=w, inc=awl[oc % 2], k=16)
        wc = [(awl[oc % 2], 48 * (oc // 2 + 1))]
        if oc >= 2:
            wc.append((mkv, 12 * (oc - 1)))
        P.on("scalar", lambda e, oc=oc: e.activation(
            out=wsb_sb.ap()[:, oc % 2].rearrange("p i a c -> p (i a c)"),
            in_=wsf_sb.ap()[:, oc % 2].rearrange("p i a c -> p (i a c)"),
            func=AF.Copy), wait=wc, inc=wcs, k=3)

    emit_wslice(0)
    emit_wslice(1)

    for oc in range(CCH):
        for ip in range(3):
            for tt in range(NTT):
                g = oc * 12 + ip * 4 + tt
                w = [(ld, LD_U), (wcs, 3 * (oc + 1))]
                if g >= 4:
                    w.append((evc, g - 3))
                def fkvr(e, oc=oc, ip=ip, tt=tt, g=g):
                    bank = ps_kvr[g % 4]
                    ins = None
                    for cin in range(CCH):
                        if cin < 4:
                            rhs = u_sb[:, ip, cin, tt * 512:(tt + 1) * 512]
                        else:
                            rhs = h_sb[:, cin, 1 + tt * 512:1 + (tt + 1) * 512]
                        ins = e.matmul(out=bank[:], lhsT=wsb_sb[:, oc % 2, ip, cin, :],
                                       rhs=rhs, start=(cin == 0), stop=(cin == CCH - 1))
                    return ins
                P.on("tensor", fkvr, wait=w, inc=mkv)
                we = [(mkv, g + 1)]
                if oc > 0 and tt == 0:
                    we.append(((xek, xekv, xsr)[ip], 16 * oc))
                    if ip <= 1:
                        we.append((vkv, 6 * oc))
                def fev(e, ip=ip, tt=tt, g=g):
                    bank = ps_kvr[g % 4]
                    sl = slice(tt * 512, (tt + 1) * 512)
                    if ip == 0:
                        return e.activation(out=ek_sb[:, sl], in_=bank[:], func=AF.Exp)
                    if ip == 1:
                        return e.activation(out=v_sb[:, sl], in_=bank[:], func=AF.Copy)
                    return e.activation(out=sr_sb[:, sl], in_=bank[:], func=AF.Sigmoid)
                P.on("scalar", fev, wait=we, inc=evc)
        if oc + 2 < CCH:
            emit_wslice(oc + 2)
        def fewt(e, oc=oc):
            e.memset(ewt_sb[:], 1.0)
            return e.tensor_scalar(out=ewt_sb[:], in0=ewt_sb[:],
                                   scalar1=ew_sb[:, oc:oc + 1], scalar2=None,
                                   op0=ALU.mult)
        P.on("vector", fewt, wait=[(vv, VV_EW), (evc, oc * 12 + 4)], inc=vkv)
        P.on("vector", lambda e: e.tensor_mul(out=ekv_sb[:], in0=ek_sb[:], in1=v_sb[:]),
             wait=[(evc, oc * 12 + 8)], inc=vkv)
        P.on("vector", lambda e: e.tensor_tensor_scan(
            out=asc_sb[:], data0=ewt_sb[:], data1=ekv_sb[:], initial=0.0,
            op0=ALU.mult, op1=ALU.add), wait=[], inc=vkv)
        P.on("vector", lambda e, oc=oc: e.tensor_copy(
            out=afin_sb[:, oc:oc + 1], in_=asc_sb[:, TC - 1:TC]), wait=[], inc=vkv)
        P.on("vector", lambda e: e.tensor_tensor_scan(
            out=bsc_sb[:], data0=ewt_sb[:], data1=ek_sb[:], initial=0.0,
            op0=ALU.mult, op1=ALU.add), wait=[], inc=vkv)
        P.on("vector", lambda e, oc=oc: e.tensor_copy(
            out=bfin_sb[:, oc:oc + 1], in_=bsc_sb[:, TC - 1:TC]), wait=[], inc=vkv)
        P.on("gpsimd", lambda e, oc=oc: e.dma_start(
            out=fm(ek_o)[:, oc, :], in_=ek_sb[:]), wait=[(vkv, 6 * oc + 5)], inc=xek, k=16)
        P.on("gpsimd", lambda e, oc=oc: e.dma_start(
            out=fm(ekv_o)[:, oc, :], in_=ekv_sb[:]), wait=[(vkv, 6 * oc + 3)], inc=xekv, k=16)
        P.on("gpsimd", lambda e, oc=oc: e.dma_start(
            out=fm(sr_o)[:, oc, :], in_=sr_sb[:]), wait=[(evc, oc * 12 + 12)], inc=xsr, k=16)
    assert mkv.n == 96 and evc.n == 96 and vkv.n == 48

    P.on("gpsimd", lambda e: e.dma_start(out=afin_o, in_=afin_sb[:]),
         wait=[(vkv, 48)], inc=xx, k=16)
    P.on("gpsimd", lambda e: e.dma_start(out=bfin_o, in_=bfin_sb[:]),
         wait=[(vkv, 48)], inc=xx, k=16)
    P.on("sync", lambda e: None, wait=[(xx, xx.n), (xek, xek.n), (xekv, xekv.n),
                                       (xsr, xsr.n)])
    P.emit()
    return nc



def build_l2():
    nc = bass.Bass("TRN2", target_bir_lowering=False)
    xT = nc.dram_tensor("xT", [C, WH], F32, kind="ExternalInput").ap()
    ek_i = nc.dram_tensor("ek_i", [C, TC], F32, kind="ExternalInput").ap()
    ekv_i = nc.dram_tensor("ekv_i", [C, TC], F32, kind="ExternalInput").ap()
    sr_i = nc.dram_tensor("sr_i", [C, TC], F32, kind="ExternalInput").ap()
    af_all = nc.dram_tensor("af_all", [128, CCH, 8], F32, kind="ExternalInput").ap()
    bf_all = nc.dram_tensor("bf_all", [128, CCH, 8], F32, kind="ExternalInput").ap()
    pw = nc.dram_tensor("pw", [128, 8], F32, kind="ExternalInput").ap()
    pmask = nc.dram_tensor("pmask", [128, 8], F32, kind="ExternalInput").ap()
    woT = nc.dram_tensor("woT", [C, C], F32, kind="ExternalInput").ap()
    mod_i = nc.dram_tensor("mod_i", [128, 48], F32, kind="ExternalInput").ap()
    sdec = nc.dram_tensor("sdec", [128, CCH], F32, kind="ExternalInput").ap()
    sfirst = nc.dram_tensor("sfirst", [128, CCH], F32, kind="ExternalInput").ap()

    x2_o = nc.dram_tensor("x2_o", [C, TC], F32, kind="ExternalOutput").ap()
    h2_o = nc.dram_tensor("h2_o", [C, TC], F32, kind="ExternalOutput").ap()
    mh2_o = nc.dram_tensor("mh2_o", [1, 1], F32, kind="ExternalOutput").ap()

    colscr = nc.dram_tensor("colscr", [128, 1], F32, kind="Internal").ap()

    y_sb = nc.alloc_sbuf_tensor("y_sb", [128, CCH, TC], BF16)
    wob_sb = nc.alloc_sbuf_tensor("wob_sb", [128, CCH, C], BF16)
    a2_sb = nc.alloc_sbuf_tensor("a2_sb", [128, TC + 1], F32)
    b2_sb = nc.alloc_sbuf_tensor("b2_sb", [128, TC + 1], F32)
    num_sb = nc.alloc_sbuf_tensor("num_sb", [128, TC], F32)
    den_sb = nc.alloc_sbuf_tensor("den_sb", [128, TC], F32)
    ewt_sb = nc.alloc_sbuf_tensor("ewt_sb", [128, TC], F32)
    xt_sb = nc.alloc_sbuf_tensor("xt_sb", [128, 4, 512], F32)
    h2t_sb = nc.alloc_sbuf_tensor("h2t_sb", [128, 2, 512], F32)
    af_sb = nc.alloc_sbuf_tensor("af_sb", [128, 2, CCH, 8], F32)
    d3_sb = nc.alloc_sbuf_tensor("d3_sb", [128, CCH, 8], F32)
    t3_sb = nc.alloc_sbuf_tensor("t3_sb", [128, CCH, 8], F32)
    ain_sb = nc.alloc_sbuf_tensor("ain_sb", [128, 2, CCH], F32)
    pw_sb = nc.alloc_sbuf_tensor("pw_sb", [128, 2, 8], F32)
    sd_sb = nc.alloc_sbuf_tensor("sd_sb", [128, CCH], F32)
    sf_sb = nc.alloc_sbuf_tensor("sf_sb", [128, CCH], F32)
    ew_sb = nc.alloc_sbuf_tensor("ew_sb", [128, CCH], F32)
    eu_sb = nc.alloc_sbuf_tensor("eu_sb", [128, CCH], F32)
    tmp8_sb = nc.alloc_sbuf_tensor("tmp8_sb", [128, CCH], F32)
    mod_sb = nc.alloc_sbuf_tensor("mod_sb", [128, 48], F32)
    ga_sb = nc.alloc_sbuf_tensor("ga_sb", [128, CCH], F32)
    gat_sb = nc.alloc_sbuf_tensor("gat_sb", [128, 2, CCH], F32)
    B2_sb = nc.alloc_sbuf_tensor("B2_sb", [128, CCH], F32)
    run_sb = nc.alloc_sbuf_tensor("run_sb", [128, 2], F32)
    mrow_sb = nc.alloc_sbuf_tensor("mrow_sb", [1, 130], F32)
    stat2_sb = nc.alloc_sbuf_tensor("stat2_sb", [1, 3, 512], F32)
    onesb_sb = nc.alloc_sbuf_tensor("onesb_sb", [128, 1], BF16)
    ones1_sb = nc.alloc_sbuf_tensor("ones1_sb", [1, 128], F32)
    eps_sb = nc.alloc_sbuf_tensor("eps_sb", [1, 1], F32)
    ZW = Arena(nc, "zoneW", 33 * 1024)
    wof_sb = ZW.carve([CCH, C], F32)
    ZW.reset()
    x2c_sb = ZW.carve([2, CCH, 512], F32)
    ZS = Arena(nc, "zoneS", 48 * 1024)
    ek2_sb = ZS.carve([2, TC], F32)
    ekv2_sb = ZS.carve([2, TC], F32)
    sr2_sb = ZS.carve([2, TC], F32)
    ZS.reset()
    xb2_sb = ZS.carve([CCH, 512], BF16)
    x2b2_sb = ZS.carve([CCH, 512], BF16)

    ps_o = [nc.alloc_psum_tensor(f"ps_o{i}", [128, 512], F32) for i in range(4)]
    ps_sum = nc.alloc_psum_tensor("ps_sum", [1, 512], F32)
    ps_ssq = nc.alloc_psum_tensor("ps_ssq", [1, 512], F32)
    ps_bc = [nc.alloc_psum_tensor(f"ps_bc{i}", [128, 512], F32) for i in range(2)]

    P = Prog(nc)
    ld = P.sem("ld")
    awo = P.sem("awo")
    cek = P.sem("cek")
    cekv = P.sem("cekv")
    csr = P.sem("csr")
    xld = [P.sem(f"xld{i}") for i in range(4)]
    hxx = [P.sem(f"hxx{i}") for i in range(2)]
    x2x = P.sem("x2x")
    vv = P.sem("vv")
    sph = P.sem("sph")
    vph = P.sem("vph")
    mm = P.sem("mm")
    evc = P.sem("evc")
    qq = P.sem("qq")
    mst = P.sem("mst")
    bcm = P.sem("bcm")
    sst = P.sem("sst")
    bb = P.sem("bb")
    hx = P.sem("hx")
    xx = P.sem("xx")

    P.on("sync", lambda e: e.dma_start(out=af_sb[:, 0], in_=af_all), inc=ld, k=16)
    P.on("sync", lambda e: e.dma_start(out=af_sb[:, 1], in_=bf_all), inc=ld, k=16)
    P.on("sync", lambda e: e.dma_start(out=pw_sb[:, 0], in_=pw), inc=ld, k=16)
    P.on("sync", lambda e: e.dma_start(out=pw_sb[:, 1], in_=pmask), inc=ld, k=16)
    P.on("sync", lambda e: e.dma_start(out=mod_sb[:], in_=mod_i), inc=ld, k=16)
    P.on("sync", lambda e: e.dma_start(out=sd_sb[:], in_=sdec), inc=ld, k=16)
    P.on("sync", lambda e: e.dma_start(out=sf_sb[:], in_=sfirst), inc=ld, k=16)
    LD_SMALL = ld.n
    P.on("gpsimd", lambda e: e.dma_start(out=wof_sb, in_=fm(woT)), inc=awo, k=16)

    for _oc in (0, 1):
        P.on("sync", lambda e, oc=_oc: e.dma_start(
            out=ek2_sb[:, oc % 2, :], in_=fm(ek_i)[:, oc, :]), inc=cek, k=16)
        P.on("scalar", lambda e, oc=_oc: e.dma_start(
            out=ekv2_sb[:, oc % 2, :], in_=fm(ekv_i)[:, oc, :]), inc=cekv, k=16)
        P.on("scalar", lambda e, oc=_oc: e.dma_start(
            out=sr2_sb[:, oc % 2, :], in_=fm(sr_i)[:, oc, :]), inc=csr, k=16)

    def fsetup(e):
        e.memset(onesb_sb[:], 1.0)
        e.memset(ones1_sb[:], 1.0)
        e.memset(eps_sb[:], EPS)
        e.tensor_scalar(out=ew_sb[:], in0=sd_sb[:], scalar1=1.0 / T, scalar2=None,
                        op0=ALU.mult)
        exp_poly(e, ew_sb[:], ew_sb[:], tmp8_sb[:])
        e.tensor_scalar(out=eu_sb[:], in0=sf_sb[:], scalar1=1.0 / T, scalar2=None,
                        op0=ALU.mult)
        exp_poly(e, eu_sb[:], eu_sb[:], tmp8_sb[:])
        e.tensor_scalar(out=B2_sb[:], in0=mod_sb[:, 32:40], scalar1=1.0, scalar2=None,
                        op0=ALU.add)
        return e.tensor_tensor(
            out=t3_sb[:], in0=sd_sb.ap().unsqueeze(2).broadcast_to([128, CCH, 8]),
            in1=pw_sb.ap()[:, 0, :].unsqueeze(1).broadcast_to([128, CCH, 8]),
            op=ALU.mult)
    P.on("vector", fsetup, wait=[(ld, LD_SMALL)], inc=vv)
    VV_T3 = vv.n
    P.on("scalar", lambda e: e.activation(out=d3_sb[:], in_=t3_sb[:], func=AF.Exp),
         wait=[(vv, VV_T3)], inc=sph)
    P.on("scalar", lambda e: e.activation(out=gat_sb[:, 0], in_=mod_sb[:, 16:24],
                                          func=AF.Abs), wait=[(ld, LD_SMALL)], inc=sph)
    P.on("scalar", lambda e: e.sign(out=gat_sb[:, 1], in_=mod_sb[:, 16:24]),
         wait=[], inc=sph)
    SPH_GA = sph.n

    def fprefix(e):
        e.tensor_tensor(out=d3_sb[:], in0=d3_sb[:],
                        in1=pw_sb.ap()[:, 1, :].unsqueeze(1).broadcast_to([128, CCH, 8]),
                        op=ALU.mult)
        e.tensor_tensor(out=t3_sb[:], in0=af_sb[:, 0], in1=d3_sb[:], op=ALU.mult)
        e.tensor_reduce(out=ain_sb[:, 0], in_=t3_sb[:], axis=AX, op=ALU.add)
        e.tensor_tensor(out=t3_sb[:], in0=af_sb[:, 1], in1=d3_sb[:], op=ALU.mult)
        e.tensor_reduce(out=ain_sb[:, 1], in_=t3_sb[:], axis=AX, op=ALU.add)
        e.tensor_scalar(out=gat_sb[:, 0], in0=gat_sb[:, 0], scalar1=0.25,
                        scalar2=None, op0=ALU.max)
        e.tensor_tensor(out=ga_sb[:], in0=gat_sb[:, 1], in1=gat_sb[:, 0],
                        op=ALU.mult)
        return e.memset(run_sb[:], 0.0)
    P.on("vector", fprefix, wait=[(sph, SPH_GA)], inc=vv)
    VV_PREFIX = vv.n

    P.on("scalar", lambda e: e.activation(
        out=wob_sb.ap().rearrange("p a b -> p (a b)"),
        in_=wof_sb.rearrange("p a b -> p (a b)"), func=AF.Copy),
        wait=[(awo, 16)], inc=qq)
    QQ_WO = qq.n

    AM, BM, YM = {}, {}, {}

    def emit_loads(oc):
        wek = [(vph, BM[oc - 2])] if oc >= 2 else []
        wsr = [(vph, YM[oc - 2])] if oc >= 2 else []
        P.on("sync", lambda e, oc=oc: e.dma_start(
            out=ek2_sb[:, oc % 2, :], in_=fm(ek_i)[:, oc, :]), wait=wek, inc=cek, k=16)
        P.on("scalar", lambda e, oc=oc: e.dma_start(
            out=ekv2_sb[:, oc % 2, :], in_=fm(ekv_i)[:, oc, :]), wait=wek,
            inc=cekv, k=16)
        P.on("scalar", lambda e, oc=oc: e.dma_start(
            out=sr2_sb[:, oc % 2, :], in_=fm(sr_i)[:, oc, :]), wait=wsr, inc=csr, k=16)

    def emit_A(oc):
        def fA(e, oc=oc):
            sl = oc % 2
            e.memset(ewt_sb[:], 1.0)
            e.tensor_scalar(out=ewt_sb[:], in0=ewt_sb[:], scalar1=ew_sb[:, oc:oc + 1],
                            scalar2=None, op0=ALU.mult)
            e.tensor_tensor_scan(out=a2_sb[:, 1:TC + 1], data0=ewt_sb[:],
                                 data1=ekv2_sb[:, sl, :],
                                 initial=ain_sb[:, 0, oc:oc + 1],
                                 op0=ALU.mult, op1=ALU.add)
            e.tensor_tensor_scan(out=b2_sb[:, 1:TC + 1], data0=ewt_sb[:],
                                 data1=ek2_sb[:, sl, :],
                                 initial=ain_sb[:, 1, oc:oc + 1],
                                 op0=ALU.mult, op1=ALU.add)
            e.tensor_copy(out=a2_sb[:, 0:1], in_=ain_sb[:, 0, oc:oc + 1])
            return e.tensor_copy(out=b2_sb[:, 0:1], in_=ain_sb[:, 1, oc:oc + 1])
        P.on("vector", fA, wait=[(cek, 16 * (oc + 1)), (cekv, 16 * (oc + 1)),
                                 (vv, VV_PREFIX)], inc=vph)
        AM[oc] = vph.n

    def emit_B(oc):
        def fB(e, oc=oc):
            sl = oc % 2
            e.scalar_tensor_tensor(out=num_sb[:], in0=ekv2_sb[:, sl, :],
                                   scalar=eu_sb[:, oc:oc + 1], in1=a2_sb[:, 0:TC],
                                   op0=ALU.mult, op1=ALU.add)
            return e.scalar_tensor_tensor(out=den_sb[:], in0=ek2_sb[:, sl, :],
                                          scalar=eu_sb[:, oc:oc + 1],
                                          in1=b2_sb[:, 0:TC],
                                          op0=ALU.mult, op1=ALU.add)
        w = [(sph, SPH_GA + oc)] if oc >= 1 else []
        P.on("vector", fB, wait=w, inc=vph)
        BM[oc] = vph.n
        def frden(e):
            e.activation(out=den_sb[:], in_=den_sb[:], func=AF.Ln)
            return e.activation(out=den_sb[:], in_=den_sb[:], func=AF.Exp, scale=-1.0)
        P.on("scalar", frden, wait=[(vph, BM[oc])], inc=sph)

    def emit_Y(oc):
        def fy(e, oc=oc):
            e.tensor_mul(out=num_sb[:], in0=num_sb[:], in1=den_sb[:])
            return e.tensor_mul(out=y_sb[:, oc, :], in0=sr2_sb[:, oc % 2, :],
                                in1=num_sb[:])
        P.on("vector", fy, wait=[(sph, SPH_GA + oc + 1), (csr, 16 * (oc + 1))],
             inc=vph)
        YM[oc] = vph.n

    emit_A(0)
    emit_B(0)
    for oc in range(1, CCH):
        emit_A(oc)
        emit_Y(oc - 1)
        if oc + 1 < CCH:
            emit_loads(oc + 1)
        emit_B(oc)
    emit_Y(CCH - 1)
    assert vph.n == 3 * CCH

    scq = P.sem("scq")

    def emit_omm(tt):
        sl = tt % 2
        for oc2 in range(CCH):
            g = tt * CCH + oc2
            w = [(evc, g - 3)] if g >= 4 else []
            P.on("sync", lambda e, tt=tt, oc2=oc2, g=g: e.dma_start(
                out=xt_sb[:, g % 4, :],
                in_=fm(xT)[:, oc2, 1 + tt * 512:1 + (tt + 1) * 512]),
                wait=w, inc=xld[g % 4], k=16)
            wm_ = [(qq, QQ_WO), (xld[g % 4], 16 * (g // 4 + 1))]
            if g >= 4:
                wm_.append((evc, g - 3))
            def fomm(e, tt=tt, oc2=oc2, g=g):
                bank = ps_o[g % 4]
                ins = None
                for cin in range(CCH):
                    if g == 0:
                        e.wait_ge(vph.h, YM[cin])
                    ins = e.matmul(out=bank[:],
                                   lhsT=wob_sb[:, cin, oc2 * 128:(oc2 + 1) * 128],
                                   rhs=y_sb[:, cin, tt * 512:(tt + 1) * 512],
                                   start=(cin == 0), stop=(cin == CCH - 1))
                return ins
            P.on("tensor", fomm, wait=wm_, inc=mm)
            wev = [(mm, g + 1)]
            if oc2 == 0 and tt >= 2:
                wev += [(x2x, 16 * (tt - 1)), (scq, 2 * (tt - 1))]
            P.on("vector", lambda e, sl=sl, oc2=oc2, g=g: e.scalar_tensor_tensor(
                out=x2c_sb[:, sl, oc2, :], in0=ps_o[g % 4][:],
                scalar=ga_sb[:, oc2:oc2 + 1],
                in1=xt_sb[:, g % 4, :], op0=ALU.mult, op1=ALU.add),
                wait=wev, inc=evc)
        P.on("gpsimd", lambda e, tt=tt, sl=sl: e.dma_start(
            out=fm(x2_o)[:, :, tt * 512:(tt + 1) * 512], in_=x2c_sb[:, sl]),
            wait=[(evc, (tt + 1) * CCH)], inc=x2x, k=16)

    def emit_castsq(tt):
        sl = tt % 2
        P.on("scalar", lambda e, sl=sl: e.activation(
            out=xb2_sb.rearrange("p a t -> p (a t)"),
            in_=x2c_sb[:, sl].rearrange("p a t -> p (a t)"), func=AF.Copy),
            wait=[(evc, (tt + 1) * CCH)] + ([(mst, tt)] if tt > 0 else []), inc=scq)
        P.on("scalar", lambda e: e.activation(
            out=x2b2_sb.rearrange("p a t -> p (a t)"),
            in_=xb2_sb.rearrange("p a t -> p (a t)"), func=AF.Square), inc=scq)

    def emit_stats(tt):
        def fstat(e):
            for a in range(CCH):
                e.matmul(out=ps_sum[0:1, :], lhsT=onesb_sb[:], rhs=xb2_sb[:, a, :],
                         start=(a == 0), stop=(a == CCH - 1))
            ins = None
            for a in range(CCH):
                ins = e.matmul(out=ps_ssq[0:1, :], lhsT=onesb_sb[:],
                               rhs=x2b2_sb[:, a, :],
                               start=(a == 0), stop=(a == CCH - 1))
            return ins
        wst = [(scq, 2 * tt + 2)]
        if tt > 0:
            wst.append((sst, 3 * tt - 2))
        P.on("tensor", fstat, wait=wst, inc=mst)

    def emit_ln2(tt):
        sl = tt % 2
        def flad1(e):
            e.tensor_scalar(out=stat2_sb[0:1, 0, :], in0=ps_sum[0:1, :],
                            scalar1=1.0 / C, scalar2=None, op0=ALU.mult)
            e.tensor_scalar(out=stat2_sb[0:1, 1, :], in0=ps_ssq[0:1, :],
                            scalar1=1.0 / C, scalar2=None, op0=ALU.mult)
            e.tensor_mul(out=stat2_sb[0:1, 2, :], in0=stat2_sb[0:1, 0, :],
                         in1=stat2_sb[0:1, 0, :])
            return e.tensor_sub(out=stat2_sb[0:1, 1, :], in0=stat2_sb[0:1, 1, :],
                                in1=stat2_sb[0:1, 2, :])
        P.on("vector", flad1, wait=[(mst, tt + 1)], inc=sst)
        def flad2(e):
            e.activation(out=stat2_sb[0:1, 1, :], in_=stat2_sb[0:1, 1, :], func=AF.Ln,
                         bias=eps_sb[0:1, :])
            return e.activation(out=stat2_sb[0:1, 1, :], in_=stat2_sb[0:1, 1, :],
                                func=AF.Exp, scale=-0.5)
        P.on("scalar", flad2, wait=[(sst, 3 * tt + 1)], inc=sst)
        def flad3(e):
            return e.tensor_mul(out=stat2_sb[0:1, 0, :], in0=stat2_sb[0:1, 0, :],
                                in1=stat2_sb[0:1, 1, :])
        P.on("vector", flad3, wait=[(sst, 3 * tt + 2)], inc=sst)
        def fbc(e):
            e.matmul(out=ps_bc[0][:, 0:256], lhsT=ones1_sb[:],
                     rhs=stat2_sb[0:1, 1, 0:256], start=True, stop=True)
            e.matmul(out=ps_bc[0][:, 256:512], lhsT=ones1_sb[:],
                     rhs=stat2_sb[0:1, 1, 256:512], start=True, stop=True)
            e.matmul(out=ps_bc[1][:, 0:256], lhsT=ones1_sb[:],
                     rhs=stat2_sb[0:1, 0, 0:256], start=True, stop=True)
            return e.matmul(out=ps_bc[1][:, 256:512], lhsT=ones1_sb[:],
                            rhs=stat2_sb[0:1, 0, 256:512], start=True, stop=True)
        wbc = [(sst, 3 * tt + 3)]
        if tt > 0:
            wbc.append((hx, tt * CCH))
        P.on("tensor", fbc, wait=wbc, inc=bcm)
        for oc2 in range(CCH):
            def fh2(e, oc2=oc2, sl=sl):
                s2 = oc2 % 2
                e.tensor_mul(out=h2t_sb[:, s2, :], in0=x2c_sb[:, sl, oc2, :],
                             in1=ps_bc[0][:])
                e.tensor_sub(out=h2t_sb[:, s2, :], in0=h2t_sb[:, s2, :],
                             in1=ps_bc[1][:])
                e.tensor_scalar(out=h2t_sb[:, s2, :], in0=h2t_sb[:, s2, :],
                                scalar1=B2_sb[:, oc2:oc2 + 1],
                                scalar2=mod_sb[:, 24 + oc2:25 + oc2],
                                op0=ALU.mult, op1=ALU.add)
                e.tensor_reduce(out=run_sb[:, 1:2], in_=h2t_sb[:, s2, :], axis=AX,
                                op=ALU.max, apply_absolute_value=True)
                return e.tensor_max(out=run_sb[:, 0:1], in0=run_sb[:, 0:1],
                                    in1=run_sb[:, 1:2])
            gh = tt * CCH + oc2
            wv = [(bcm, tt + 1)]
            if gh >= 2:
                wv.append((hxx[gh % 2], 16 * ((gh - 2) // 2 + 1)))
            P.on("vector", fh2, wait=wv, inc=hx)
            P.on("scalar", lambda e, tt=tt, oc2=oc2: e.dma_start(
                out=fm(h2_o)[:, oc2, tt * 512:(tt + 1) * 512],
                in_=h2t_sb[:, oc2 % 2, :]),
                wait=[(hx, tt * CCH + oc2 + 1)], inc=hxx[(tt * CCH + oc2) % 2], k=16)

    for tt in range(NTT):
        emit_omm(tt)
        if tt >= 1:
            emit_stats(tt - 1)
        emit_castsq(tt)
        if tt >= 1:
            emit_ln2(tt - 1)
    emit_stats(NTT - 1)
    emit_ln2(NTT - 1)
    assert mm.n == 32 and evc.n == 32

    emit_absmax_export(P, run_sb[:, 0:1], mrow_sb, colscr.rearrange("a b -> (a b)"),
                       mh2_o, [(hx, 4 * CCH)], bb, vv, xx)

    P.on("sync", lambda e: None,
         wait=[(xx, xx.n), (x2x, x2x.n), (hxx[0], hxx[0].n), (hxx[1], hxx[1].n)])
    P.emit()
    return nc



def emit_scale_prep(P, maxa_sb, srow_sb, scr2, scol_sb, icol_sb, qmax, wait, vv, bb):
    def fs(e):
        e.reduce_max(out=srow_sb[0:1, 0:1], in_=maxa_sb[0:1, :], axis=AX)
        e.tensor_scalar(out=srow_sb[0:1, 1:2], in0=srow_sb[0:1, 0:1],
                        scalar1=1.0 / qmax, scalar2=1e-12, op0=ALU.mult, op1=ALU.max)
        return e.reciprocal(out=srow_sb[0:1, 2:3], in_=srow_sb[0:1, 1:2])
    P.on("vector", fs, wait=wait, inc=vv)
    V1 = vv.n
    P.on("gpsimd", lambda e: e.dma_start(out=scr2.unsqueeze(0), in_=srow_sb[0:1, 1:3]),
         wait=[(vv, V1)], inc=bb, k=16)
    B1 = bb.n
    P.on("gpsimd", lambda e: e.dma_start(
        out=scol_sb[:], in_=scr2[0:1].unsqueeze(0).partition_broadcast(128).squeeze(1)),
        wait=[(bb, B1)], inc=bb, k=16)
    P.on("gpsimd", lambda e: e.dma_start(
        out=icol_sb[:], in_=scr2[1:2].unsqueeze(0).partition_broadcast(128).squeeze(1)),
        wait=[], inc=bb, k=16)
    return bb.n



def build_l3():
    nc = bass.Bass("TRN2", target_bir_lowering=False)
    h2_i = nc.dram_tensor("h2_i", [C, TC], F32, kind="ExternalInput").ap()
    mh2a = nc.dram_tensor("mh2a", [1, 8], F32, kind="ExternalInput").ap()
    wupT_d = nc.dram_tensor("wupT_d", [C, NI], BF16, kind="ExternalInput").ap()
    wgateT_d = nc.dram_tensor("wgateT_d", [C, NI], BF16, kind="ExternalInput").ap()
    gate_o = nc.dram_tensor("gate_o", [NI, TC], F32, kind="ExternalOutput").ap()
    up_o = nc.dram_tensor("up_o", [NI, TC], F32, kind="ExternalOutput").ap()
    mg_o = nc.dram_tensor("mg_o", [1, 1], F32, kind="ExternalOutput").ap()
    mu_o = nc.dram_tensor("mu_o", [1, 1], F32, kind="ExternalOutput").ap()
    scr2 = nc.dram_tensor("scr2", [2], F32, kind="Internal").ap()
    colscr = nc.dram_tensor("colscr", [128], F32, kind="Internal").ap()

    wg_sb = nc.alloc_sbuf_tensor("wg_sb", [128, CCH, NI], BF16)
    wu_sb = nc.alloc_sbuf_tensor("wu_sb", [128, CCH, NI], BF16)
    h2q_sb = nc.alloc_sbuf_tensor("h2q_sb", [128, CCH, TC], BF16)
    h2s_sb = nc.alloc_sbuf_tensor("h2s_sb", [128, 2, CCH, 512], F32)
    ev_sb = nc.alloc_sbuf_tensor("ev_sb", [128, 4, 512], F32)
    maxa_sb = nc.alloc_sbuf_tensor("maxa_sb", [1, 8], F32)
    srow_sb = nc.alloc_sbuf_tensor("srow_sb", [1, 4], F32)
    scol_sb = nc.alloc_sbuf_tensor("scol_sb", [128, 1], F32)
    icol_sb = nc.alloc_sbuf_tensor("icol_sb", [128, 1], F32)
    rung_sb = nc.alloc_sbuf_tensor("rung_sb", [128, 2], F32)
    runu_sb = nc.alloc_sbuf_tensor("runu_sb", [128, 2], F32)
    mrow_sb = nc.alloc_sbuf_tensor("mrow_sb", [1, 130], F32)

    ps = [nc.alloc_psum_tensor(f"ps{i}", [128, 512], F32) for i in range(4)]

    P = Prog(nc)
    ld = P.sem("ld")
    wld = P.sem("wld")
    wld2 = P.sem("wld2")
    hld = [P.sem(f"hld{i}") for i in range(2)]
    xsl = [P.sem(f"xsl{i}") for i in range(4)]
    vv = P.sem("vv")
    ss = P.sem("ss")
    qq = P.sem("qq")
    mm = P.sem("mm")
    evc = P.sem("evc")
    bb = P.sem("bb")
    xx = P.sem("xx")

    P.on("sync", lambda e: e.dma_start(out=maxa_sb[0:1, :], in_=mh2a), inc=ld, k=16)
    for hf in range(2):
        P.on("sync", lambda e, hf=hf: e.dma_start(
            out=wg_sb[:, :, hf * 1536:(hf + 1) * 1536],
            in_=fm(wgateT_d)[:, :, hf * 1536:(hf + 1) * 1536]), inc=wld, k=16)
        P.on("scalar", lambda e, hf=hf: e.dma_start(
            out=wu_sb[:, :, hf * 1536:(hf + 1) * 1536],
            in_=fm(wupT_d)[:, :, hf * 1536:(hf + 1) * 1536]), inc=wld2, k=16)
    BB_SC = emit_scale_prep(P, maxa_sb, srow_sb, scr2, scol_sb, icol_sb,
                            32767.0, [(ld, 16)], vv, bb)
    P.on("vector", lambda e: e.memset(rung_sb[:], 0.0), inc=vv)
    P.on("vector", lambda e: e.memset(runu_sb[:], 0.0), inc=vv)
    VV_MAXBASE = vv.n

    for tt in range(NTT):
        w = [(qq, 2 * (tt - 1))] if tt >= 2 else []
        P.on("sync", lambda e, tt=tt: e.dma_start(
            out=h2s_sb[:, tt % 2], in_=fm(h2_i)[:, :, tt * 512:(tt + 1) * 512]),
            wait=w, inc=hld[tt % 2], k=16)
        def fqdq(e, tt=tt):
            e.tensor_scalar(out=h2s_sb[:, tt % 2], in0=h2s_sb[:, tt % 2],
                            scalar1=icol_sb[:], scalar2=MAGIC,
                            op0=ALU.mult, op1=ALU.add)
            return e.tensor_scalar(out=h2q_sb[:, :, tt * 512:(tt + 1) * 512],
                                   in0=h2s_sb[:, tt % 2], scalar1=MAGIC,
                                   scalar2=scol_sb[:], op0=ALU.subtract, op1=ALU.mult)
        P.on("vector", fqdq, wait=[(hld[tt % 2], 16 * (tt // 2 + 1)), (bb, BB_SC)], inc=qq, k=2)
    QQ_ALL = qq.n

    NG = 0
    for wi, (wsb, out_d, run) in enumerate(((wg_sb, gate_o, rung_sb),
                                            (wu_sb, up_o, runu_sb))):
        for ic in range(ICH):
            for tt in range(NTT):
                g = NG
                NG += 1
                wm_ = [(qq, QQ_ALL), ((wld, 32) if wi == 0 else (wld2, 32))]
                if g >= 4:
                    wm_.append((evc, g - 3))
                def fmm(e, wsb=wsb, ic=ic, tt=tt, g=g):
                    bank = ps[g % 4]
                    ins = None
                    for hc in range(CCH):
                        ins = e.matmul(out=bank[:],
                                       lhsT=wsb[:, hc, ic * 128:(ic + 1) * 128],
                                       rhs=h2q_sb[:, hc, tt * 512:(tt + 1) * 512],
                                       start=(hc == 0), stop=(hc == CCH - 1))
                    return ins
                P.on("tensor", fmm, wait=wm_, inc=mm)
                eng = "scalar"
                sl = g % 4
                wev = [(mm, g + 1)]
                if g >= 4:
                    wev.append((xsl[g % 4], 16 * ((g - 4) // 4 + 1)))
                    wev.append((vv, VV_MAXBASE + g - 3))
                P.on(eng, lambda e, g=g, sl=sl: e.activation(
                    out=ev_sb[:, sl], in_=ps[g % 4][:], func=AF.Copy),
                    wait=wev, inc=evc)
                def fmax(e, g=g, sl=sl, run=run):
                    e.tensor_reduce(out=run[:, 1:2], in_=ev_sb[:, sl], axis=AX,
                                    op=ALU.max, apply_absolute_value=True)
                    return e.tensor_max(out=run[:, 0:1], in0=run[:, 0:1],
                                        in1=run[:, 1:2])
                P.on("vector", fmax, wait=[(evc, g + 1)], inc=vv)
                P.on("gpsimd", lambda e, out_d=out_d, ic=ic, tt=tt, g=g, sl=sl: e.dma_start(
                    out=out_d.rearrange("(a p) t -> p a t", p=128)[:, ic,
                                                                  tt * 512:(tt + 1) * 512],
                    in_=ev_sb[:, sl]), wait=[(evc, g + 1)], inc=xsl[g % 4], k=16)
    assert NG == 192

    emit_absmax_export(P, rung_sb[:, 0:1], mrow_sb, colscr, mg_o,
                       [(vv, vv.n)], bb, vv, xx)
    emit_absmax_export(P, runu_sb[:, 0:1], mrow_sb, colscr, mu_o,
                       [(vv, vv.n)], bb, vv, xx)
    P.on("sync", lambda e: None,
         wait=[(xx, xx.n)] + [(x, x.n) for x in xsl])
    P.emit()
    return nc



def build_l4():
    nc = bass.Bass("TRN2", target_bir_lowering=False)
    gate_i = nc.dram_tensor("gate_i", [NI, TC], F32, kind="ExternalInput").ap()
    mga = nc.dram_tensor("mga", [1, 8], F32, kind="ExternalInput").ap()
    g2_o = nc.dram_tensor("g2_o", [NI, TC], F32, kind="ExternalOutput").ap()
    mg2_o = nc.dram_tensor("mg2_o", [1, 1], F32, kind="ExternalOutput").ap()
    scr2 = nc.dram_tensor("scr2", [2], F32, kind="Internal").ap()
    colscr = nc.dram_tensor("colscr", [128], F32, kind="Internal").ap()

    gt_sb = nc.alloc_sbuf_tensor("gt_sb", [128, 4, TC], F32)
    gq_sb = nc.alloc_sbuf_tensor("gq_sb", [128, 4, TC], F32)
    sg_sb = nc.alloc_sbuf_tensor("sg_sb", [128, 4, TC], F32)
    g2_sb = nc.alloc_sbuf_tensor("g2_sb", [128, 4, TC], F32)
    maxa_sb = nc.alloc_sbuf_tensor("maxa_sb", [1, 8], F32)
    srow_sb = nc.alloc_sbuf_tensor("srow_sb", [1, 4], F32)
    scol_sb = nc.alloc_sbuf_tensor("scol_sb", [128, 1], F32)
    icol_sb = nc.alloc_sbuf_tensor("icol_sb", [128, 1], F32)
    run_sb = nc.alloc_sbuf_tensor("run_sb", [128, 2], F32)
    mrow_sb = nc.alloc_sbuf_tensor("mrow_sb", [1, 130], F32)

    P = Prog(nc)
    ld = P.sem("ld")
    gld = [P.sem(f"gld{i}") for i in range(4)]
    xsl = [P.sem(f"xsl{i}") for i in range(4)]
    vv = P.sem("vv")
    ss = P.sem("ss")
    pp = P.sem("pp")
    qq = P.sem("qq")
    bb = P.sem("bb")
    xx = P.sem("xx")

    P.on("sync", lambda e: e.dma_start(out=maxa_sb[0:1, :], in_=mga), inc=ld, k=16)
    BB_SC = emit_scale_prep(P, maxa_sb, srow_sb, scr2, scol_sb, icol_sb,
                            32767.0, [(ld, 16)], vv, bb)
    P.on("vector", lambda e: e.memset(run_sb[:], 0.0), inc=vv)

    NS = 4
    def front(g):
        sl = g % NS
        w = [(pp, g - NS + 1)] if g >= NS else []
        P.on("sync", lambda e, g=g, sl=sl: e.dma_start(
            out=gt_sb[:, sl, :],
            in_=gate_i.rearrange("(a p) t -> p a t", p=128)[:, g, :]),
            wait=w, inc=gld[sl], k=16)
        P.on("scalar", lambda e, g=g, sl=sl: e.activation(
            out=gq_sb[:, sl, :], in_=gt_sb[:, sl, :], func=AF.Copy, scale=icol_sb[:],
            bias=MAGIC),
            wait=[(gld[sl], 16 * (g // NS + 1)), (bb, BB_SC)]
                 + ([(pp, g - NS + 1)] if g >= NS else []), inc=ss)
        SSQ = ss.n
        P.on("vector", lambda e, sl=sl: e.tensor_scalar(
            out=gq_sb[:, sl, :], in0=gq_sb[:, sl, :], scalar1=MAGIC, scalar2=scol_sb[:],
            op0=ALU.subtract, op1=ALU.mult), wait=[(ss, SSQ)], inc=vv)
        VVQ = vv.n
        P.on("scalar", lambda e, sl=sl: e.activation(out=sg_sb[:, sl, :],
                                                     in_=gq_sb[:, sl, :],
                                                     func=AF.Sigmoid),
             wait=[(vv, VVQ)], inc=ss)
        SSS = ss.n
        P.on("gpsimd", lambda e, sl=sl: e.tensor_scalar(
            out=sg_sb[:, sl, :], in0=sg_sb[:, sl, :], scalar1=65536.0,
            scalar2=MAGIC, op0=ALU.mult, op1=ALU.add), wait=[(ss, SSS)], inc=qq)

    def back(g):
        sl = g % NS
        QQS = g + 1
        def fg2(e, g=g, sl=sl):
            e.tensor_scalar(out=sg_sb[:, sl, :], in0=sg_sb[:, sl, :], scalar1=MAGIC,
                            scalar2=65535.0, op0=ALU.subtract, op1=ALU.min)
            e.scalar_tensor_tensor(out=g2_sb[:, sl, :], in0=sg_sb[:, sl, :],
                                   scalar=1.0 / 65536.0, in1=gq_sb[:, sl, :],
                                   op0=ALU.mult, op1=ALU.mult)
            e.tensor_reduce(out=run_sb[:, 1:2], in_=g2_sb[:, sl, :], axis=AX,
                            op=ALU.max, apply_absolute_value=True)
            return e.tensor_max(out=run_sb[:, 0:1], in0=run_sb[:, 0:1],
                                in1=run_sb[:, 1:2])
        wg2 = [(qq, QQS)]
        if g >= NS:
            wg2.append((xsl[sl], 16 * ((g - NS) // NS + 1)))
        P.on("vector", fg2, wait=wg2, inc=pp)
        P.on("sync", lambda e, g=g, sl=sl: e.dma_start(
            out=g2_o.rearrange("(a p) t -> p a t", p=128)[:, g, :],
            in_=g2_sb[:, sl, :]), wait=[(pp, g + 1)], inc=xsl[sl], k=16)

    for g in range(ICH):
        front(g)
        if g >= 1:
            back(g - 1)
    back(ICH - 1)

    emit_absmax_export(P, run_sb[:, 0:1], mrow_sb, colscr, mg2_o,
                       [(pp, ICH)], bb, vv, xx)
    P.on("sync", lambda e: None,
         wait=[(xx, xx.n)] + [(x, x.n) for x in xsl])
    P.emit()
    return nc



def build_l5():
    nc = bass.Bass("TRN2", target_bir_lowering=False)
    g2_i = nc.dram_tensor("g2_i", [NI, TC], F32, kind="ExternalInput").ap()
    up_i = nc.dram_tensor("up_i", [NI, TC], F32, kind="ExternalInput").ap()
    mg2a = nc.dram_tensor("mg2a", [1, 8], F32, kind="ExternalInput").ap()
    mua = nc.dram_tensor("mua", [1, 8], F32, kind="ExternalInput").ap()
    op_o = nc.dram_tensor("op_o", [NI, TC], F32, kind="ExternalOutput").ap()
    mo_o = nc.dram_tensor("mo_o", [1, 1], F32, kind="ExternalOutput").ap()
    scr2 = nc.dram_tensor("scr2", [2], F32, kind="Internal").ap()
    scr2b = nc.dram_tensor("scr2b", [2], F32, kind="Internal").ap()
    colscr = nc.dram_tensor("colscr", [128], F32, kind="Internal").ap()

    g2t_sb = nc.alloc_sbuf_tensor("g2t_sb", [128, 3, TC], F32)
    upt_sb = nc.alloc_sbuf_tensor("upt_sb", [128, 3, TC], F32)
    q1_sb = nc.alloc_sbuf_tensor("q1_sb", [128, 3, TC], F32)
    q2_sb = nc.alloc_sbuf_tensor("q2_sb", [128, 3, TC], F32)
    op_sb = nc.alloc_sbuf_tensor("op_sb", [128, 3, TC], F32)
    maxa_sb = nc.alloc_sbuf_tensor("maxa_sb", [1, 2, 8], F32)
    srow_sb = nc.alloc_sbuf_tensor("srow_sb", [1, 4], F32)
    srow2_sb = nc.alloc_sbuf_tensor("srow2_sb", [1, 4], F32)
    scol_sb = nc.alloc_sbuf_tensor("scol_sb", [128, 1], F32)
    icol_sb = nc.alloc_sbuf_tensor("icol_sb", [128, 1], F32)
    scol2_sb = nc.alloc_sbuf_tensor("scol2_sb", [128, 1], F32)
    icol2_sb = nc.alloc_sbuf_tensor("icol2_sb", [128, 1], F32)
    run_sb = nc.alloc_sbuf_tensor("run_sb", [128, 2], F32)
    mrow_sb = nc.alloc_sbuf_tensor("mrow_sb", [1, 130], F32)

    P = Prog(nc)
    ld = P.sem("ld")
    gld = [P.sem(f"gld{i}") for i in range(3)]
    uld = [P.sem(f"uld{i}") for i in range(3)]
    xsl = [P.sem(f"xsl{i}") for i in range(3)]
    vv = P.sem("vv")
    ss = P.sem("ss")
    pp = P.sem("pp")
    bb = P.sem("bb")
    xx = P.sem("xx")

    P.on("sync", lambda e: e.dma_start(out=maxa_sb[:, 0, :], in_=mg2a), inc=ld, k=16)
    P.on("sync", lambda e: e.dma_start(out=maxa_sb[:, 1, :], in_=mua), inc=ld, k=16)
    B1 = emit_scale_prep(P, maxa_sb[:, 0, :], srow_sb, scr2, scol_sb, icol_sb,
                         32767.0, [(ld, 32)], vv, bb)
    B2 = emit_scale_prep(P, maxa_sb[:, 1, :], srow2_sb, scr2b, scol2_sb, icol2_sb,
                         32767.0, [(ld, 32)], vv, bb)
    P.on("vector", lambda e: e.memset(run_sb[:], 0.0), inc=vv)

    for g in range(ICH):
        sl = g % 3
        w = [(pp, g - 2)] if g >= 3 else []
        P.on("sync", lambda e, g=g, sl=sl: e.dma_start(
            out=g2t_sb[:, sl, :],
            in_=g2_i.rearrange("(a p) t -> p a t", p=128)[:, g, :]),
            wait=w, inc=gld[sl], k=16)
        P.on("scalar", lambda e, g=g, sl=sl: e.dma_start(
            out=upt_sb[:, sl, :],
            in_=up_i.rearrange("(a p) t -> p a t", p=128)[:, g, :]),
            wait=w, inc=uld[sl], k=16)
        P.on("scalar", lambda e, g=g, sl=sl: e.activation(
            out=q1_sb[:, sl, :], in_=g2t_sb[:, sl, :], func=AF.Copy, scale=icol_sb[:],
            bias=MAGIC),
            wait=[(gld[sl], 16 * (g // 3 + 1)), (bb, B2)]
                 + ([(pp, g - 2)] if g >= 3 else []), inc=ss)
        P.on("scalar", lambda e, g=g, sl=sl: e.activation(
            out=q2_sb[:, sl, :], in_=upt_sb[:, sl, :], func=AF.Copy, scale=icol2_sb[:],
            bias=MAGIC), wait=[(uld[sl], 16 * (g // 3 + 1))], inc=ss)
        S2 = ss.n
        def fq(e, sl=sl):
            e.tensor_scalar(out=q1_sb[:, sl, :], in0=q1_sb[:, sl, :], scalar1=MAGIC,
                            scalar2=scol_sb[:], op0=ALU.subtract, op1=ALU.mult)
            return e.tensor_scalar(out=q2_sb[:, sl, :], in0=q2_sb[:, sl, :],
                                   scalar1=MAGIC, scalar2=scol2_sb[:],
                                   op0=ALU.subtract, op1=ALU.mult)
        P.on("vector", fq, wait=[(ss, S2)], inc=vv)
        VQ = vv.n
        def fop(e, g=g, sl=sl):
            e.tensor_mul(out=op_sb[:, sl, :], in0=q1_sb[:, sl, :], in1=q2_sb[:, sl, :])
            e.tensor_reduce(out=run_sb[:, 1:2], in_=op_sb[:, sl, :], axis=AX,
                            op=ALU.max, apply_absolute_value=True)
            return e.tensor_max(out=run_sb[:, 0:1], in0=run_sb[:, 0:1],
                                in1=run_sb[:, 1:2])
        wo = [(vv, VQ)]
        if g >= 3:
            wo.append((xsl[sl], 16 * ((g - 3) // 3 + 1)))
        P.on("vector", fop, wait=wo, inc=pp)
        P.on("gpsimd", lambda e, g=g, sl=sl: e.dma_start(
            out=op_o.rearrange("(a p) t -> p a t", p=128)[:, g, :],
            in_=op_sb[:, sl, :]), wait=[(pp, g + 1)], inc=xsl[sl], k=16)

    emit_absmax_export(P, run_sb[:, 0:1], mrow_sb, colscr, mo_o,
                       [(pp, ICH)], bb, vv, xx)
    P.on("sync", lambda e: None,
         wait=[(xx, xx.n)] + [(x, x.n) for x in xsl])
    P.emit()
    return nc



def build_l6():
    nc = bass.Bass("TRN2", target_bir_lowering=False)
    op_i = nc.dram_tensor("op_i", [NI, TC], F32, kind="ExternalInput").ap()
    moa = nc.dram_tensor("moa", [1, 8], F32, kind="ExternalInput").ap()
    x2_i = nc.dram_tensor("x2_i", [C, TC], F32, kind="ExternalInput").ap()
    wdT_d = nc.dram_tensor("wdT_d", [NI, C], BF16, kind="ExternalInput").ap()
    mod_i = nc.dram_tensor("mod_i", [128, 48], F32, kind="ExternalInput").ap()
    out_o = nc.dram_tensor("out_o", [C, TC], F32, kind="ExternalOutput").ap()
    scr2 = nc.dram_tensor("scr2", [2], F32, kind="Internal").ap()

    wd_sb = nc.alloc_sbuf_tensor("wd_sb", [128, ICH, C], BF16)
    ot_sb = nc.alloc_sbuf_tensor("ot_sb", [128, 4, 512], F32)
    ob_sb = nc.alloc_sbuf_tensor("ob_sb", [128, 4, 512], BF16)
    oq_sb = nc.alloc_sbuf_tensor("oq_sb", [128, 512], F32)
    x2t_sb = nc.alloc_sbuf_tensor("x2t_sb", [128, 2, CCH, 512], F32)
    os_sb = nc.alloc_sbuf_tensor("os_sb", [128, 2, CCH, 512], F32)
    mod_sb = nc.alloc_sbuf_tensor("mod_sb", [128, 48], F32)
    gb_sb = nc.alloc_sbuf_tensor("gb_sb", [128, 2, CCH], F32)
    maxa_sb = nc.alloc_sbuf_tensor("maxa_sb", [1, 8], F32)
    srow_sb = nc.alloc_sbuf_tensor("srow_sb", [1, 4], F32)
    scol_sb = nc.alloc_sbuf_tensor("scol_sb", [128, 1], F32)
    icol_sb = nc.alloc_sbuf_tensor("icol_sb", [128, 1], F32)

    ps = [nc.alloc_psum_tensor(f"ps{i}", [128, 512], F32) for i in range(CCH)]

    P = Prog(nc)
    ld = P.sem("ld")
    wld = P.sem("wld")
    old_ = [P.sem(f"old{i}") for i in range(4)]
    xld = [P.sem(f"xld{i}") for i in range(2)]
    xos = [P.sem(f"xos{i}") for i in range(2)]
    vv = P.sem("vv")
    ss = P.sem("ss")
    qrd = P.sem("qrd")
    mm = P.sem("mm")
    evc = P.sem("evc")
    bb = P.sem("bb")
    xx = P.sem("xx")

    P.on("sync", lambda e: e.dma_start(out=maxa_sb[0:1, :], in_=moa), inc=ld, k=16)
    P.on("sync", lambda e: e.dma_start(out=mod_sb[:], in_=mod_i), inc=ld, k=16)
    P.on("sync", lambda e: e.dma_start(
        out=wd_sb[:], in_=wdT_d.rearrange("(a p) o -> p a o", p=128)), inc=wld, k=16)
    BB_SC = emit_scale_prep(P, maxa_sb, srow_sb, scr2, scol_sb, icol_sb,
                            32767.0, [(ld, 16)], vv, bb)
    P.on("scalar", lambda e: e.activation(out=gb_sb[:, 0], in_=mod_sb[:, 40:48],
                                          func=AF.Abs), wait=[(ld, 32)], inc=ss)
    P.on("scalar", lambda e: e.sign(out=gb_sb[:, 1], in_=mod_sb[:, 40:48]),
         wait=[], inc=ss)
    SS_GB = ss.n
    def fgb(e):
        e.tensor_scalar(out=gb_sb[:, 0], in0=gb_sb[:, 0], scalar1=0.25, scalar2=None,
                        op0=ALU.max)
        return e.tensor_mul(out=gb_sb[:, 0], in0=gb_sb[:, 1], in1=gb_sb[:, 0])
    P.on("vector", fgb, wait=[(ss, SS_GB)], inc=vv)
    VV_GB = vv.n

    for tt in range(NTT):
        w = [(evc, (tt - 1) * CCH)] if tt >= 2 else []
        P.on("sync", lambda e, tt=tt: e.dma_start(
            out=x2t_sb[:, tt % 2], in_=fm(x2_i)[:, :, tt * 512:(tt + 1) * 512]),
            wait=w, inc=xld[tt % 2], k=16)
        for ic in range(ICH):
            g = tt * ICH + ic
            w = [(qrd, g - 3)] if g >= 4 else []
            P.on("scalar", lambda e, ic=ic, tt=tt, g=g: e.dma_start(
                out=ot_sb[:, g % 4, :],
                in_=op_i.rearrange("(a p) t -> p a t", p=128)[:, ic,
                                                              tt * 512:(tt + 1) * 512]),
                wait=w, inc=old_[g % 4], k=16)
            def fq(e, g=g):
                e.tensor_scalar(out=oq_sb[:], in0=ot_sb[:, g % 4, :],
                                scalar1=icol_sb[:], scalar2=MAGIC,
                                op0=ALU.mult, op1=ALU.add)
                return e.tensor_scalar(out=ob_sb[:, g % 4, :], in0=oq_sb[:],
                                       scalar1=MAGIC, scalar2=scol_sb[:],
                                       op0=ALU.subtract, op1=ALU.mult)
            P.on("vector", fq,
                 wait=[(old_[g % 4], 16 * (g // 4 + 1)), (bb, BB_SC)]
                      + ([(mm, (g - 3) * CCH)] if g >= 4 else []), inc=qrd)
            def fmm(e, ic=ic, tt=tt, g=g):
                ins = None
                for oc2 in range(CCH):
                    ins = e.matmul(out=ps[oc2][:],
                                   lhsT=wd_sb[:, ic, oc2 * 128:(oc2 + 1) * 128],
                                   rhs=ob_sb[:, g % 4, :],
                                   start=(ic == 0), stop=(ic == ICH - 1))
                return ins
            wm_ = [(qrd, g + 1), (wld, 16)]
            if ic == 0 and tt > 0:
                wm_.append((evc, tt * CCH))
            P.on("tensor", fmm, wait=wm_, inc=mm, k=CCH)
        for oc2 in range(CCH):
            P.on("vector", lambda e, oc2=oc2, tt=tt: e.scalar_tensor_tensor(
                out=os_sb[:, tt % 2, oc2, :], in0=ps[oc2][:],
                scalar=gb_sb[:, 0, oc2:oc2 + 1], in1=x2t_sb[:, tt % 2, oc2, :],
                op0=ALU.mult, op1=ALU.add),
                wait=[(mm, (tt + 1) * ICH * CCH), (xld[tt % 2], 16 * (tt // 2 + 1)), (vv, VV_GB)]
                     + ([(xos[tt % 2], 16 * (tt // 2))] if tt >= 2 else []),
                inc=evc)
        P.on("gpsimd", lambda e, tt=tt: e.dma_start(
            out=fm(out_o)[:, :, tt * 512:(tt + 1) * 512], in_=os_sb[:, tt % 2]),
            wait=[(evc, (tt + 1) * CCH)], inc=xos[tt % 2], k=16)

    P.on("sync", lambda e: None,
         wait=[(xos[0], xos[0].n), (xos[1], xos[1].n)])
    P.emit()
    return nc



def _t(a):
    return np.ascontiguousarray(np.asarray(a).T)


def _cols(v):
    return np.ascontiguousarray(np.asarray(v).reshape(CCH, 128).T)


def la_maps(inp):
    x = np.asarray(inp["L0_attn_x"])[0]
    xT = _t(x)
    eye = np.eye(64, dtype=np.float32)
    maps = []
    for r in range(NCORES):
        lo, hi = r * TC, (r + 1) * TC
        xs = np.zeros((C, WH), np.float32)
        xs[:, 1:TC + 1] = xT[:, lo:hi]
        if r > 0:
            xs[:, 0] = xT[:, lo - 1]
        if r < NCORES - 1:
            xs[:, TC + 1] = xT[:, hi]
        maskLR = np.zeros((128, 2), np.float32)
        maskLR[:, 0] = 1.0 if r > 0 else 0.0
        maskLR[:, 1] = 1.0 if r < NCORES - 1 else 0.0
        maps.append({
            "xT": xs,
            "mixk": _cols(inp["L0_attn_mix_k"]),
            "mixv": _cols(inp["L0_attn_mix_v"]),
            "mixr": _cols(inp["L0_attn_mix_r"]),
            "cvec": _cols(np.asarray(inp["L0_c"])[0]),
            "wmod": np.asarray(inp["L0_wmod"]),
            "maskLR": maskLR,
            "eye": eye,
            "wup_sh": np.asarray(inp["L0_ffn_w_up"])[r * 384:(r + 1) * 384],
            "wgate_sh": np.asarray(inp["L0_ffn_w_gate"])[r * 384:(r + 1) * 384],
            "wdown_sh": np.asarray(inp["L0_ffn_w_down"])[r * 128:(r + 1) * 128],
        })
    return maps


def lb_maps(inp, ra):
    wk, wv, wr = _t(inp["L0_attn_Wk"]), _t(inp["L0_attn_Wv"]), _t(inp["L0_attn_Wr"])
    sd = _cols(inp["L0_attn_spatial_decay"])
    return [{
        "h_i": ra[r]["h_o"], "u_i": ra[r]["u_o"],
        "wkT": wk, "wvT": wv, "wrT": wr, "sdec": sd,
    } for r in range(NCORES)]


def l2_maps(inp, ra, rb):
    af_all = np.stack([rb[j]["afin_o"] for j in range(NCORES)], axis=2)
    bf_all = np.stack([rb[j]["bfin_o"] for j in range(NCORES)], axis=2)
    af_all = np.ascontiguousarray(af_all, np.float32)
    bf_all = np.ascontiguousarray(bf_all, np.float32)
    x = np.asarray(inp["L0_attn_x"])[0]
    xT = _t(x)
    wo = _t(inp["L0_attn_Wo"])
    sd = _cols(inp["L0_attn_spatial_decay"])
    sf = _cols(inp["L0_attn_spatial_first"])
    maps = []
    for r in range(NCORES):
        lo, hi = r * TC, (r + 1) * TC
        xs = np.zeros((C, WH), np.float32)
        xs[:, 1:TC + 1] = xT[:, lo:hi]
        pwv = np.zeros((128, 8), np.float32)
        pmk = np.zeros((128, 8), np.float32)
        for j in range(NCORES):
            if j < r:
                pwv[:, j] = (r - 1 - j) * (float(TC) / T)
                pmk[:, j] = 1.0
        maps.append({
            "xT": xs,
            "ek_i": rb[r]["ek_o"], "ekv_i": rb[r]["ekv_o"], "sr_i": rb[r]["sr_o"],
            "af_all": af_all, "bf_all": bf_all,
            "pw": pwv, "pmask": pmk,
            "woT": wo, "mod_i": ra[r]["mod_o"],
            "sdec": sd, "sfirst": sf,
        })
    return maps




def l3_maps(inp, ra, r2):
    wup_d = np.concatenate([ra[j]["wupd_o"] for j in range(NCORES)], axis=0)
    wgate_d = np.concatenate([ra[j]["wgated_o"] for j in range(NCORES)], axis=0)
    wupT = np.ascontiguousarray(wup_d.T)
    wgateT = np.ascontiguousarray(wgate_d.T)
    mh2a = np.ascontiguousarray(
        np.concatenate([r2[j]["mh2_o"] for j in range(NCORES)], axis=1))
    return [{
        "h2_i": r2[r]["h2_o"], "mh2a": mh2a,
        "wupT_d": wupT, "wgateT_d": wgateT,
    } for r in range(NCORES)]


def l4_maps(r3):
    mga = np.ascontiguousarray(
        np.concatenate([r3[j]["mg_o"] for j in range(NCORES)], axis=1))
    return [{"gate_i": r3[r]["gate_o"], "mga": mga} for r in range(NCORES)]


def l5_maps(r3, r4):
    mg2a = np.ascontiguousarray(
        np.concatenate([r4[j]["mg2_o"] for j in range(NCORES)], axis=1))
    mua = np.ascontiguousarray(
        np.concatenate([r3[j]["mu_o"] for j in range(NCORES)], axis=1))
    return [{"g2_i": r4[r]["g2_o"], "up_i": r3[r]["up_o"],
             "mg2a": mg2a, "mua": mua} for r in range(NCORES)]


def l6_maps(inp, ra, r2, r5):
    wdown_d = np.concatenate([ra[j]["wdownd_o"] for j in range(NCORES)], axis=0)
    wdT = np.ascontiguousarray(wdown_d.T)
    moa = np.ascontiguousarray(
        np.concatenate([r5[j]["mo_o"] for j in range(NCORES)], axis=1))
    return [{"op_i": r5[r]["op_o"], "moa": moa, "x2_i": r2[r]["x2_o"],
             "wdT_d": wdT, "mod_i": ra[r]["mod_o"]} for r in range(NCORES)]


def run_all(inp):
    cores = list(range(NCORES))
    ra = run_bass_kernel_spmd(build_la(), la_maps(inp), core_ids=cores).results
    rb = run_bass_kernel_spmd(build_lb(), lb_maps(inp, ra), core_ids=cores).results
    r2 = run_bass_kernel_spmd(build_l2(), l2_maps(inp, ra, rb), core_ids=cores).results
    r3 = run_bass_kernel_spmd(build_l3(), l3_maps(inp, ra, r2), core_ids=cores).results
    r4 = run_bass_kernel_spmd(build_l4(), l4_maps(r3), core_ids=cores).results
    r5 = run_bass_kernel_spmd(build_l5(), l5_maps(r3, r4), core_ids=cores).results
    r6 = run_bass_kernel_spmd(build_l6(), l6_maps(inp, ra, r2, r5), core_ids=cores).results
    return ra, rb, r2, r3, r4, r5, r6


def kernel(**inp):
    ra, rb, r2, r3, r4, r5, r6 = run_all(inp)
    out = np.concatenate(
        [np.asarray(r6[r]["out_o"]).T for r in range(NCORES)], axis=0)
    return np.ascontiguousarray(out).reshape(1, T, C)


# version 03
# speedup vs baseline: 1.4322x; 1.0767x over previous
import numpy as np
import ml_dtypes
import concourse.bass as bass
import concourse.mybir as mybir
from concourse.bass_utils import run_bass_kernel_spmd

F32 = mybir.dt.float32
BF16 = mybir.dt.bfloat16
AF = mybir.ActivationFunctionType
ALU = mybir.AluOpType
AX = mybir.AxisListType.X

NCORES = 8
T = 16384
TC = T // NCORES
C = 1024
CCH = C // 128
NI = 3072
ICH = NI // 128
NTT = TC // 512
WH = TC + 2
MAGIC = float(np.float32(1.5 * 2 ** 23))
EPS = 1e-5
BF = ml_dtypes.bfloat16
ENGS = ("sync", "tensor", "vector", "scalar", "gpsimd")


class Sem:
    def __init__(self, nc, name):
        self.h = nc.alloc_semaphore(name)
        self.n = 0
        self.name = name


_CHAIN_SKIP = {"dma_start", "wait_ge", "wait_op", "sem_inc", "then_inc",
               "dma_start_transpose", "indirect_dma_start", "engine_nop",
               "value_load", "reg_mov", "nop"}


class Chained:

    def __init__(self, eng, sem):
        self._e = eng
        self._s = sem

    def __getattr__(self, k):
        a = getattr(self._e, k)
        if not callable(a) or k in _CHAIN_SKIP or k.startswith("_"):
            return a

        def wrap(*args, **kw):
            r = a(*args, **kw)
            if hasattr(r, "then_inc"):
                self._s.n += 1
                r.then_inc(self._s.h, 1)
                self._e.wait_ge(self._s.h, self._s.n)
            return r
        return wrap


class Prog:

    def __init__(self, nc):
        self.nc = nc
        self.steps = {e: [] for e in ENGS}
        self.chains = {e: Sem(nc, f"chain_{e}") for e in ("vector", "scalar", "gpsimd")}

    def sem(self, name):
        return Sem(self.nc, name)

    def on(self, eng, fn, wait=(), inc=None, k=1):
        for s, v in wait:
            assert v <= s.n, f"{eng}: wait {s.name}>={v} but only {s.n} reserved"
        if inc is not None:
            inc.n += k
        self.steps[eng].append((fn, tuple(wait), inc, k))

    def emit(self):
        def mk(ename):
            chain = self.chains.get(ename)

            def body(eng):
                ceng = Chained(eng, chain) if chain is not None else eng
                for fn, waits, inc, k in self.steps[ename]:
                    for s, v in waits:
                        eng.wait_ge(s.h, v)
                    ins = fn(ceng)
                    if inc is not None:
                        if chain is not None and k != 16:
                            eng.sem_inc(inc.h, k)
                        else:
                            assert ins is not None, ename
                            ins.then_inc(inc.h, k)
            return body

        with self.nc.Block() as block:
            block.sync(mk("sync"))
            block.tensor(mk("tensor"))
            block.vector(mk("vector"))
            block.scalar(mk("scalar"))
            block.gpsimd(mk("gpsimd"))


class Arena:

    def __init__(self, nc, name, nbytes):
        self.t = nc.alloc_sbuf_tensor(name, [128, nbytes], mybir.dt.uint8)
        self.off = 0
        self.nbytes = nbytes

    def carve(self, shape, dtype):
        esz = mybir.dt.size(dtype)
        n = int(np.prod(shape)) * esz
        off = (self.off + 511) // 512 * 512
        assert off + n <= self.nbytes, f"arena overflow {off}+{n} > {self.nbytes}"
        self.off = off + n
        ap = self.t.ap()[:, off:off + n].bitcast(dtype)
        if len(shape) > 1:
            dims = " ".join(f"d{i}" for i in range(len(shape)))
            kw = {f"d{i}": s for i, s in enumerate(shape)}
            ap = ap.rearrange(f"p ({dims}) -> p {dims}", **kw)
        return ap

    def reset(self, off=0):
        self.off = off


def exp_poly(eng, out_ap, x_ap, tmp_ap):
    eng.tensor_scalar(out=tmp_ap, in0=x_ap, scalar1=1.0 / 6.0, scalar2=0.5,
                      op0=ALU.mult, op1=ALU.add)
    eng.tensor_tensor(out=tmp_ap, in0=tmp_ap, in1=x_ap, op=ALU.mult)
    eng.tensor_scalar(out=tmp_ap, in0=tmp_ap, scalar1=1.0, scalar2=None, op0=ALU.add)
    eng.tensor_tensor(out=tmp_ap, in0=tmp_ap, in1=x_ap, op=ALU.mult)
    return eng.tensor_scalar(out=out_ap, in0=tmp_ap, scalar1=1.0, scalar2=None,
                             op0=ALU.add)


def emit_absmax_export(P, col_ap, row_sb, scr_dram, out_dram, wait, bb, vv, xx):
    P.on("gpsimd", lambda e: e.dma_start(out=scr_dram, in_=col_ap),
         wait=wait, inc=bb, k=16)
    B1 = bb.n
    P.on("gpsimd", lambda e: e.dma_start(out=row_sb[0:1, 0:128],
                                         in_=scr_dram.unsqueeze(0)),
         wait=[(bb, B1)], inc=bb, k=16)
    B2 = bb.n
    P.on("vector", lambda e: e.reduce_max(out=row_sb[0:1, 128:129],
                                          in_=row_sb[0:1, 0:128], axis=AX),
         wait=[(bb, B2)], inc=vv)
    V1 = vv.n
    P.on("gpsimd", lambda e: e.dma_start(out=out_dram, in_=row_sb[0:1, 128:129]),
         wait=[(vv, V1)], inc=xx, k=16)


def cvec_ap(dram_ap):
    return dram_ap.rearrange("(a p) -> p a", p=128)


def fm(dram_ap):
    return dram_ap.rearrange("(a p) t -> p a t", p=128)



def build_la():
    nc = bass.Bass("TRN2", target_bir_lowering=False)
    xT = nc.dram_tensor("xT", [C, WH], F32, kind="ExternalInput").ap()
    mixk = nc.dram_tensor("mixk", [128, CCH], F32, kind="ExternalInput").ap()
    mixv = nc.dram_tensor("mixv", [128, CCH], F32, kind="ExternalInput").ap()
    mixr = nc.dram_tensor("mixr", [128, CCH], F32, kind="ExternalInput").ap()
    cvec = nc.dram_tensor("cvec", [128, CCH], F32, kind="ExternalInput").ap()
    wmod = nc.dram_tensor("wmod", [C, 6 * C], F32, kind="ExternalInput").ap()
    maskLR = nc.dram_tensor("maskLR", [128, 2], F32, kind="ExternalInput").ap()
    eye = nc.dram_tensor("eye", [64, 64], F32, kind="ExternalInput").ap()
    wup_sh = nc.dram_tensor("wup_sh", [NI // 8, C], F32, kind="ExternalInput").ap()
    wgate_sh = nc.dram_tensor("wgate_sh", [NI // 8, C], F32, kind="ExternalInput").ap()
    wdown_sh = nc.dram_tensor("wdown_sh", [C // 8, NI], F32, kind="ExternalInput").ap()

    h_o = nc.dram_tensor("h_o", [C, WH], BF16, kind="ExternalOutput").ap()
    u_o = nc.dram_tensor("u_o", [3, C // 2, TC], BF16, kind="ExternalOutput").ap()
    mod_o = nc.dram_tensor("mod_o", [128, 48], F32, kind="ExternalOutput").ap()
    wupd_o = nc.dram_tensor("wupd_o", [NI // 8, C], BF16, kind="ExternalOutput").ap()
    wgated_o = nc.dram_tensor("wgated_o", [NI // 8, C], BF16, kind="ExternalOutput").ap()
    wdownd_o = nc.dram_tensor("wdownd_o", [C // 8, NI], BF16, kind="ExternalOutput").ap()

    modscr = nc.dram_tensor("modscr", [48, 128], F32, kind="Internal").ap()
    rowscr = nc.dram_tensor("rowscr", [2, WH], F32, kind="Internal").ap()

    h_sb = nc.alloc_sbuf_tensor("h_sb", [128, CCH, WH], BF16)
    mask_sb = nc.alloc_sbuf_tensor("mask_sb", [128, 2], BF16)
    maskf_sb = nc.alloc_sbuf_tensor("maskf_sb", [128, 2], F32)
    mix_sb = nc.alloc_sbuf_tensor("mix_sb", [128, 3, CCH], F32)
    scf_sb = nc.alloc_sbuf_tensor("scf_sb", [128, CCH], F32)
    scf2_sb = nc.alloc_sbuf_tensor("scf2_sb", [128, CCH, 1], F32)
    tmp8_sb = nc.alloc_sbuf_tensor("tmp8_sb", [128, CCH], F32)
    sc_sb = nc.alloc_sbuf_tensor("sc_sb", [128, CCH, 1], BF16)
    m48_sb = nc.alloc_sbuf_tensor("m48_sb", [48, 128], F32)
    mod_sb = nc.alloc_sbuf_tensor("mod_sb", [128, 48], F32)
    A_sb = nc.alloc_sbuf_tensor("A_sb", [128, CCH], F32)
    eye_sb = nc.alloc_sbuf_tensor("eye_sb", [64, 64], F32)
    evrow_sb = nc.alloc_sbuf_tensor("evrow_sb", [1, 512], F32)
    xb_sb = nc.alloc_sbuf_tensor("xb_sb", [128, CCH, 512], BF16)
    x2b_sb = nc.alloc_sbuf_tensor("x2b_sb", [128, CCH, 512], BF16)
    stat_sb = nc.alloc_sbuf_tensor("stat_sb", [1, 2, WH], F32)
    brow_sb = nc.alloc_sbuf_tensor("brow_sb", [128, 2, WH], F32)
    ht1_sb = nc.alloc_sbuf_tensor("ht1_sb", [128, WH], F32)
    ht2_sb = nc.alloc_sbuf_tensor("ht2_sb", [128, WH], F32)
    wm_sb = nc.alloc_sbuf_tensor("wm_sb", [128, 2, 2048], F32)
    wmb_sb = nc.alloc_sbuf_tensor("wmb_sb", [128, 2, 2048], BF16)
    dqw_sb = nc.alloc_sbuf_tensor("dqw_sb", [128, 2, 1024], F32)
    dqo_sb = nc.alloc_sbuf_tensor("dqo_sb", [128, 2, 1024], BF16)
    dqi_sb = nc.alloc_sbuf_tensor("dqi_sb", [128, 32], F32)
    dqs_sb = nc.alloc_sbuf_tensor("dqs_sb", [128, 32], F32)

    Z1 = Arena(nc, "zone1", 68 * 1024)
    x_sb = Z1.carve([CCH, WH], F32)
    Z1.reset()
    u_sb = Z1.carve([3, 4, TC], BF16)
    dscr_sb = Z1.carve([TC], BF16)

    ps_mod = [nc.alloc_psum_tensor(f"ps_mod{i}", [1, 512], F32) for i in range(2)]
    ps_modB = [nc.alloc_psum_tensor(f"ps_modB{i}", [1, 256], F32) for i in range(2)]
    ps_sum = nc.alloc_psum_tensor("ps_sum", [1, 512], F32)
    ps_ssq = nc.alloc_psum_tensor("ps_ssq", [1, 512], F32)
    ps_tr = nc.alloc_psum_tensor("ps_tr", [64, 48], F32)

    P = Prog(nc)
    ld = P.sem("ld")
    wld = [P.sem(f"wld{i}") for i in range(2)]
    cst = P.sem("cst")
    mmod = P.sem("mmod")
    vmd = P.sem("vmd")
    bb = P.sem("bb")
    ss = P.sem("ss")
    vv = P.sem("vv")
    rr = P.sem("rr")
    hh = P.sem("hh")
    mst = P.sem("mst")
    sst = P.sem("sst")
    hrd = P.sem("hrd")
    urd = P.sem("urd")
    xx = P.sem("xx")
    dql = [P.sem(f"dql{i}") for i in range(2)]
    xdq = P.sem("xdq")
    dqp = P.sem("dqp")
    dqv = P.sem("dqv")

    P.on("sync", lambda e: e.dma_start(out=x_sb, in_=fm(xT)), inc=ld, k=16)
    P.on("sync", lambda e: e.dma_start(out=maskf_sb[:], in_=maskLR), inc=ld, k=16)
    for i, t in enumerate((mixk, mixv, mixr)):
        P.on("sync", lambda e, i=i, t=t: e.dma_start(out=mix_sb[:, i, :], in_=t),
             inc=ld, k=16)
    P.on("sync", lambda e: e.dma_start(out=scf_sb[:], in_=cvec), inc=ld, k=16)
    P.on("sync", lambda e: e.dma_start(out=eye_sb[:], in_=eye), inc=ld, k=16)
    LD_X = LD_MASK = LD_MIX = LD_C = LD_EYE = 112

    def fsilu(e):
        coef = [1.0 / 3628800, -1.0 / 362880, 1.0 / 40320, -1.0 / 5040,
                1.0 / 720, -1.0 / 120, 1.0 / 24, -1.0 / 6, 0.5, -1.0, 1.0]
        e.tensor_scalar(out=tmp8_sb[:], in0=scf_sb[:], scalar1=coef[0],
                        scalar2=coef[1], op0=ALU.mult, op1=ALU.add)
        for k in range(2, len(coef)):
            e.tensor_tensor(out=tmp8_sb[:], in0=tmp8_sb[:], in1=scf_sb[:], op=ALU.mult)
            e.tensor_scalar(out=tmp8_sb[:], in0=tmp8_sb[:], scalar1=coef[k],
                            scalar2=None, op0=ALU.add)
        e.tensor_scalar(out=tmp8_sb[:], in0=tmp8_sb[:], scalar1=1.0, scalar2=None,
                        op0=ALU.add)
        e.reciprocal(out=tmp8_sb[:], in_=tmp8_sb[:])
        e.tensor_mul(out=scf2_sb[:, :, 0], in0=scf_sb[:], in1=tmp8_sb[:])
        return e.tensor_copy(out=sc_sb[:, :, 0], in_=scf2_sb[:, :, 0])
    P.on("vector", fsilu, wait=[(ld, LD_C)], inc=ss)
    SS_SILU = ss.n

    for it in range(96):
        j, hc = divmod(it, CCH)
        I, q = divmod(it, 4)
        if q == 0:
            w = [(mmod, 4 * (I - 1))] if I >= 2 else []
            P.on("sync", lambda e, j=j, hc=hc, I=I: e.dma_start(
                out=wm_sb.ap()[:, I % 2].rearrange("p (a c) -> p a c", c=512),
                in_=wmod.rearrange("(a p) j -> p a j", p=128)[:, hc:hc + 4,
                                                             j * 512:(j + 1) * 512]),
                wait=w, inc=wld[I % 2], k=16)
            wc = [(wld[I % 2], 16 * (I // 2 + 1))]
            if I >= 2:
                wc.append((mmod, 4 * (I - 1)))
            P.on("scalar", lambda e, I=I: e.activation(
                out=wmb_sb[:, I % 2, :], in_=wm_sb[:, I % 2, :], func=AF.Copy),
                wait=wc, inc=cst)
        wm = [(cst, I + 1), (ss, SS_SILU)]
        if hc == 0 and j >= 2:
            wm.append((vmd, j - 1))
        if j in (4, 5, 10, 11):
            def fmodf(e, j=j, hc=hc, I=I, q=q):
                e.matmul(out=ps_mod[j % 2][0:1, 0:256], lhsT=scf2_sb[:, hc, :],
                         rhs=wm_sb[:, I % 2, q * 512:q * 512 + 256],
                         start=(hc == 0), stop=(hc == CCH - 1))
                return e.matmul(out=ps_modB[j % 2][0:1, :], lhsT=scf2_sb[:, hc, :],
                                rhs=wm_sb[:, I % 2, q * 512 + 256:q * 512 + 512],
                                start=(hc == 0), stop=(hc == CCH - 1))
            P.on("tensor", fmodf, wait=wm, inc=mmod)
        else:
            P.on("tensor", lambda e, j=j, hc=hc, I=I, q=q: e.matmul(
                out=ps_mod[j % 2][:], lhsT=sc_sb[:, hc, :],
                rhs=wmb_sb[:, I % 2, q * 512:(q + 1) * 512],
                start=(hc == 0), stop=(hc == CCH - 1)), wait=wm, inc=mmod)
        if hc == CCH - 1:
            wv = [(mmod, 8 * (j + 1))] + ([(bb, 16 * j)] if j > 0 else [])
            if j in (4, 5, 10, 11):
                def fev(e, j=j):
                    e.tensor_copy(out=evrow_sb[0:1, 0:256], in_=ps_mod[j % 2][0:1, 0:256])
                    return e.tensor_copy(out=evrow_sb[0:1, 256:512],
                                         in_=ps_modB[j % 2][0:1, :])
                P.on("vector", fev, wait=wv, inc=vmd)
            else:
                P.on("vector", lambda e, j=j: e.tensor_copy(out=evrow_sb[0:1, :],
                                                            in_=ps_mod[j % 2][:]),
                     wait=wv, inc=vmd)
            P.on("gpsimd", lambda e, j=j: e.dma_start(
                out=modscr.rearrange("a b -> (a b)").unsqueeze(0)[:, j * 512:(j + 1) * 512],
                in_=evrow_sb[0:1, :]), wait=[(vmd, j + 1)], inc=bb, k=16)
    BB_MOD = bb.n
    P.on("gpsimd", lambda e: e.dma_start(out=m48_sb[:], in_=modscr),
         wait=[(bb, BB_MOD)], inc=bb, k=16)
    BB_M48 = bb.n
    P.on("tensor", lambda e: e.transpose(out=ps_tr[:], in_=m48_sb[:, 0:64],
                                         identity=eye_sb[0:48, 0:48]),
         wait=[(bb, BB_M48), (ld, LD_EYE)], inc=mmod)
    P.on("vector", lambda e: e.tensor_copy(out=mod_sb[0:64, :], in_=ps_tr[0:64, :]),
         wait=[(mmod, 97)], inc=vmd)
    P.on("tensor", lambda e: e.transpose(out=ps_tr[:], in_=m48_sb[:, 64:128],
                                         identity=eye_sb[0:48, 0:48]),
         wait=[(vmd, 13)], inc=mmod)
    P.on("vector", lambda e: e.tensor_copy(out=mod_sb[64:128, :], in_=ps_tr[0:64, :]),
         wait=[(mmod, 98)], inc=vmd)
    P.on("vector", lambda e: e.tensor_scalar(out=A_sb[:], in0=mod_sb[:, 8:16],
                                             scalar1=1.0, scalar2=None, op0=ALU.add),
         wait=[(vmd, 14)], inc=vmd)
    VMD_DONE = vmd.n
    P.on("vector", lambda e: e.tensor_copy(out=mask_sb[:], in_=maskf_sb[:]),
         wait=[(ld, LD_MASK)], inc=vv)
    VV_MASK = vv.n

    onesb_sb = nc.alloc_sbuf_tensor("onesb_sb", [128, 1], BF16)
    eps_sb = nc.alloc_sbuf_tensor("eps_sb", [128, 1], F32)
    P.on("vector", lambda e: e.memset(onesb_sb[:], 1.0), inc=vv)
    P.on("vector", lambda e: e.memset(eps_sb[:], EPS), inc=vv)
    VV_ONES = vv.n
    tiles = [(st * 512, 512) for st in range(4)] + [(2048, 2)]
    VV_CAST = {}
    SS_SQ = {}
    for st, (t0, tw) in enumerate(tiles):
        wv = [(ld, LD_X)]
        if st > 0:
            wv += [(ss, SS_SQ[st - 1]), (mst, st)]
        P.on("vector", lambda e, t0=t0, tw=tw: e.tensor_copy(
            out=xb_sb[:, :, 0:tw], in_=x_sb[:, :, t0:t0 + tw]), wait=wv, inc=vv)
        VV_CAST[st] = vv.n
        ws = [(vv, VV_CAST[st])]
        if st > 0:
            ws.append((mst, st))
        P.on("scalar", lambda e, tw=tw: e.activation(
            out=x2b_sb[:, :, 0:tw], in_=xb_sb[:, :, 0:tw], func=AF.Square),
            wait=ws, inc=ss)
        SS_SQ[st] = ss.n
        def fstat(e, tw=tw):
            for a in range(CCH):
                e.matmul(out=ps_sum[0:1, 0:tw], lhsT=onesb_sb[:], rhs=xb_sb[:, a, 0:tw],
                         start=(a == 0), stop=(a == CCH - 1))
            ins = None
            for a in range(CCH):
                ins = e.matmul(out=ps_ssq[0:1, 0:tw], lhsT=onesb_sb[:],
                               rhs=x2b_sb[:, a, 0:tw], start=(a == 0), stop=(a == CCH - 1))
            return ins
        wt = [(vv, VV_CAST[st]), (ss, SS_SQ[st]), (vv, VV_ONES)]
        if st > 0:
            wt.append((sst, 2 * st))
        P.on("tensor", fstat, wait=wt, inc=mst)
        P.on("scalar", lambda e, t0=t0, tw=tw: e.activation(
            out=stat_sb[0:1, 0, t0:t0 + tw], in_=ps_sum[0:1, 0:tw], func=AF.Copy),
            wait=[(mst, st + 1)], inc=sst)
        P.on("scalar", lambda e, t0=t0, tw=tw: e.activation(
            out=stat_sb[0:1, 1, t0:t0 + tw], in_=ps_ssq[0:1, 0:tw], func=AF.Copy),
            wait=[], inc=sst)
    assert mst.n == 5 and sst.n == 10

    P.on("scalar", lambda e: e.activation(out=stat_sb[0:1, 0, :], in_=stat_sb[0:1, 0, :],
                                          func=AF.Copy, scale=1.0 / C),
         wait=[(sst, 10)], inc=ss)
    P.on("scalar", lambda e: e.activation(out=ht1_sb[0:1, :], in_=stat_sb[0:1, 0, :],
                                          func=AF.Square), wait=[], inc=ss)
    SS_MSQ = ss.n
    P.on("vector", lambda e: e.scalar_tensor_tensor(
        out=stat_sb[0:1, 1, :], in0=stat_sb[0:1, 1, :], scalar=1.0 / C,
        in1=ht1_sb[0:1, :], op0=ALU.mult, op1=ALU.subtract),
        wait=[(ss, SS_MSQ)], inc=rr)
    RR_VAR = rr.n
    P.on("scalar", lambda e: e.activation(out=stat_sb[0:1, 1, :], in_=stat_sb[0:1, 1, :],
                                          func=AF.Ln, bias=eps_sb[0:1, :]),
         wait=[(rr, RR_VAR), (vv, VV_ONES)], inc=ss)
    P.on("scalar", lambda e: e.activation(out=stat_sb[0:1, 1, :], in_=stat_sb[0:1, 1, :],
                                          func=AF.Exp, scale=-0.5), wait=[], inc=ss)
    SS_RSTD = ss.n
    P.on("gpsimd", lambda e: e.tensor_mul(out=ht1_sb[0:1, :], in0=stat_sb[0:1, 0, :],
                                          in1=stat_sb[0:1, 1, :]),
         wait=[(ss, SS_RSTD)], inc=rr)
    RR_MH = rr.n
    P.on("gpsimd", lambda e: e.dma_start(out=rowscr[0:1, :], in_=stat_sb[0:1, 1, :]),
         wait=[], inc=bb, k=16)
    P.on("gpsimd", lambda e: e.dma_start(out=rowscr[1:2, :], in_=ht1_sb[0:1, :]),
         wait=[], inc=bb, k=16)
    BB_W = bb.n
    P.on("gpsimd", lambda e: e.dma_start(
        out=brow_sb[:, 0, :], in_=rowscr[0:1, :].partition_broadcast(128).squeeze(1)),
        wait=[(bb, BB_W)], inc=bb, k=16)
    P.on("gpsimd", lambda e: e.dma_start(
        out=brow_sb[:, 1, :], in_=rowscr[1:2, :].partition_broadcast(128).squeeze(1)),
        wait=[], inc=bb, k=16)
    BB_ROWS = bb.n

    for c in range(CCH):
        def fh(e, c=c):
            e.tensor_mul(out=ht1_sb[:], in0=x_sb[:, c, :], in1=brow_sb[:, 0, :])
            e.tensor_sub(out=ht2_sb[:], in0=ht1_sb[:], in1=brow_sb[:, 1, :])
            e.tensor_scalar(out=h_sb[:, c, :], in0=ht2_sb[:],
                            scalar1=A_sb[:, c:c + 1], scalar2=mod_sb[:, c:c + 1],
                            op0=ALU.mult, op1=ALU.add)
            e.tensor_mul(out=h_sb[:, c, 0:1], in0=h_sb[:, c, 0:1], in1=mask_sb[:, 0:1])
            return e.tensor_mul(out=h_sb[:, c, WH - 1:WH], in0=h_sb[:, c, WH - 1:WH],
                                in1=mask_sb[:, 1:2])
        wv = [(bb, BB_ROWS), (rr, RR_MH), (vmd, VMD_DONE), (vv, VV_MASK)] \
            if c == 0 else []
        P.on("vector", fh, wait=wv, inc=hrd)
    assert hrd.n == CCH

    P.on("gpsimd", lambda e: e.dma_start(out=fm(h_o), in_=h_sb[:]),
         wait=[(hrd, CCH)], inc=xx, k=16)
    P.on("gpsimd", lambda e: e.dma_start(out=mod_o, in_=mod_sb[:]),
         wait=[(vmd, VMD_DONE)], inc=xx, k=16)

    for c in range(4):
        hp = h_sb[:, c, 0:TC] if c < 2 else h_sb[:, c, 2:WH]
        wd = [(hrd, CCH)] + ([(urd, 4 * c)] if c > 0 else [])
        P.on("vector", lambda e, c=c, hp=hp: e.tensor_sub(
            out=dscr_sb, in0=h_sb[:, c, 1:TC + 1], in1=hp), wait=wd, inc=urd)
        U_D = urd.n
        for ip in range(3):
            eng = "vector"
            P.on(eng, lambda e, c=c, ip=ip, hp=hp: e.scalar_tensor_tensor(
                out=u_sb[:, ip, c, :], in0=dscr_sb, scalar=mix_sb[:, ip, c:c + 1],
                in1=hp, op0=ALU.mult, op1=ALU.add),
                wait=[(urd, U_D), (ld, LD_MIX)], inc=urd)
    assert urd.n == 16
    P.on("gpsimd", lambda e: e.dma_start(
        out=u_o.rearrange("i (a p) t -> p i a t", p=128), in_=u_sb),
        wait=[(urd, 16)], inc=xx, k=16)

    for t9 in range(9):
        w = [(dqp, t9 - 1)] if t9 >= 2 else []
        if t9 < 3:
            src = wup_sh.rearrange("(a p) c -> p a c", p=128)[:, t9, :]
        elif t9 < 6:
            src = wgate_sh.rearrange("(a p) c -> p a c", p=128)[:, t9 - 3, :]
        else:
            src = wdown_sh[:, (t9 - 6) * 1024:(t9 - 5) * 1024]
        P.on("sync", lambda e, src=src, t9=t9: e.dma_start(
            out=dqw_sb[:, t9 % 2, :], in_=src), wait=w, inc=dql[t9 % 2], k=16)
        def fdq1(e, t9=t9):
            e.tensor_reduce(out=dqi_sb[:],
                            in_=dqw_sb[:, t9 % 2, :].rearrange("p (b k) -> p b k", k=32),
                            axis=AX, op=ALU.max, apply_absolute_value=True)
            e.tensor_scalar(out=dqs_sb[:], in0=dqi_sb[:],
                            scalar1=1.0 / 7.0, scalar2=1e-12,
                            op0=ALU.mult, op1=ALU.max)
            return e.reciprocal(out=dqi_sb[:], in_=dqs_sb[:])
        wq1 = [(dql[t9 % 2], 16 * (t9 // 2 + 1))] + ([(dqp, t9)] if t9 >= 1 else [])
        P.on("vector", fdq1, wait=wq1, inc=dqv)
        def fdq2(e, t9=t9):
            b1 = dqi_sb.ap().unsqueeze(2).broadcast_to([128, 32, 32])
            w3 = dqw_sb[:, t9 % 2, :].rearrange("p (b k) -> p b k", k=32)
            e.tensor_tensor(out=w3, in0=w3, in1=b1, op=ALU.mult)
            e.tensor_scalar(out=dqw_sb[:, t9 % 2, :], in0=dqw_sb[:, t9 % 2, :],
                            scalar1=MAGIC, scalar2=None, op0=ALU.add)
            e.tensor_scalar(out=dqw_sb[:, t9 % 2, :], in0=dqw_sb[:, t9 % 2, :],
                            scalar1=MAGIC, scalar2=None, op0=ALU.subtract)
            b2 = dqs_sb.ap().unsqueeze(2).broadcast_to([128, 32, 32])
            return e.tensor_tensor(
                out=dqo_sb[:, t9 % 2, :].rearrange("p (b k) -> p b k", k=32),
                in0=w3, in1=b2, op=ALU.mult)
        P.on("vector", fdq2,
             wait=[(dqv, t9 + 1)] + ([(xdq, 16 * (t9 - 1))] if t9 >= 2 else []),
             inc=dqp)
        if t9 < 3:
            P.on("gpsimd", lambda e, t9=t9: e.dma_start(
                out=wupd_o.rearrange("(a p) c -> p a c", p=128)[:, t9, :],
                in_=dqo_sb[:, t9 % 2, :]), wait=[(dqp, t9 + 1)], inc=xdq, k=16)
        elif t9 < 6:
            P.on("gpsimd", lambda e, t9=t9: e.dma_start(
                out=wgated_o.rearrange("(a p) c -> p a c", p=128)[:, t9 - 3, :],
                in_=dqo_sb[:, t9 % 2, :]), wait=[(dqp, t9 + 1)], inc=xdq, k=16)
        else:
            P.on("gpsimd", lambda e, t9=t9: e.dma_start(
                out=wdownd_o[:, (t9 - 6) * 1024:(t9 - 5) * 1024],
                in_=dqo_sb[:, t9 % 2, :]), wait=[(dqp, t9 + 1)], inc=xdq, k=16)

    P.on("sync", lambda e: None, wait=[(xx, xx.n), (xdq, xdq.n)])
    P.emit()
    return nc



def build_lb():
    nc = bass.Bass("TRN2", target_bir_lowering=False)
    h_i = nc.dram_tensor("h_i", [C, WH], BF16, kind="ExternalInput").ap()
    u_i = nc.dram_tensor("u_i", [3, C // 2, TC], BF16, kind="ExternalInput").ap()
    wkT = nc.dram_tensor("wkT", [C, C], F32, kind="ExternalInput").ap()
    wvT = nc.dram_tensor("wvT", [C, C], F32, kind="ExternalInput").ap()
    wrT = nc.dram_tensor("wrT", [C, C], F32, kind="ExternalInput").ap()
    sdec = nc.dram_tensor("sdec", [128, CCH], F32, kind="ExternalInput").ap()

    ek_o = nc.dram_tensor("ek_o", [C, TC], F32, kind="ExternalOutput").ap()
    ekv_o = nc.dram_tensor("ekv_o", [C, TC], F32, kind="ExternalOutput").ap()
    sr_o = nc.dram_tensor("sr_o", [C, TC], F32, kind="ExternalOutput").ap()
    afin_o = nc.dram_tensor("afin_o", [128, CCH], F32, kind="ExternalOutput").ap()
    bfin_o = nc.dram_tensor("bfin_o", [128, CCH], F32, kind="ExternalOutput").ap()

    h_sb = nc.alloc_sbuf_tensor("h_sb", [128, CCH, WH], BF16)
    u_sb = nc.alloc_sbuf_tensor("u_sb", [128, 3, 4, TC], BF16)
    wsf_sb = nc.alloc_sbuf_tensor("wsf_sb", [128, 2, 3, CCH, 128], F32)
    wsb_sb = nc.alloc_sbuf_tensor("wsb_sb", [128, 2, 3, CCH, 128], BF16)
    sd_sb = nc.alloc_sbuf_tensor("sd_sb", [128, CCH], F32)
    ew_sb = nc.alloc_sbuf_tensor("ew_sb", [128, CCH], F32)
    tmp8_sb = nc.alloc_sbuf_tensor("tmp8_sb", [128, CCH], F32)
    ek_sb = nc.alloc_sbuf_tensor("ek_sb", [128, TC], F32)
    v_sb = nc.alloc_sbuf_tensor("v_sb", [128, TC], F32)
    ekv_sb = nc.alloc_sbuf_tensor("ekv_sb", [128, TC], F32)
    sr_sb = nc.alloc_sbuf_tensor("sr_sb", [128, TC], F32)
    ewt_sb = nc.alloc_sbuf_tensor("ewt_sb", [128, TC], F32)
    asc_sb = nc.alloc_sbuf_tensor("asc_sb", [128, TC], F32)
    bsc_sb = nc.alloc_sbuf_tensor("bsc_sb", [128, TC], F32)
    afin_sb = nc.alloc_sbuf_tensor("afin_sb", [128, CCH], F32)
    bfin_sb = nc.alloc_sbuf_tensor("bfin_sb", [128, CCH], F32)

    ps_kvr = [nc.alloc_psum_tensor(f"ps_kvr{i}", [128, 512], F32) for i in range(4)]

    P = Prog(nc)
    ld = P.sem("ld")
    awl = [P.sem(f"awl{i}") for i in range(2)]
    wcs = P.sem("wcs")
    vv = P.sem("vv")
    mkv = P.sem("mkv")
    evc = P.sem("evc")
    vkv = P.sem("vkv")
    xx = P.sem("xx")
    xek = P.sem("xek")
    xekv = P.sem("xekv")
    xsr = P.sem("xsr")

    P.on("sync", lambda e: e.dma_start(out=h_sb[:], in_=fm(h_i)), inc=ld, k=16)
    P.on("sync", lambda e: e.dma_start(
        out=u_sb[:], in_=u_i.rearrange("i (a p) t -> p i a t", p=128)), inc=ld, k=16)
    P.on("sync", lambda e: e.dma_start(out=sd_sb[:], in_=sdec), inc=ld, k=16)
    LD_H = LD_U = LD_SD = 48

    def few(e):
        e.tensor_scalar(out=ew_sb[:], in0=sd_sb[:], scalar1=1.0 / T, scalar2=None,
                        op0=ALU.mult)
        return exp_poly(e, ew_sb[:], ew_sb[:], tmp8_sb[:])
    P.on("vector", few, wait=[(ld, LD_SD)], inc=vv)
    VV_EW = vv.n

    def emit_wslice(oc):
        for i, t in enumerate((wkT, wvT, wrT)):
            w = []
            if oc >= 2:
                w += [(wcs, 3 * (oc - 1)), (mkv, 12 * (oc - 1))]
            P.on("sync", lambda e, t=t, oc=oc, i=i: e.dma_start(
                out=wsf_sb[:, oc % 2, i, :, :],
                in_=fm(t)[:, :, oc * 128:(oc + 1) * 128]), wait=w, inc=awl[oc % 2], k=16)
        wc = [(awl[oc % 2], 48 * (oc // 2 + 1))]
        if oc >= 2:
            wc.append((mkv, 12 * (oc - 1)))
        P.on("scalar", lambda e, oc=oc: e.activation(
            out=wsb_sb.ap()[:, oc % 2].rearrange("p i a c -> p (i a c)"),
            in_=wsf_sb.ap()[:, oc % 2].rearrange("p i a c -> p (i a c)"),
            func=AF.Copy), wait=wc, inc=wcs, k=3)

    emit_wslice(0)
    emit_wslice(1)

    for oc in range(CCH):
        for ip in range(3):
            for tt in range(NTT):
                g = oc * 12 + ip * 4 + tt
                w = [(ld, LD_U), (wcs, 3 * (oc + 1))]
                if g >= 4:
                    w.append((evc, g - 3))
                def fkvr(e, oc=oc, ip=ip, tt=tt, g=g):
                    bank = ps_kvr[g % 4]
                    ins = None
                    for cin in range(CCH):
                        if cin < 4:
                            rhs = u_sb[:, ip, cin, tt * 512:(tt + 1) * 512]
                        else:
                            rhs = h_sb[:, cin, 1 + tt * 512:1 + (tt + 1) * 512]
                        ins = e.matmul(out=bank[:], lhsT=wsb_sb[:, oc % 2, ip, cin, :],
                                       rhs=rhs, start=(cin == 0), stop=(cin == CCH - 1))
                    return ins
                P.on("tensor", fkvr, wait=w, inc=mkv)
                we = [(mkv, g + 1)]
                if oc > 0 and tt == 0:
                    we.append(((xek, xekv, xsr)[ip], 16 * oc))
                    if ip <= 1:
                        we.append((vkv, 6 * oc))
                def fev(e, ip=ip, tt=tt, g=g):
                    bank = ps_kvr[g % 4]
                    sl = slice(tt * 512, (tt + 1) * 512)
                    if ip == 0:
                        return e.activation(out=ek_sb[:, sl], in_=bank[:], func=AF.Exp)
                    if ip == 1:
                        return e.activation(out=v_sb[:, sl], in_=bank[:], func=AF.Copy)
                    return e.activation(out=sr_sb[:, sl], in_=bank[:], func=AF.Sigmoid)
                P.on("scalar", fev, wait=we, inc=evc)
        if oc + 2 < CCH:
            emit_wslice(oc + 2)
        def fewt(e, oc=oc):
            e.memset(ewt_sb[:], 1.0)
            return e.tensor_scalar(out=ewt_sb[:], in0=ewt_sb[:],
                                   scalar1=ew_sb[:, oc:oc + 1], scalar2=None,
                                   op0=ALU.mult)
        P.on("vector", fewt, wait=[(vv, VV_EW), (evc, oc * 12 + 4)], inc=vkv)
        P.on("vector", lambda e: e.tensor_mul(out=ekv_sb[:], in0=ek_sb[:], in1=v_sb[:]),
             wait=[(evc, oc * 12 + 8)], inc=vkv)
        P.on("vector", lambda e: e.tensor_tensor_scan(
            out=asc_sb[:], data0=ewt_sb[:], data1=ekv_sb[:], initial=0.0,
            op0=ALU.mult, op1=ALU.add), wait=[], inc=vkv)
        P.on("vector", lambda e, oc=oc: e.tensor_copy(
            out=afin_sb[:, oc:oc + 1], in_=asc_sb[:, TC - 1:TC]), wait=[], inc=vkv)
        P.on("vector", lambda e: e.tensor_tensor_scan(
            out=bsc_sb[:], data0=ewt_sb[:], data1=ek_sb[:], initial=0.0,
            op0=ALU.mult, op1=ALU.add), wait=[], inc=vkv)
        P.on("vector", lambda e, oc=oc: e.tensor_copy(
            out=bfin_sb[:, oc:oc + 1], in_=bsc_sb[:, TC - 1:TC]), wait=[], inc=vkv)
        P.on("gpsimd", lambda e, oc=oc: e.dma_start(
            out=fm(ek_o)[:, oc, :], in_=ek_sb[:]), wait=[(vkv, 6 * oc + 5)], inc=xek, k=16)
        P.on("gpsimd", lambda e, oc=oc: e.dma_start(
            out=fm(ekv_o)[:, oc, :], in_=ekv_sb[:]), wait=[(vkv, 6 * oc + 3)], inc=xekv, k=16)
        P.on("gpsimd", lambda e, oc=oc: e.dma_start(
            out=fm(sr_o)[:, oc, :], in_=sr_sb[:]), wait=[(evc, oc * 12 + 12)], inc=xsr, k=16)
    assert mkv.n == 96 and evc.n == 96 and vkv.n == 48

    P.on("gpsimd", lambda e: e.dma_start(out=afin_o, in_=afin_sb[:]),
         wait=[(vkv, 48)], inc=xx, k=16)
    P.on("gpsimd", lambda e: e.dma_start(out=bfin_o, in_=bfin_sb[:]),
         wait=[(vkv, 48)], inc=xx, k=16)
    P.on("sync", lambda e: None, wait=[(xx, xx.n), (xek, xek.n), (xekv, xekv.n),
                                       (xsr, xsr.n)])
    P.emit()
    return nc



def build_l2():
    nc = bass.Bass("TRN2", target_bir_lowering=False)
    xT = nc.dram_tensor("xT", [C, WH], F32, kind="ExternalInput").ap()
    ek_i = nc.dram_tensor("ek_i", [C, TC], F32, kind="ExternalInput").ap()
    ekv_i = nc.dram_tensor("ekv_i", [C, TC], F32, kind="ExternalInput").ap()
    sr_i = nc.dram_tensor("sr_i", [C, TC], F32, kind="ExternalInput").ap()
    af_all = nc.dram_tensor("af_all", [128, CCH, 8], F32, kind="ExternalInput").ap()
    bf_all = nc.dram_tensor("bf_all", [128, CCH, 8], F32, kind="ExternalInput").ap()
    pw = nc.dram_tensor("pw", [128, 8], F32, kind="ExternalInput").ap()
    pmask = nc.dram_tensor("pmask", [128, 8], F32, kind="ExternalInput").ap()
    woT = nc.dram_tensor("woT", [C, C], F32, kind="ExternalInput").ap()
    mod_i = nc.dram_tensor("mod_i", [128, 48], F32, kind="ExternalInput").ap()
    sdec = nc.dram_tensor("sdec", [128, CCH], F32, kind="ExternalInput").ap()
    sfirst = nc.dram_tensor("sfirst", [128, CCH], F32, kind="ExternalInput").ap()

    x2_o = nc.dram_tensor("x2_o", [C, TC], F32, kind="ExternalOutput").ap()
    h2_o = nc.dram_tensor("h2_o", [C, TC], F32, kind="ExternalOutput").ap()
    mh2_o = nc.dram_tensor("mh2_o", [1, 1], F32, kind="ExternalOutput").ap()

    colscr = nc.dram_tensor("colscr", [128, 1], F32, kind="Internal").ap()

    y_sb = nc.alloc_sbuf_tensor("y_sb", [128, CCH, TC], BF16)
    wob_sb = nc.alloc_sbuf_tensor("wob_sb", [128, CCH, C], BF16)
    a2_sb = nc.alloc_sbuf_tensor("a2_sb", [128, TC + 1], F32)
    b2_sb = nc.alloc_sbuf_tensor("b2_sb", [128, TC + 1], F32)
    num_sb = nc.alloc_sbuf_tensor("num_sb", [128, TC], F32)
    den_sb = nc.alloc_sbuf_tensor("den_sb", [128, TC], F32)
    ewt_sb = nc.alloc_sbuf_tensor("ewt_sb", [128, TC], F32)
    xt_sb = nc.alloc_sbuf_tensor("xt_sb", [128, 4, 512], F32)
    h2t_sb = nc.alloc_sbuf_tensor("h2t_sb", [128, 2, 512], F32)
    af_sb = nc.alloc_sbuf_tensor("af_sb", [128, 2, CCH, 8], F32)
    d3_sb = nc.alloc_sbuf_tensor("d3_sb", [128, CCH, 8], F32)
    t3_sb = nc.alloc_sbuf_tensor("t3_sb", [128, CCH, 8], F32)
    ain_sb = nc.alloc_sbuf_tensor("ain_sb", [128, 2, CCH], F32)
    pw_sb = nc.alloc_sbuf_tensor("pw_sb", [128, 2, 8], F32)
    sd_sb = nc.alloc_sbuf_tensor("sd_sb", [128, CCH], F32)
    sf_sb = nc.alloc_sbuf_tensor("sf_sb", [128, CCH], F32)
    ew_sb = nc.alloc_sbuf_tensor("ew_sb", [128, CCH], F32)
    eu_sb = nc.alloc_sbuf_tensor("eu_sb", [128, CCH], F32)
    tmp8_sb = nc.alloc_sbuf_tensor("tmp8_sb", [128, CCH], F32)
    mod_sb = nc.alloc_sbuf_tensor("mod_sb", [128, 48], F32)
    ga_sb = nc.alloc_sbuf_tensor("ga_sb", [128, CCH], F32)
    gat_sb = nc.alloc_sbuf_tensor("gat_sb", [128, 2, CCH], F32)
    B2_sb = nc.alloc_sbuf_tensor("B2_sb", [128, CCH], F32)
    run_sb = nc.alloc_sbuf_tensor("run_sb", [128, 2], F32)
    mrow_sb = nc.alloc_sbuf_tensor("mrow_sb", [1, 130], F32)
    stat2_sb = nc.alloc_sbuf_tensor("stat2_sb", [1, 3, 512], F32)
    onesb_sb = nc.alloc_sbuf_tensor("onesb_sb", [128, 1], BF16)
    ones1_sb = nc.alloc_sbuf_tensor("ones1_sb", [1, 128], F32)
    eps_sb = nc.alloc_sbuf_tensor("eps_sb", [1, 1], F32)
    ZW = Arena(nc, "zoneW", 33 * 1024)
    wof_sb = ZW.carve([CCH, C], F32)
    ZW.reset()
    x2c_sb = ZW.carve([2, CCH, 512], F32)
    ZS = Arena(nc, "zoneS", 48 * 1024)
    ek2_sb = ZS.carve([2, TC], F32)
    ekv2_sb = ZS.carve([2, TC], F32)
    sr2_sb = ZS.carve([2, TC], F32)
    ZS.reset()
    xb2_sb = ZS.carve([CCH, 512], BF16)
    x2b2_sb = ZS.carve([CCH, 512], BF16)

    ps_o = [nc.alloc_psum_tensor(f"ps_o{i}", [128, 512], F32) for i in range(4)]
    ps_sum = nc.alloc_psum_tensor("ps_sum", [1, 512], F32)
    ps_ssq = nc.alloc_psum_tensor("ps_ssq", [1, 512], F32)
    ps_bc = [nc.alloc_psum_tensor(f"ps_bc{i}", [128, 512], F32) for i in range(2)]

    P = Prog(nc)
    ld = P.sem("ld")
    awo = P.sem("awo")
    cek = P.sem("cek")
    cekv = P.sem("cekv")
    csr = P.sem("csr")
    xld = [P.sem(f"xld{i}") for i in range(4)]
    hxx = [P.sem(f"hxx{i}") for i in range(2)]
    x2x = P.sem("x2x")
    vv = P.sem("vv")
    sph = P.sem("sph")
    vph = P.sem("vph")
    mm = P.sem("mm")
    evc = P.sem("evc")
    qq = P.sem("qq")
    mst = P.sem("mst")
    bcm = P.sem("bcm")
    sst = P.sem("sst")
    bb = P.sem("bb")
    hx = P.sem("hx")
    xx = P.sem("xx")

    P.on("sync", lambda e: e.dma_start(out=af_sb[:, 0], in_=af_all), inc=ld, k=16)
    P.on("sync", lambda e: e.dma_start(out=af_sb[:, 1], in_=bf_all), inc=ld, k=16)
    P.on("sync", lambda e: e.dma_start(out=pw_sb[:, 0], in_=pw), inc=ld, k=16)
    P.on("sync", lambda e: e.dma_start(out=pw_sb[:, 1], in_=pmask), inc=ld, k=16)
    P.on("sync", lambda e: e.dma_start(out=mod_sb[:], in_=mod_i), inc=ld, k=16)
    P.on("sync", lambda e: e.dma_start(out=sd_sb[:], in_=sdec), inc=ld, k=16)
    P.on("sync", lambda e: e.dma_start(out=sf_sb[:], in_=sfirst), inc=ld, k=16)
    LD_SMALL = ld.n
    P.on("gpsimd", lambda e: e.dma_start(out=wof_sb, in_=fm(woT)), inc=awo, k=16)

    for _oc in (0, 1):
        P.on("sync", lambda e, oc=_oc: e.dma_start(
            out=ek2_sb[:, oc % 2, :], in_=fm(ek_i)[:, oc, :]), inc=cek, k=16)
        P.on("scalar", lambda e, oc=_oc: e.dma_start(
            out=ekv2_sb[:, oc % 2, :], in_=fm(ekv_i)[:, oc, :]), inc=cekv, k=16)
        P.on("scalar", lambda e, oc=_oc: e.dma_start(
            out=sr2_sb[:, oc % 2, :], in_=fm(sr_i)[:, oc, :]), inc=csr, k=16)

    def fsetup(e):
        e.memset(onesb_sb[:], 1.0)
        e.memset(ones1_sb[:], 1.0)
        e.memset(eps_sb[:], EPS)
        e.tensor_scalar(out=ew_sb[:], in0=sd_sb[:], scalar1=1.0 / T, scalar2=None,
                        op0=ALU.mult)
        exp_poly(e, ew_sb[:], ew_sb[:], tmp8_sb[:])
        e.tensor_scalar(out=eu_sb[:], in0=sf_sb[:], scalar1=1.0 / T, scalar2=None,
                        op0=ALU.mult)
        exp_poly(e, eu_sb[:], eu_sb[:], tmp8_sb[:])
        e.tensor_scalar(out=B2_sb[:], in0=mod_sb[:, 32:40], scalar1=1.0, scalar2=None,
                        op0=ALU.add)
        return e.tensor_tensor(
            out=t3_sb[:], in0=sd_sb.ap().unsqueeze(2).broadcast_to([128, CCH, 8]),
            in1=pw_sb.ap()[:, 0, :].unsqueeze(1).broadcast_to([128, CCH, 8]),
            op=ALU.mult)
    P.on("vector", fsetup, wait=[(ld, LD_SMALL)], inc=vv)
    VV_T3 = vv.n
    P.on("scalar", lambda e: e.activation(out=d3_sb[:], in_=t3_sb[:], func=AF.Exp),
         wait=[(vv, VV_T3)], inc=sph)
    P.on("scalar", lambda e: e.activation(out=gat_sb[:, 0], in_=mod_sb[:, 16:24],
                                          func=AF.Abs), wait=[(ld, LD_SMALL)], inc=sph)
    P.on("scalar", lambda e: e.sign(out=gat_sb[:, 1], in_=mod_sb[:, 16:24]),
         wait=[], inc=sph)
    SPH_GA = sph.n

    def fprefix(e):
        e.tensor_tensor(out=d3_sb[:], in0=d3_sb[:],
                        in1=pw_sb.ap()[:, 1, :].unsqueeze(1).broadcast_to([128, CCH, 8]),
                        op=ALU.mult)
        e.tensor_tensor(out=t3_sb[:], in0=af_sb[:, 0], in1=d3_sb[:], op=ALU.mult)
        e.tensor_reduce(out=ain_sb[:, 0], in_=t3_sb[:], axis=AX, op=ALU.add)
        e.tensor_tensor(out=t3_sb[:], in0=af_sb[:, 1], in1=d3_sb[:], op=ALU.mult)
        e.tensor_reduce(out=ain_sb[:, 1], in_=t3_sb[:], axis=AX, op=ALU.add)
        e.tensor_scalar(out=gat_sb[:, 0], in0=gat_sb[:, 0], scalar1=0.25,
                        scalar2=None, op0=ALU.max)
        e.tensor_tensor(out=ga_sb[:], in0=gat_sb[:, 1], in1=gat_sb[:, 0],
                        op=ALU.mult)
        return e.memset(run_sb[:], 0.0)
    P.on("vector", fprefix, wait=[(sph, SPH_GA)], inc=vv)
    VV_PREFIX = vv.n

    P.on("scalar", lambda e: e.activation(
        out=wob_sb.ap().rearrange("p a b -> p (a b)"),
        in_=wof_sb.rearrange("p a b -> p (a b)"), func=AF.Copy),
        wait=[(awo, 16)], inc=qq)
    QQ_WO = qq.n

    AM, BM, YM = {}, {}, {}

    def emit_loads(oc):
        wek = [(vph, BM[oc - 2])] if oc >= 2 else []
        wsr = [(vph, YM[oc - 2])] if oc >= 2 else []
        P.on("sync", lambda e, oc=oc: e.dma_start(
            out=ek2_sb[:, oc % 2, :], in_=fm(ek_i)[:, oc, :]), wait=wek, inc=cek, k=16)
        P.on("scalar", lambda e, oc=oc: e.dma_start(
            out=ekv2_sb[:, oc % 2, :], in_=fm(ekv_i)[:, oc, :]), wait=wek,
            inc=cekv, k=16)
        P.on("scalar", lambda e, oc=oc: e.dma_start(
            out=sr2_sb[:, oc % 2, :], in_=fm(sr_i)[:, oc, :]), wait=wsr, inc=csr, k=16)

    def emit_A(oc):
        def fA(e, oc=oc):
            sl = oc % 2
            e.memset(ewt_sb[:], 1.0)
            e.tensor_scalar(out=ewt_sb[:], in0=ewt_sb[:], scalar1=ew_sb[:, oc:oc + 1],
                            scalar2=None, op0=ALU.mult)
            e.tensor_tensor_scan(out=a2_sb[:, 1:TC + 1], data0=ewt_sb[:],
                                 data1=ekv2_sb[:, sl, :],
                                 initial=ain_sb[:, 0, oc:oc + 1],
                                 op0=ALU.mult, op1=ALU.add)
            e.tensor_tensor_scan(out=b2_sb[:, 1:TC + 1], data0=ewt_sb[:],
                                 data1=ek2_sb[:, sl, :],
                                 initial=ain_sb[:, 1, oc:oc + 1],
                                 op0=ALU.mult, op1=ALU.add)
            e.tensor_copy(out=a2_sb[:, 0:1], in_=ain_sb[:, 0, oc:oc + 1])
            return e.tensor_copy(out=b2_sb[:, 0:1], in_=ain_sb[:, 1, oc:oc + 1])
        P.on("vector", fA, wait=[(cek, 16 * (oc + 1)), (cekv, 16 * (oc + 1)),
                                 (vv, VV_PREFIX)], inc=vph)
        AM[oc] = vph.n

    def emit_B(oc):
        def fB(e, oc=oc):
            sl = oc % 2
            e.scalar_tensor_tensor(out=num_sb[:], in0=ekv2_sb[:, sl, :],
                                   scalar=eu_sb[:, oc:oc + 1], in1=a2_sb[:, 0:TC],
                                   op0=ALU.mult, op1=ALU.add)
            return e.scalar_tensor_tensor(out=den_sb[:], in0=ek2_sb[:, sl, :],
                                          scalar=eu_sb[:, oc:oc + 1],
                                          in1=b2_sb[:, 0:TC],
                                          op0=ALU.mult, op1=ALU.add)
        w = [(sph, SPH_GA + oc)] if oc >= 1 else []
        P.on("vector", fB, wait=w, inc=vph)
        BM[oc] = vph.n
        def frden(e):
            e.activation(out=den_sb[:], in_=den_sb[:], func=AF.Ln)
            return e.activation(out=den_sb[:], in_=den_sb[:], func=AF.Exp, scale=-1.0)
        P.on("scalar", frden, wait=[(vph, BM[oc])], inc=sph)

    def emit_Y(oc):
        def fy(e, oc=oc):
            e.tensor_mul(out=num_sb[:], in0=num_sb[:], in1=den_sb[:])
            return e.tensor_mul(out=y_sb[:, oc, :], in0=sr2_sb[:, oc % 2, :],
                                in1=num_sb[:])
        P.on("vector", fy, wait=[(sph, SPH_GA + oc + 1), (csr, 16 * (oc + 1))],
             inc=vph)
        YM[oc] = vph.n

    emit_A(0)
    emit_B(0)
    for oc in range(1, CCH):
        emit_A(oc)
        emit_Y(oc - 1)
        if oc + 1 < CCH:
            emit_loads(oc + 1)
        emit_B(oc)
    emit_Y(CCH - 1)
    assert vph.n == 3 * CCH

    scq = P.sem("scq")

    def emit_omm(tt):
        sl = tt % 2
        for oc2 in range(CCH):
            g = tt * CCH + oc2
            w = [(evc, g - 3)] if g >= 4 else []
            P.on("sync", lambda e, tt=tt, oc2=oc2, g=g: e.dma_start(
                out=xt_sb[:, g % 4, :],
                in_=fm(xT)[:, oc2, 1 + tt * 512:1 + (tt + 1) * 512]),
                wait=w, inc=xld[g % 4], k=16)
            wm_ = [(qq, QQ_WO), (xld[g % 4], 16 * (g // 4 + 1))]
            if g >= 4:
                wm_.append((evc, g - 3))
            def fomm(e, tt=tt, oc2=oc2, g=g):
                bank = ps_o[g % 4]
                ins = None
                for cin in range(CCH):
                    if g == 0:
                        e.wait_ge(vph.h, YM[cin])
                    ins = e.matmul(out=bank[:],
                                   lhsT=wob_sb[:, cin, oc2 * 128:(oc2 + 1) * 128],
                                   rhs=y_sb[:, cin, tt * 512:(tt + 1) * 512],
                                   start=(cin == 0), stop=(cin == CCH - 1))
                return ins
            P.on("tensor", fomm, wait=wm_, inc=mm)
            wev = [(mm, g + 1)]
            if oc2 == 0 and tt >= 2:
                wev += [(x2x, 16 * (tt - 1)), (scq, 2 * (tt - 1))]
            P.on("vector", lambda e, sl=sl, oc2=oc2, g=g: e.scalar_tensor_tensor(
                out=x2c_sb[:, sl, oc2, :], in0=ps_o[g % 4][:],
                scalar=ga_sb[:, oc2:oc2 + 1],
                in1=xt_sb[:, g % 4, :], op0=ALU.mult, op1=ALU.add),
                wait=wev, inc=evc)
        P.on("gpsimd", lambda e, tt=tt, sl=sl: e.dma_start(
            out=fm(x2_o)[:, :, tt * 512:(tt + 1) * 512], in_=x2c_sb[:, sl]),
            wait=[(evc, (tt + 1) * CCH)], inc=x2x, k=16)

    def emit_castsq(tt):
        sl = tt % 2
        P.on("scalar", lambda e, sl=sl: e.activation(
            out=xb2_sb.rearrange("p a t -> p (a t)"),
            in_=x2c_sb[:, sl].rearrange("p a t -> p (a t)"), func=AF.Copy),
            wait=[(evc, (tt + 1) * CCH)] + ([(mst, tt)] if tt > 0 else []), inc=scq)
        P.on("scalar", lambda e: e.activation(
            out=x2b2_sb.rearrange("p a t -> p (a t)"),
            in_=xb2_sb.rearrange("p a t -> p (a t)"), func=AF.Square), inc=scq)

    def emit_stats(tt):
        def fstat(e):
            for a in range(CCH):
                e.matmul(out=ps_sum[0:1, :], lhsT=onesb_sb[:], rhs=xb2_sb[:, a, :],
                         start=(a == 0), stop=(a == CCH - 1))
            ins = None
            for a in range(CCH):
                ins = e.matmul(out=ps_ssq[0:1, :], lhsT=onesb_sb[:],
                               rhs=x2b2_sb[:, a, :],
                               start=(a == 0), stop=(a == CCH - 1))
            return ins
        wst = [(scq, 2 * tt + 2)]
        if tt > 0:
            wst.append((sst, 3 * tt - 2))
        P.on("tensor", fstat, wait=wst, inc=mst)

    def emit_ln2(tt):
        sl = tt % 2
        def flad1(e):
            e.tensor_scalar(out=stat2_sb[0:1, 0, :], in0=ps_sum[0:1, :],
                            scalar1=1.0 / C, scalar2=None, op0=ALU.mult)
            e.tensor_scalar(out=stat2_sb[0:1, 1, :], in0=ps_ssq[0:1, :],
                            scalar1=1.0 / C, scalar2=None, op0=ALU.mult)
            e.tensor_mul(out=stat2_sb[0:1, 2, :], in0=stat2_sb[0:1, 0, :],
                         in1=stat2_sb[0:1, 0, :])
            return e.tensor_sub(out=stat2_sb[0:1, 1, :], in0=stat2_sb[0:1, 1, :],
                                in1=stat2_sb[0:1, 2, :])
        P.on("vector", flad1, wait=[(mst, tt + 1)], inc=sst)
        def flad2(e):
            e.activation(out=stat2_sb[0:1, 1, :], in_=stat2_sb[0:1, 1, :], func=AF.Ln,
                         bias=eps_sb[0:1, :])
            return e.activation(out=stat2_sb[0:1, 1, :], in_=stat2_sb[0:1, 1, :],
                                func=AF.Exp, scale=-0.5)
        P.on("scalar", flad2, wait=[(sst, 3 * tt + 1)], inc=sst)
        def flad3(e):
            return e.tensor_mul(out=stat2_sb[0:1, 0, :], in0=stat2_sb[0:1, 0, :],
                                in1=stat2_sb[0:1, 1, :])
        P.on("vector", flad3, wait=[(sst, 3 * tt + 2)], inc=sst)
        def fbc(e):
            e.matmul(out=ps_bc[0][:, 0:256], lhsT=ones1_sb[:],
                     rhs=stat2_sb[0:1, 1, 0:256], start=True, stop=True)
            e.matmul(out=ps_bc[0][:, 256:512], lhsT=ones1_sb[:],
                     rhs=stat2_sb[0:1, 1, 256:512], start=True, stop=True)
            e.matmul(out=ps_bc[1][:, 0:256], lhsT=ones1_sb[:],
                     rhs=stat2_sb[0:1, 0, 0:256], start=True, stop=True)
            return e.matmul(out=ps_bc[1][:, 256:512], lhsT=ones1_sb[:],
                            rhs=stat2_sb[0:1, 0, 256:512], start=True, stop=True)
        wbc = [(sst, 3 * tt + 3)]
        if tt > 0:
            wbc.append((hx, tt * CCH))
        P.on("tensor", fbc, wait=wbc, inc=bcm)
        for oc2 in range(CCH):
            def fh2(e, oc2=oc2, sl=sl):
                s2 = oc2 % 2
                e.tensor_mul(out=h2t_sb[:, s2, :], in0=x2c_sb[:, sl, oc2, :],
                             in1=ps_bc[0][:])
                e.tensor_sub(out=h2t_sb[:, s2, :], in0=h2t_sb[:, s2, :],
                             in1=ps_bc[1][:])
                e.tensor_scalar(out=h2t_sb[:, s2, :], in0=h2t_sb[:, s2, :],
                                scalar1=B2_sb[:, oc2:oc2 + 1],
                                scalar2=mod_sb[:, 24 + oc2:25 + oc2],
                                op0=ALU.mult, op1=ALU.add)
                e.tensor_reduce(out=run_sb[:, 1:2], in_=h2t_sb[:, s2, :], axis=AX,
                                op=ALU.max, apply_absolute_value=True)
                return e.tensor_max(out=run_sb[:, 0:1], in0=run_sb[:, 0:1],
                                    in1=run_sb[:, 1:2])
            gh = tt * CCH + oc2
            wv = [(bcm, tt + 1)]
            if gh >= 2:
                wv.append((hxx[gh % 2], 16 * ((gh - 2) // 2 + 1)))
            P.on("vector", fh2, wait=wv, inc=hx)
            P.on("scalar", lambda e, tt=tt, oc2=oc2: e.dma_start(
                out=fm(h2_o)[:, oc2, tt * 512:(tt + 1) * 512],
                in_=h2t_sb[:, oc2 % 2, :]),
                wait=[(hx, tt * CCH + oc2 + 1)], inc=hxx[(tt * CCH + oc2) % 2], k=16)

    for tt in range(NTT):
        emit_omm(tt)
        if tt >= 1:
            emit_stats(tt - 1)
        emit_castsq(tt)
        if tt >= 1:
            emit_ln2(tt - 1)
    emit_stats(NTT - 1)
    emit_ln2(NTT - 1)
    assert mm.n == 32 and evc.n == 32

    emit_absmax_export(P, run_sb[:, 0:1], mrow_sb, colscr.rearrange("a b -> (a b)"),
                       mh2_o, [(hx, 4 * CCH)], bb, vv, xx)

    P.on("sync", lambda e: None,
         wait=[(xx, xx.n), (x2x, x2x.n), (hxx[0], hxx[0].n), (hxx[1], hxx[1].n)])
    P.emit()
    return nc



def emit_scale_prep(P, maxa_sb, srow_sb, scr2, scol_sb, icol_sb, qmax, wait, vv, bb):
    def fs(e):
        e.reduce_max(out=srow_sb[0:1, 0:1], in_=maxa_sb[0:1, :], axis=AX)
        e.tensor_scalar(out=srow_sb[0:1, 1:2], in0=srow_sb[0:1, 0:1],
                        scalar1=1.0 / qmax, scalar2=1e-12, op0=ALU.mult, op1=ALU.max)
        return e.reciprocal(out=srow_sb[0:1, 2:3], in_=srow_sb[0:1, 1:2])
    P.on("vector", fs, wait=wait, inc=vv)
    V1 = vv.n
    P.on("gpsimd", lambda e: e.dma_start(out=scr2.unsqueeze(0), in_=srow_sb[0:1, 1:3]),
         wait=[(vv, V1)], inc=bb, k=16)
    B1 = bb.n
    P.on("gpsimd", lambda e: e.dma_start(
        out=scol_sb[:], in_=scr2[0:1].unsqueeze(0).partition_broadcast(128).squeeze(1)),
        wait=[(bb, B1)], inc=bb, k=16)
    P.on("gpsimd", lambda e: e.dma_start(
        out=icol_sb[:], in_=scr2[1:2].unsqueeze(0).partition_broadcast(128).squeeze(1)),
        wait=[], inc=bb, k=16)
    return bb.n



def build_l3():
    nc = bass.Bass("TRN2", target_bir_lowering=False)
    h2_i = nc.dram_tensor("h2_i", [C, TC], F32, kind="ExternalInput").ap()
    mh2a = nc.dram_tensor("mh2a", [1, 8], F32, kind="ExternalInput").ap()
    wupT_d = nc.dram_tensor("wupT_d", [C, NI], BF16, kind="ExternalInput").ap()
    wgateT_d = nc.dram_tensor("wgateT_d", [C, NI], BF16, kind="ExternalInput").ap()
    gate_o = nc.dram_tensor("gate_o", [NI, TC], F32, kind="ExternalOutput").ap()
    up_o = nc.dram_tensor("up_o", [NI, TC], F32, kind="ExternalOutput").ap()
    mg_o = nc.dram_tensor("mg_o", [1, 1], F32, kind="ExternalOutput").ap()
    mu_o = nc.dram_tensor("mu_o", [1, 1], F32, kind="ExternalOutput").ap()
    scr2 = nc.dram_tensor("scr2", [2], F32, kind="Internal").ap()
    colscr = nc.dram_tensor("colscr", [128], F32, kind="Internal").ap()

    wg_sb = nc.alloc_sbuf_tensor("wg_sb", [128, CCH, NI], BF16)
    wu_sb = nc.alloc_sbuf_tensor("wu_sb", [128, CCH, NI], BF16)
    h2q_sb = nc.alloc_sbuf_tensor("h2q_sb", [128, CCH, TC], BF16)
    h2s_sb = nc.alloc_sbuf_tensor("h2s_sb", [128, 2, CCH, 512], F32)
    ev_sb = nc.alloc_sbuf_tensor("ev_sb", [128, 4, 512], F32)
    maxa_sb = nc.alloc_sbuf_tensor("maxa_sb", [1, 8], F32)
    srow_sb = nc.alloc_sbuf_tensor("srow_sb", [1, 4], F32)
    scol_sb = nc.alloc_sbuf_tensor("scol_sb", [128, 1], F32)
    icol_sb = nc.alloc_sbuf_tensor("icol_sb", [128, 1], F32)
    rung_sb = nc.alloc_sbuf_tensor("rung_sb", [128, 2], F32)
    runu_sb = nc.alloc_sbuf_tensor("runu_sb", [128, 2], F32)
    mrow_sb = nc.alloc_sbuf_tensor("mrow_sb", [1, 130], F32)

    ps = [nc.alloc_psum_tensor(f"ps{i}", [128, 512], F32) for i in range(4)]

    P = Prog(nc)
    ld = P.sem("ld")
    wld = P.sem("wld")
    wld2 = P.sem("wld2")
    hld = [P.sem(f"hld{i}") for i in range(2)]
    xsl = [P.sem(f"xsl{i}") for i in range(4)]
    vv = P.sem("vv")
    ss = P.sem("ss")
    qq = P.sem("qq")
    mm = P.sem("mm")
    evc = P.sem("evc")
    bb = P.sem("bb")
    xx = P.sem("xx")

    P.on("sync", lambda e: e.dma_start(out=maxa_sb[0:1, :], in_=mh2a), inc=ld, k=16)
    for hf in range(2):
        P.on("sync", lambda e, hf=hf: e.dma_start(
            out=wg_sb[:, :, hf * 1536:(hf + 1) * 1536],
            in_=fm(wgateT_d)[:, :, hf * 1536:(hf + 1) * 1536]), inc=wld, k=16)
        P.on("scalar", lambda e, hf=hf: e.dma_start(
            out=wu_sb[:, :, hf * 1536:(hf + 1) * 1536],
            in_=fm(wupT_d)[:, :, hf * 1536:(hf + 1) * 1536]), inc=wld2, k=16)
    BB_SC = emit_scale_prep(P, maxa_sb, srow_sb, scr2, scol_sb, icol_sb,
                            32767.0, [(ld, 16)], vv, bb)
    P.on("vector", lambda e: e.memset(rung_sb[:], 0.0), inc=vv)
    P.on("vector", lambda e: e.memset(runu_sb[:], 0.0), inc=vv)
    VV_MAXBASE = vv.n

    for tt in range(NTT):
        w = [(qq, 2 * (tt - 1))] if tt >= 2 else []
        P.on("sync", lambda e, tt=tt: e.dma_start(
            out=h2s_sb[:, tt % 2], in_=fm(h2_i)[:, :, tt * 512:(tt + 1) * 512]),
            wait=w, inc=hld[tt % 2], k=16)
        def fqdq(e, tt=tt):
            e.tensor_scalar(out=h2s_sb[:, tt % 2], in0=h2s_sb[:, tt % 2],
                            scalar1=icol_sb[:], scalar2=MAGIC,
                            op0=ALU.mult, op1=ALU.add)
            return e.tensor_scalar(out=h2q_sb[:, :, tt * 512:(tt + 1) * 512],
                                   in0=h2s_sb[:, tt % 2], scalar1=MAGIC,
                                   scalar2=scol_sb[:], op0=ALU.subtract, op1=ALU.mult)
        P.on("vector", fqdq, wait=[(hld[tt % 2], 16 * (tt // 2 + 1)), (bb, BB_SC)], inc=qq, k=2)
    QQ_ALL = qq.n

    NG = 0
    for wi, (wsb, out_d, run) in enumerate(((wg_sb, gate_o, rung_sb),
                                            (wu_sb, up_o, runu_sb))):
        for ic in range(ICH):
            for tt in range(NTT):
                g = NG
                NG += 1
                wm_ = [(qq, QQ_ALL), ((wld, 32) if wi == 0 else (wld2, 32))]
                if g >= 4:
                    wm_.append((evc, g - 3))
                def fmm(e, wsb=wsb, ic=ic, tt=tt, g=g):
                    bank = ps[g % 4]
                    ins = None
                    for hc in range(CCH):
                        ins = e.matmul(out=bank[:],
                                       lhsT=wsb[:, hc, ic * 128:(ic + 1) * 128],
                                       rhs=h2q_sb[:, hc, tt * 512:(tt + 1) * 512],
                                       start=(hc == 0), stop=(hc == CCH - 1))
                    return ins
                P.on("tensor", fmm, wait=wm_, inc=mm)
                eng = "scalar"
                sl = g % 4
                wev = [(mm, g + 1)]
                if g >= 4:
                    wev.append((xsl[g % 4], 16 * ((g - 4) // 4 + 1)))
                    wev.append((vv, VV_MAXBASE + g - 3))
                P.on(eng, lambda e, g=g, sl=sl: e.activation(
                    out=ev_sb[:, sl], in_=ps[g % 4][:], func=AF.Copy),
                    wait=wev, inc=evc)
                def fmax(e, g=g, sl=sl, run=run):
                    e.tensor_reduce(out=run[:, 1:2], in_=ev_sb[:, sl], axis=AX,
                                    op=ALU.max, apply_absolute_value=True)
                    return e.tensor_max(out=run[:, 0:1], in0=run[:, 0:1],
                                        in1=run[:, 1:2])
                P.on("vector", fmax, wait=[(evc, g + 1)], inc=vv)
                P.on("gpsimd", lambda e, out_d=out_d, ic=ic, tt=tt, g=g, sl=sl: e.dma_start(
                    out=out_d.rearrange("(a p) t -> p a t", p=128)[:, ic,
                                                                  tt * 512:(tt + 1) * 512],
                    in_=ev_sb[:, sl]), wait=[(evc, g + 1)], inc=xsl[g % 4], k=16)
    assert NG == 192

    emit_absmax_export(P, rung_sb[:, 0:1], mrow_sb, colscr, mg_o,
                       [(vv, vv.n)], bb, vv, xx)
    emit_absmax_export(P, runu_sb[:, 0:1], mrow_sb, colscr, mu_o,
                       [(vv, vv.n)], bb, vv, xx)
    P.on("sync", lambda e: None,
         wait=[(xx, xx.n)] + [(x, x.n) for x in xsl])
    P.emit()
    return nc



def build_l4():
    nc = bass.Bass("TRN2", target_bir_lowering=False)
    gate_i = nc.dram_tensor("gate_i", [NI, TC], F32, kind="ExternalInput").ap()
    mga = nc.dram_tensor("mga", [1, 8], F32, kind="ExternalInput").ap()
    g2_o = nc.dram_tensor("g2_o", [NI, TC], F32, kind="ExternalOutput").ap()
    mg2_o = nc.dram_tensor("mg2_o", [1, 1], F32, kind="ExternalOutput").ap()
    scr2 = nc.dram_tensor("scr2", [2], F32, kind="Internal").ap()
    colscr = nc.dram_tensor("colscr", [128], F32, kind="Internal").ap()

    gt_sb = nc.alloc_sbuf_tensor("gt_sb", [128, 4, TC], F32)
    gq_sb = nc.alloc_sbuf_tensor("gq_sb", [128, 4, TC], F32)
    sg_sb = nc.alloc_sbuf_tensor("sg_sb", [128, 4, TC], F32)
    g2_sb = nc.alloc_sbuf_tensor("g2_sb", [128, 4, TC], F32)
    maxa_sb = nc.alloc_sbuf_tensor("maxa_sb", [1, 8], F32)
    srow_sb = nc.alloc_sbuf_tensor("srow_sb", [1, 4], F32)
    scol_sb = nc.alloc_sbuf_tensor("scol_sb", [128, 1], F32)
    icol_sb = nc.alloc_sbuf_tensor("icol_sb", [128, 1], F32)
    run_sb = nc.alloc_sbuf_tensor("run_sb", [128, 2], F32)
    mrow_sb = nc.alloc_sbuf_tensor("mrow_sb", [1, 130], F32)

    P = Prog(nc)
    ld = P.sem("ld")
    gld = [P.sem(f"gld{i}") for i in range(4)]
    xsl = [P.sem(f"xsl{i}") for i in range(4)]
    vv = P.sem("vv")
    ss = P.sem("ss")
    pp = P.sem("pp")
    qq = P.sem("qq")
    bb = P.sem("bb")
    xx = P.sem("xx")

    P.on("sync", lambda e: e.dma_start(out=maxa_sb[0:1, :], in_=mga), inc=ld, k=16)
    BB_SC = emit_scale_prep(P, maxa_sb, srow_sb, scr2, scol_sb, icol_sb,
                            32767.0, [(ld, 16)], vv, bb)
    P.on("vector", lambda e: e.memset(run_sb[:], 0.0), inc=vv)

    NS = 4
    def front(g):
        sl = g % NS
        w = [(pp, g - NS + 1)] if g >= NS else []
        P.on("sync", lambda e, g=g, sl=sl: e.dma_start(
            out=gt_sb[:, sl, :],
            in_=gate_i.rearrange("(a p) t -> p a t", p=128)[:, g, :]),
            wait=w, inc=gld[sl], k=16)
        P.on("scalar", lambda e, g=g, sl=sl: e.activation(
            out=gq_sb[:, sl, :], in_=gt_sb[:, sl, :], func=AF.Copy, scale=icol_sb[:],
            bias=MAGIC),
            wait=[(gld[sl], 16 * (g // NS + 1)), (bb, BB_SC)]
                 + ([(pp, g - NS + 1)] if g >= NS else []), inc=ss)
        SSQ = ss.n
        P.on("vector", lambda e, sl=sl: e.tensor_scalar(
            out=gq_sb[:, sl, :], in0=gq_sb[:, sl, :], scalar1=MAGIC, scalar2=scol_sb[:],
            op0=ALU.subtract, op1=ALU.mult), wait=[(ss, SSQ)], inc=vv)
        VVQ = vv.n
        P.on("scalar", lambda e, sl=sl: e.activation(out=sg_sb[:, sl, :],
                                                     in_=gq_sb[:, sl, :],
                                                     func=AF.Sigmoid),
             wait=[(vv, VVQ)], inc=ss)
        SSS = ss.n
        P.on("gpsimd", lambda e, sl=sl: e.tensor_scalar(
            out=sg_sb[:, sl, :], in0=sg_sb[:, sl, :], scalar1=65536.0,
            scalar2=MAGIC, op0=ALU.mult, op1=ALU.add), wait=[(ss, SSS)], inc=qq)

    def back(g):
        sl = g % NS
        QQS = g + 1
        def fg2(e, g=g, sl=sl):
            e.tensor_scalar(out=sg_sb[:, sl, :], in0=sg_sb[:, sl, :], scalar1=MAGIC,
                            scalar2=65535.0, op0=ALU.subtract, op1=ALU.min)
            e.scalar_tensor_tensor(out=g2_sb[:, sl, :], in0=sg_sb[:, sl, :],
                                   scalar=1.0 / 65536.0, in1=gq_sb[:, sl, :],
                                   op0=ALU.mult, op1=ALU.mult)
            e.tensor_reduce(out=run_sb[:, 1:2], in_=g2_sb[:, sl, :], axis=AX,
                            op=ALU.max, apply_absolute_value=True)
            return e.tensor_max(out=run_sb[:, 0:1], in0=run_sb[:, 0:1],
                                in1=run_sb[:, 1:2])
        wg2 = [(qq, QQS)]
        if g >= NS:
            wg2.append((xsl[sl], 16 * ((g - NS) // NS + 1)))
        P.on("vector", fg2, wait=wg2, inc=pp)
        P.on("sync", lambda e, g=g, sl=sl: e.dma_start(
            out=g2_o.rearrange("(a p) t -> p a t", p=128)[:, g, :],
            in_=g2_sb[:, sl, :]), wait=[(pp, g + 1)], inc=xsl[sl], k=16)

    for g in range(ICH):
        front(g)
        if g >= 1:
            back(g - 1)
    back(ICH - 1)

    emit_absmax_export(P, run_sb[:, 0:1], mrow_sb, colscr, mg2_o,
                       [(pp, ICH)], bb, vv, xx)
    P.on("sync", lambda e: None,
         wait=[(xx, xx.n)] + [(x, x.n) for x in xsl])
    P.emit()
    return nc



def build_l5():
    nc = bass.Bass("TRN2", target_bir_lowering=False)
    g2_i = nc.dram_tensor("g2_i", [NI, TC], F32, kind="ExternalInput").ap()
    up_i = nc.dram_tensor("up_i", [NI, TC], F32, kind="ExternalInput").ap()
    mg2a = nc.dram_tensor("mg2a", [1, 8], F32, kind="ExternalInput").ap()
    mua = nc.dram_tensor("mua", [1, 8], F32, kind="ExternalInput").ap()
    op_o = nc.dram_tensor("op_o", [NI, TC], F32, kind="ExternalOutput").ap()
    mo_o = nc.dram_tensor("mo_o", [1, 1], F32, kind="ExternalOutput").ap()
    scr2 = nc.dram_tensor("scr2", [2], F32, kind="Internal").ap()
    scr2b = nc.dram_tensor("scr2b", [2], F32, kind="Internal").ap()
    colscr = nc.dram_tensor("colscr", [128], F32, kind="Internal").ap()

    g2t_sb = nc.alloc_sbuf_tensor("g2t_sb", [128, 3, TC], F32)
    upt_sb = nc.alloc_sbuf_tensor("upt_sb", [128, 3, TC], F32)
    q1_sb = nc.alloc_sbuf_tensor("q1_sb", [128, 3, TC], F32)
    q2_sb = nc.alloc_sbuf_tensor("q2_sb", [128, 3, TC], F32)
    op_sb = nc.alloc_sbuf_tensor("op_sb", [128, 3, TC], F32)
    maxa_sb = nc.alloc_sbuf_tensor("maxa_sb", [1, 2, 8], F32)
    srow_sb = nc.alloc_sbuf_tensor("srow_sb", [1, 4], F32)
    srow2_sb = nc.alloc_sbuf_tensor("srow2_sb", [1, 4], F32)
    scol_sb = nc.alloc_sbuf_tensor("scol_sb", [128, 1], F32)
    icol_sb = nc.alloc_sbuf_tensor("icol_sb", [128, 1], F32)
    scol2_sb = nc.alloc_sbuf_tensor("scol2_sb", [128, 1], F32)
    icol2_sb = nc.alloc_sbuf_tensor("icol2_sb", [128, 1], F32)
    run_sb = nc.alloc_sbuf_tensor("run_sb", [128, 2], F32)
    mrow_sb = nc.alloc_sbuf_tensor("mrow_sb", [1, 130], F32)

    P = Prog(nc)
    ld = P.sem("ld")
    gld = [P.sem(f"gld{i}") for i in range(3)]
    uld = [P.sem(f"uld{i}") for i in range(3)]
    xsl = [P.sem(f"xsl{i}") for i in range(3)]
    vv = P.sem("vv")
    ss = P.sem("ss")
    pp = P.sem("pp")
    bb = P.sem("bb")
    xx = P.sem("xx")

    P.on("sync", lambda e: e.dma_start(out=maxa_sb[:, 0, :], in_=mg2a), inc=ld, k=16)
    P.on("sync", lambda e: e.dma_start(out=maxa_sb[:, 1, :], in_=mua), inc=ld, k=16)
    B1 = emit_scale_prep(P, maxa_sb[:, 0, :], srow_sb, scr2, scol_sb, icol_sb,
                         32767.0, [(ld, 32)], vv, bb)
    B2 = emit_scale_prep(P, maxa_sb[:, 1, :], srow2_sb, scr2b, scol2_sb, icol2_sb,
                         32767.0, [(ld, 32)], vv, bb)
    P.on("vector", lambda e: e.memset(run_sb[:], 0.0), inc=vv)

    for g in range(ICH):
        sl = g % 3
        w = [(pp, g - 2)] if g >= 3 else []
        P.on("sync", lambda e, g=g, sl=sl: e.dma_start(
            out=g2t_sb[:, sl, :],
            in_=g2_i.rearrange("(a p) t -> p a t", p=128)[:, g, :]),
            wait=w, inc=gld[sl], k=16)
        P.on("scalar", lambda e, g=g, sl=sl: e.dma_start(
            out=upt_sb[:, sl, :],
            in_=up_i.rearrange("(a p) t -> p a t", p=128)[:, g, :]),
            wait=w, inc=uld[sl], k=16)
        P.on("scalar", lambda e, g=g, sl=sl: e.activation(
            out=q1_sb[:, sl, :], in_=g2t_sb[:, sl, :], func=AF.Copy, scale=icol_sb[:],
            bias=MAGIC),
            wait=[(gld[sl], 16 * (g // 3 + 1)), (bb, B2)]
                 + ([(pp, g - 2)] if g >= 3 else []), inc=ss)
        P.on("scalar", lambda e, g=g, sl=sl: e.activation(
            out=q2_sb[:, sl, :], in_=upt_sb[:, sl, :], func=AF.Copy, scale=icol2_sb[:],
            bias=MAGIC), wait=[(uld[sl], 16 * (g // 3 + 1))], inc=ss)
        S2 = ss.n
        def fq(e, sl=sl):
            e.tensor_scalar(out=q1_sb[:, sl, :], in0=q1_sb[:, sl, :], scalar1=MAGIC,
                            scalar2=scol_sb[:], op0=ALU.subtract, op1=ALU.mult)
            return e.tensor_scalar(out=q2_sb[:, sl, :], in0=q2_sb[:, sl, :],
                                   scalar1=MAGIC, scalar2=scol2_sb[:],
                                   op0=ALU.subtract, op1=ALU.mult)
        P.on("vector", fq, wait=[(ss, S2)], inc=vv)
        VQ = vv.n
        def fop(e, g=g, sl=sl):
            e.tensor_mul(out=op_sb[:, sl, :], in0=q1_sb[:, sl, :], in1=q2_sb[:, sl, :])
            e.tensor_reduce(out=run_sb[:, 1:2], in_=op_sb[:, sl, :], axis=AX,
                            op=ALU.max, apply_absolute_value=True)
            return e.tensor_max(out=run_sb[:, 0:1], in0=run_sb[:, 0:1],
                                in1=run_sb[:, 1:2])
        wo = [(vv, VQ)]
        if g >= 3:
            wo.append((xsl[sl], 16 * ((g - 3) // 3 + 1)))
        P.on("vector", fop, wait=wo, inc=pp)
        P.on("gpsimd", lambda e, g=g, sl=sl: e.dma_start(
            out=op_o.rearrange("(a p) t -> p a t", p=128)[:, g, :],
            in_=op_sb[:, sl, :]), wait=[(pp, g + 1)], inc=xsl[sl], k=16)

    emit_absmax_export(P, run_sb[:, 0:1], mrow_sb, colscr, mo_o,
                       [(pp, ICH)], bb, vv, xx)
    P.on("sync", lambda e: None,
         wait=[(xx, xx.n)] + [(x, x.n) for x in xsl])
    P.emit()
    return nc



def build_l6():
    nc = bass.Bass("TRN2", target_bir_lowering=False)
    op_i = nc.dram_tensor("op_i", [NI, TC], F32, kind="ExternalInput").ap()
    moa = nc.dram_tensor("moa", [1, 8], F32, kind="ExternalInput").ap()
    x2_i = nc.dram_tensor("x2_i", [C, TC], F32, kind="ExternalInput").ap()
    wdT_d = nc.dram_tensor("wdT_d", [NI, C], BF16, kind="ExternalInput").ap()
    mod_i = nc.dram_tensor("mod_i", [128, 48], F32, kind="ExternalInput").ap()
    out_o = nc.dram_tensor("out_o", [C, TC], F32, kind="ExternalOutput").ap()
    scr2 = nc.dram_tensor("scr2", [2], F32, kind="Internal").ap()

    wd_sb = nc.alloc_sbuf_tensor("wd_sb", [128, ICH, C], BF16)
    ot_sb = nc.alloc_sbuf_tensor("ot_sb", [128, 4, 512], F32)
    ob_sb = nc.alloc_sbuf_tensor("ob_sb", [128, 4, 512], BF16)
    oq_sb = nc.alloc_sbuf_tensor("oq_sb", [128, 512], F32)
    x2t_sb = nc.alloc_sbuf_tensor("x2t_sb", [128, 2, CCH, 512], F32)
    os_sb = nc.alloc_sbuf_tensor("os_sb", [128, 2, CCH, 512], F32)
    mod_sb = nc.alloc_sbuf_tensor("mod_sb", [128, 48], F32)
    gb_sb = nc.alloc_sbuf_tensor("gb_sb", [128, 2, CCH], F32)
    maxa_sb = nc.alloc_sbuf_tensor("maxa_sb", [1, 8], F32)
    srow_sb = nc.alloc_sbuf_tensor("srow_sb", [1, 4], F32)
    scol_sb = nc.alloc_sbuf_tensor("scol_sb", [128, 1], F32)
    icol_sb = nc.alloc_sbuf_tensor("icol_sb", [128, 1], F32)

    ps = [nc.alloc_psum_tensor(f"ps{i}", [128, 512], F32) for i in range(CCH)]

    P = Prog(nc)
    ld = P.sem("ld")
    wld = P.sem("wld")
    old_ = [P.sem(f"old{i}") for i in range(4)]
    xld = [P.sem(f"xld{i}") for i in range(2)]
    xos = [P.sem(f"xos{i}") for i in range(2)]
    vv = P.sem("vv")
    ss = P.sem("ss")
    qrd = P.sem("qrd")
    mm = P.sem("mm")
    evc = P.sem("evc")
    bb = P.sem("bb")
    xx = P.sem("xx")

    P.on("sync", lambda e: e.dma_start(out=maxa_sb[0:1, :], in_=moa), inc=ld, k=16)
    P.on("sync", lambda e: e.dma_start(out=mod_sb[:], in_=mod_i), inc=ld, k=16)
    P.on("sync", lambda e: e.dma_start(
        out=wd_sb[:], in_=wdT_d.rearrange("(a p) o -> p a o", p=128)), inc=wld, k=16)
    BB_SC = emit_scale_prep(P, maxa_sb, srow_sb, scr2, scol_sb, icol_sb,
                            32767.0, [(ld, 16)], vv, bb)
    P.on("scalar", lambda e: e.activation(out=gb_sb[:, 0], in_=mod_sb[:, 40:48],
                                          func=AF.Abs), wait=[(ld, 32)], inc=ss)
    P.on("scalar", lambda e: e.sign(out=gb_sb[:, 1], in_=mod_sb[:, 40:48]),
         wait=[], inc=ss)
    SS_GB = ss.n
    def fgb(e):
        e.tensor_scalar(out=gb_sb[:, 0], in0=gb_sb[:, 0], scalar1=0.25, scalar2=None,
                        op0=ALU.max)
        return e.tensor_mul(out=gb_sb[:, 0], in0=gb_sb[:, 1], in1=gb_sb[:, 0])
    P.on("vector", fgb, wait=[(ss, SS_GB)], inc=vv)
    VV_GB = vv.n

    for tt in range(NTT):
        w = [(evc, (tt - 1) * CCH)] if tt >= 2 else []
        P.on("sync", lambda e, tt=tt: e.dma_start(
            out=x2t_sb[:, tt % 2], in_=fm(x2_i)[:, :, tt * 512:(tt + 1) * 512]),
            wait=w, inc=xld[tt % 2], k=16)
        for ic in range(ICH):
            g = tt * ICH + ic
            w = [(qrd, g - 3)] if g >= 4 else []
            P.on("scalar", lambda e, ic=ic, tt=tt, g=g: e.dma_start(
                out=ot_sb[:, g % 4, :],
                in_=op_i.rearrange("(a p) t -> p a t", p=128)[:, ic,
                                                              tt * 512:(tt + 1) * 512]),
                wait=w, inc=old_[g % 4], k=16)
            def fq(e, g=g):
                e.tensor_scalar(out=oq_sb[:], in0=ot_sb[:, g % 4, :],
                                scalar1=icol_sb[:], scalar2=MAGIC,
                                op0=ALU.mult, op1=ALU.add)
                return e.tensor_scalar(out=ob_sb[:, g % 4, :], in0=oq_sb[:],
                                       scalar1=MAGIC, scalar2=scol_sb[:],
                                       op0=ALU.subtract, op1=ALU.mult)
            P.on("vector", fq,
                 wait=[(old_[g % 4], 16 * (g // 4 + 1)), (bb, BB_SC)]
                      + ([(mm, (g - 3) * CCH)] if g >= 4 else []), inc=qrd)
            def fmm(e, ic=ic, tt=tt, g=g):
                ins = None
                for oc2 in range(CCH):
                    ins = e.matmul(out=ps[oc2][:],
                                   lhsT=wd_sb[:, ic, oc2 * 128:(oc2 + 1) * 128],
                                   rhs=ob_sb[:, g % 4, :],
                                   start=(ic == 0), stop=(ic == ICH - 1))
                return ins
            wm_ = [(qrd, g + 1), (wld, 16)]
            if ic == 0 and tt > 0:
                wm_.append((evc, tt * CCH))
            P.on("tensor", fmm, wait=wm_, inc=mm, k=CCH)
        for oc2 in range(CCH):
            P.on("vector", lambda e, oc2=oc2, tt=tt: e.scalar_tensor_tensor(
                out=os_sb[:, tt % 2, oc2, :], in0=ps[oc2][:],
                scalar=gb_sb[:, 0, oc2:oc2 + 1], in1=x2t_sb[:, tt % 2, oc2, :],
                op0=ALU.mult, op1=ALU.add),
                wait=[(mm, (tt + 1) * ICH * CCH), (xld[tt % 2], 16 * (tt // 2 + 1)), (vv, VV_GB)]
                     + ([(xos[tt % 2], 16 * (tt // 2))] if tt >= 2 else []),
                inc=evc)
        P.on("gpsimd", lambda e, tt=tt: e.dma_start(
            out=fm(out_o)[:, :, tt * 512:(tt + 1) * 512], in_=os_sb[:, tt % 2]),
            wait=[(evc, (tt + 1) * CCH)], inc=xos[tt % 2], k=16)

    P.on("sync", lambda e: None,
         wait=[(xos[0], xos[0].n), (xos[1], xos[1].n)])
    P.emit()
    return nc



def _t(a):
    return np.ascontiguousarray(np.asarray(a).T)


def _cols(v):
    return np.ascontiguousarray(np.asarray(v).reshape(CCH, 128).T)


def la_maps(inp):
    x = np.asarray(inp["L0_attn_x"])[0]
    xT = _t(x)
    eye = np.eye(64, dtype=np.float32)
    maps = []
    for r in range(NCORES):
        lo, hi = r * TC, (r + 1) * TC
        xs = np.zeros((C, WH), np.float32)
        xs[:, 1:TC + 1] = xT[:, lo:hi]
        if r > 0:
            xs[:, 0] = xT[:, lo - 1]
        if r < NCORES - 1:
            xs[:, TC + 1] = xT[:, hi]
        maskLR = np.zeros((128, 2), np.float32)
        maskLR[:, 0] = 1.0 if r > 0 else 0.0
        maskLR[:, 1] = 1.0 if r < NCORES - 1 else 0.0
        maps.append({
            "xT": xs,
            "mixk": _cols(inp["L0_attn_mix_k"]),
            "mixv": _cols(inp["L0_attn_mix_v"]),
            "mixr": _cols(inp["L0_attn_mix_r"]),
            "cvec": _cols(np.asarray(inp["L0_c"])[0]),
            "wmod": np.asarray(inp["L0_wmod"]),
            "maskLR": maskLR,
            "eye": eye,
            "wup_sh": np.asarray(inp["L0_ffn_w_up"])[r * 384:(r + 1) * 384],
            "wgate_sh": np.asarray(inp["L0_ffn_w_gate"])[r * 384:(r + 1) * 384],
            "wdown_sh": np.asarray(inp["L0_ffn_w_down"])[r * 128:(r + 1) * 128],
        })
    return maps


def lb_maps(inp, ra):
    wk, wv, wr = _t(inp["L0_attn_Wk"]), _t(inp["L0_attn_Wv"]), _t(inp["L0_attn_Wr"])
    sd = _cols(inp["L0_attn_spatial_decay"])
    return [{
        "h_i": ra[r]["h_o"], "u_i": ra[r]["u_o"],
        "wkT": wk, "wvT": wv, "wrT": wr, "sdec": sd,
    } for r in range(NCORES)]


def l2_maps(inp, ra, rb):
    af_all = np.stack([rb[j]["afin_o"] for j in range(NCORES)], axis=2)
    bf_all = np.stack([rb[j]["bfin_o"] for j in range(NCORES)], axis=2)
    af_all = np.ascontiguousarray(af_all, np.float32)
    bf_all = np.ascontiguousarray(bf_all, np.float32)
    x = np.asarray(inp["L0_attn_x"])[0]
    xT = _t(x)
    wo = _t(inp["L0_attn_Wo"])
    sd = _cols(inp["L0_attn_spatial_decay"])
    sf = _cols(inp["L0_attn_spatial_first"])
    maps = []
    for r in range(NCORES):
        lo, hi = r * TC, (r + 1) * TC
        xs = np.zeros((C, WH), np.float32)
        xs[:, 1:TC + 1] = xT[:, lo:hi]
        pwv = np.zeros((128, 8), np.float32)
        pmk = np.zeros((128, 8), np.float32)
        for j in range(NCORES):
            if j < r:
                pwv[:, j] = (r - 1 - j) * (float(TC) / T)
                pmk[:, j] = 1.0
        maps.append({
            "xT": xs,
            "ek_i": rb[r]["ek_o"], "ekv_i": rb[r]["ekv_o"], "sr_i": rb[r]["sr_o"],
            "af_all": af_all, "bf_all": bf_all,
            "pw": pwv, "pmask": pmk,
            "woT": wo, "mod_i": ra[r]["mod_o"],
            "sdec": sd, "sfirst": sf,
        })
    return maps




def l3_maps(inp, ra, r2):
    wup_d = np.concatenate([ra[j]["wupd_o"] for j in range(NCORES)], axis=0)
    wgate_d = np.concatenate([ra[j]["wgated_o"] for j in range(NCORES)], axis=0)
    wupT = np.ascontiguousarray(wup_d.T)
    wgateT = np.ascontiguousarray(wgate_d.T)
    mh2a = np.ascontiguousarray(
        np.concatenate([r2[j]["mh2_o"] for j in range(NCORES)], axis=1))
    return [{
        "h2_i": r2[r]["h2_o"], "mh2a": mh2a,
        "wupT_d": wupT, "wgateT_d": wgateT,
    } for r in range(NCORES)]


def l4_maps(r3):
    mga = np.ascontiguousarray(
        np.concatenate([r3[j]["mg_o"] for j in range(NCORES)], axis=1))
    return [{"gate_i": r3[r]["gate_o"], "mga": mga} for r in range(NCORES)]


def l5_maps(r3, r4):
    mg2a = np.ascontiguousarray(
        np.concatenate([r4[j]["mg2_o"] for j in range(NCORES)], axis=1))
    mua = np.ascontiguousarray(
        np.concatenate([r3[j]["mu_o"] for j in range(NCORES)], axis=1))
    return [{"g2_i": r4[r]["g2_o"], "up_i": r3[r]["up_o"],
             "mg2a": mg2a, "mua": mua} for r in range(NCORES)]


def l6_maps(inp, ra, r2, r5):
    wdown_d = np.concatenate([ra[j]["wdownd_o"] for j in range(NCORES)], axis=0)
    wdT = np.ascontiguousarray(wdown_d.T)
    moa = np.ascontiguousarray(
        np.concatenate([r5[j]["mo_o"] for j in range(NCORES)], axis=1))
    return [{"op_i": r5[r]["op_o"], "moa": moa, "x2_i": r2[r]["x2_o"],
             "wdT_d": wdT, "mod_i": ra[r]["mod_o"]} for r in range(NCORES)]


def run_all(inp):
    cores = list(range(NCORES))
    ra = run_bass_kernel_spmd(build_la(), la_maps(inp), core_ids=cores).results
    rb = run_bass_kernel_spmd(build_lb(), lb_maps(inp, ra), core_ids=cores).results
    r2 = run_bass_kernel_spmd(build_l2(), l2_maps(inp, ra, rb), core_ids=cores).results
    r3 = run_bass_kernel_spmd(build_l3(), l3_maps(inp, ra, r2), core_ids=cores).results
    r4 = run_bass_kernel_spmd(build_l4(), l4_maps(r3), core_ids=cores).results
    r5 = run_bass_kernel_spmd(build_l5(), l5_maps(r3, r4), core_ids=cores).results
    r6 = run_bass_kernel_spmd(build_l6(), l6_maps(inp, ra, r2, r5), core_ids=cores).results
    return ra, rb, r2, r3, r4, r5, r6


def kernel(**inp):
    ra, rb, r2, r3, r4, r5, r6 = run_all(inp)
    out = np.concatenate(
        [np.asarray(r6[r]["out_o"]).T for r in range(NCORES)], axis=0)
    return np.ascontiguousarray(out).reshape(1, T, C)


# version 04
# speedup vs baseline: 1.5507x; 1.0827x over previous
import numpy as np
import ml_dtypes
import concourse.bass as bass
import concourse.mybir as mybir
from concourse.bass_utils import run_bass_kernel_spmd

F32 = mybir.dt.float32
BF16 = mybir.dt.bfloat16
AF = mybir.ActivationFunctionType
ALU = mybir.AluOpType
AX = mybir.AxisListType.X

NCORES = 8
T = 16384
TC = T // NCORES
C = 1024
CCH = C // 128
NI = 3072
ICH = NI // 128
NTT = TC // 512
WH = TC + 2
MAGIC = float(np.float32(1.5 * 2 ** 23))
EPS = 1e-5
BF = ml_dtypes.bfloat16
ENGS = ("sync", "tensor", "vector", "scalar", "gpsimd")


class Sem:
    def __init__(self, nc, name):
        self.h = nc.alloc_semaphore(name)
        self.n = 0
        self.name = name


_CHAIN_SKIP = {"dma_start", "wait_ge", "wait_op", "sem_inc", "then_inc",
               "dma_start_transpose", "indirect_dma_start", "engine_nop",
               "value_load", "reg_mov", "nop"}


class Chained:

    def __init__(self, eng, sem):
        self._e = eng
        self._s = sem

    def __getattr__(self, k):
        a = getattr(self._e, k)
        if not callable(a) or k in _CHAIN_SKIP or k.startswith("_"):
            return a

        def wrap(*args, **kw):
            r = a(*args, **kw)
            if hasattr(r, "then_inc"):
                self._s.n += 1
                r.then_inc(self._s.h, 1)
                self._e.wait_ge(self._s.h, self._s.n)
            return r
        return wrap


class Prog:

    def __init__(self, nc):
        self.nc = nc
        self.steps = {e: [] for e in ENGS}
        self.chains = {e: Sem(nc, f"chain_{e}") for e in ("vector", "scalar", "gpsimd")}

    def sem(self, name):
        return Sem(self.nc, name)

    def on(self, eng, fn, wait=(), inc=None, k=1):
        for s, v in wait:
            assert v <= s.n, f"{eng}: wait {s.name}>={v} but only {s.n} reserved"
        if inc is not None:
            inc.n += k
        self.steps[eng].append((fn, tuple(wait), inc, k))

    def emit(self):
        def mk(ename):
            chain = self.chains.get(ename)

            def body(eng):
                ceng = Chained(eng, chain) if chain is not None else eng
                for fn, waits, inc, k in self.steps[ename]:
                    for s, v in waits:
                        eng.wait_ge(s.h, v)
                    ins = fn(ceng)
                    if inc is not None:
                        if chain is not None and k != 16:
                            eng.sem_inc(inc.h, k)
                        else:
                            assert ins is not None, ename
                            ins.then_inc(inc.h, k)
            return body

        with self.nc.Block() as block:
            block.sync(mk("sync"))
            block.tensor(mk("tensor"))
            block.vector(mk("vector"))
            block.scalar(mk("scalar"))
            block.gpsimd(mk("gpsimd"))


class Arena:

    def __init__(self, nc, name, nbytes):
        self.t = nc.alloc_sbuf_tensor(name, [128, nbytes], mybir.dt.uint8)
        self.off = 0
        self.nbytes = nbytes

    def carve(self, shape, dtype):
        esz = mybir.dt.size(dtype)
        n = int(np.prod(shape)) * esz
        off = (self.off + 511) // 512 * 512
        assert off + n <= self.nbytes, f"arena overflow {off}+{n} > {self.nbytes}"
        self.off = off + n
        ap = self.t.ap()[:, off:off + n].bitcast(dtype)
        if len(shape) > 1:
            dims = " ".join(f"d{i}" for i in range(len(shape)))
            kw = {f"d{i}": s for i, s in enumerate(shape)}
            ap = ap.rearrange(f"p ({dims}) -> p {dims}", **kw)
        return ap

    def reset(self, off=0):
        self.off = off


def exp_poly(eng, out_ap, x_ap, tmp_ap):
    eng.tensor_scalar(out=tmp_ap, in0=x_ap, scalar1=1.0 / 6.0, scalar2=0.5,
                      op0=ALU.mult, op1=ALU.add)
    eng.tensor_tensor(out=tmp_ap, in0=tmp_ap, in1=x_ap, op=ALU.mult)
    eng.tensor_scalar(out=tmp_ap, in0=tmp_ap, scalar1=1.0, scalar2=None, op0=ALU.add)
    eng.tensor_tensor(out=tmp_ap, in0=tmp_ap, in1=x_ap, op=ALU.mult)
    return eng.tensor_scalar(out=out_ap, in0=tmp_ap, scalar1=1.0, scalar2=None,
                             op0=ALU.add)


def emit_absmax_export(P, col_ap, row_sb, scr_dram, out_dram, wait, bb, vv, xx):
    P.on("gpsimd", lambda e: e.dma_start(out=scr_dram, in_=col_ap),
         wait=wait, inc=bb, k=16)
    B1 = bb.n
    P.on("gpsimd", lambda e: e.dma_start(out=row_sb[0:1, 0:128],
                                         in_=scr_dram.unsqueeze(0)),
         wait=[(bb, B1)], inc=bb, k=16)
    B2 = bb.n
    P.on("vector", lambda e: e.reduce_max(out=row_sb[0:1, 128:129],
                                          in_=row_sb[0:1, 0:128], axis=AX),
         wait=[(bb, B2)], inc=vv)
    V1 = vv.n
    P.on("gpsimd", lambda e: e.dma_start(out=out_dram, in_=row_sb[0:1, 128:129]),
         wait=[(vv, V1)], inc=xx, k=16)


def cvec_ap(dram_ap):
    return dram_ap.rearrange("(a p) -> p a", p=128)


def fm(dram_ap):
    return dram_ap.rearrange("(a p) t -> p a t", p=128)



def build_la():
    nc = bass.Bass("TRN2", target_bir_lowering=False)
    xT = nc.dram_tensor("xT", [C, WH], F32, kind="ExternalInput").ap()
    mixk = nc.dram_tensor("mixk", [128, CCH], F32, kind="ExternalInput").ap()
    mixv = nc.dram_tensor("mixv", [128, CCH], F32, kind="ExternalInput").ap()
    mixr = nc.dram_tensor("mixr", [128, CCH], F32, kind="ExternalInput").ap()
    cvec = nc.dram_tensor("cvec", [128, CCH], F32, kind="ExternalInput").ap()
    wmod = nc.dram_tensor("wmod", [C, 6 * C], F32, kind="ExternalInput").ap()
    maskLR = nc.dram_tensor("maskLR", [128, 2], F32, kind="ExternalInput").ap()
    eye = nc.dram_tensor("eye", [64, 64], F32, kind="ExternalInput").ap()
    wup_sh = nc.dram_tensor("wup_sh", [NI // 8, C], F32, kind="ExternalInput").ap()
    wgate_sh = nc.dram_tensor("wgate_sh", [NI // 8, C], F32, kind="ExternalInput").ap()
    wdown_sh = nc.dram_tensor("wdown_sh", [C // 8, NI], F32, kind="ExternalInput").ap()

    h_o = nc.dram_tensor("h_o", [C, WH], BF16, kind="ExternalOutput").ap()
    u_o = nc.dram_tensor("u_o", [3, C // 2, TC], BF16, kind="ExternalOutput").ap()
    mod_o = nc.dram_tensor("mod_o", [128, 48], F32, kind="ExternalOutput").ap()
    wupd_o = nc.dram_tensor("wupd_o", [NI // 8, C], BF16, kind="ExternalOutput").ap()
    wgated_o = nc.dram_tensor("wgated_o", [NI // 8, C], BF16, kind="ExternalOutput").ap()
    wdownd_o = nc.dram_tensor("wdownd_o", [C // 8, NI], BF16, kind="ExternalOutput").ap()

    modscr = nc.dram_tensor("modscr", [48, 128], F32, kind="Internal").ap()
    rowscr = nc.dram_tensor("rowscr", [2, WH], F32, kind="Internal").ap()

    h_sb = nc.alloc_sbuf_tensor("h_sb", [128, CCH, WH], BF16)
    mask_sb = nc.alloc_sbuf_tensor("mask_sb", [128, 2], BF16)
    maskf_sb = nc.alloc_sbuf_tensor("maskf_sb", [128, 2], F32)
    mix_sb = nc.alloc_sbuf_tensor("mix_sb", [128, 3, CCH], F32)
    scf_sb = nc.alloc_sbuf_tensor("scf_sb", [128, CCH], F32)
    scf2_sb = nc.alloc_sbuf_tensor("scf2_sb", [128, CCH, 1], F32)
    tmp8_sb = nc.alloc_sbuf_tensor("tmp8_sb", [128, CCH], F32)
    sc_sb = nc.alloc_sbuf_tensor("sc_sb", [128, CCH, 1], BF16)
    m48_sb = nc.alloc_sbuf_tensor("m48_sb", [48, 128], F32)
    mod_sb = nc.alloc_sbuf_tensor("mod_sb", [128, 48], F32)
    A_sb = nc.alloc_sbuf_tensor("A_sb", [128, CCH], F32)
    eye_sb = nc.alloc_sbuf_tensor("eye_sb", [64, 64], F32)
    evrow_sb = nc.alloc_sbuf_tensor("evrow_sb", [1, 512], F32)
    xb_sb = nc.alloc_sbuf_tensor("xb_sb", [128, CCH, 512], BF16)
    x2b_sb = nc.alloc_sbuf_tensor("x2b_sb", [128, CCH, 512], BF16)
    stat_sb = nc.alloc_sbuf_tensor("stat_sb", [1, 2, WH], F32)
    brow_sb = nc.alloc_sbuf_tensor("brow_sb", [128, 2, WH], F32)
    ht1_sb = nc.alloc_sbuf_tensor("ht1_sb", [128, WH], F32)
    ht2_sb = nc.alloc_sbuf_tensor("ht2_sb", [128, WH], F32)
    wm_sb = nc.alloc_sbuf_tensor("wm_sb", [128, 2, 2048], F32)
    wmb_sb = nc.alloc_sbuf_tensor("wmb_sb", [128, 2, 2048], BF16)
    dqw_sb = nc.alloc_sbuf_tensor("dqw_sb", [128, 2, 1024], F32)
    dqo_sb = nc.alloc_sbuf_tensor("dqo_sb", [128, 2, 1024], BF16)
    dqi_sb = nc.alloc_sbuf_tensor("dqi_sb", [128, 32], F32)
    dqs_sb = nc.alloc_sbuf_tensor("dqs_sb", [128, 32], F32)

    Z1 = Arena(nc, "zone1", 68 * 1024)
    x_sb = Z1.carve([CCH, WH], F32)
    Z1.reset()
    u_sb = Z1.carve([3, 4, TC], BF16)
    dscr_sb = Z1.carve([TC], BF16)

    ps_mod = [nc.alloc_psum_tensor(f"ps_mod{i}", [1, 512], F32) for i in range(2)]
    ps_modB = [nc.alloc_psum_tensor(f"ps_modB{i}", [1, 256], F32) for i in range(2)]
    ps_sum = nc.alloc_psum_tensor("ps_sum", [1, 512], F32)
    ps_ssq = nc.alloc_psum_tensor("ps_ssq", [1, 512], F32)
    ps_tr = nc.alloc_psum_tensor("ps_tr", [64, 48], F32)

    P = Prog(nc)
    ld = P.sem("ld")
    wld = [P.sem(f"wld{i}") for i in range(2)]
    cst = P.sem("cst")
    mmod = P.sem("mmod")
    vmd = P.sem("vmd")
    bb = P.sem("bb")
    ss = P.sem("ss")
    vv = P.sem("vv")
    rr = P.sem("rr")
    hh = P.sem("hh")
    mst = P.sem("mst")
    sst = P.sem("sst")
    hrd = P.sem("hrd")
    urd = P.sem("urd")
    xx = P.sem("xx")
    dql = [P.sem(f"dql{i}") for i in range(2)]
    xdq = P.sem("xdq")
    dqp = P.sem("dqp")
    dqv = P.sem("dqv")

    P.on("sync", lambda e: e.dma_start(out=x_sb, in_=fm(xT)), inc=ld, k=16)
    P.on("sync", lambda e: e.dma_start(out=maskf_sb[:], in_=maskLR), inc=ld, k=16)
    for i, t in enumerate((mixk, mixv, mixr)):
        P.on("sync", lambda e, i=i, t=t: e.dma_start(out=mix_sb[:, i, :], in_=t),
             inc=ld, k=16)
    P.on("sync", lambda e: e.dma_start(out=scf_sb[:], in_=cvec), inc=ld, k=16)
    P.on("sync", lambda e: e.dma_start(out=eye_sb[:], in_=eye), inc=ld, k=16)
    LD_X = LD_MASK = LD_MIX = LD_C = LD_EYE = 112

    def fsilu(e):
        coef = [1.0 / 3628800, -1.0 / 362880, 1.0 / 40320, -1.0 / 5040,
                1.0 / 720, -1.0 / 120, 1.0 / 24, -1.0 / 6, 0.5, -1.0, 1.0]
        e.tensor_scalar(out=tmp8_sb[:], in0=scf_sb[:], scalar1=coef[0],
                        scalar2=coef[1], op0=ALU.mult, op1=ALU.add)
        for k in range(2, len(coef)):
            e.tensor_tensor(out=tmp8_sb[:], in0=tmp8_sb[:], in1=scf_sb[:], op=ALU.mult)
            e.tensor_scalar(out=tmp8_sb[:], in0=tmp8_sb[:], scalar1=coef[k],
                            scalar2=None, op0=ALU.add)
        e.tensor_scalar(out=tmp8_sb[:], in0=tmp8_sb[:], scalar1=1.0, scalar2=None,
                        op0=ALU.add)
        e.reciprocal(out=tmp8_sb[:], in_=tmp8_sb[:])
        e.tensor_mul(out=scf2_sb[:, :, 0], in0=scf_sb[:], in1=tmp8_sb[:])
        return e.tensor_copy(out=sc_sb[:, :, 0], in_=scf2_sb[:, :, 0])
    P.on("vector", fsilu, wait=[(ld, LD_C)], inc=ss)
    SS_SILU = ss.n

    for it in range(96):
        j, hc = divmod(it, CCH)
        I, q = divmod(it, 4)
        if q == 0:
            w = [(mmod, 4 * (I - 1))] if I >= 2 else []
            P.on("sync", lambda e, j=j, hc=hc, I=I: e.dma_start(
                out=wm_sb.ap()[:, I % 2].rearrange("p (a c) -> p a c", c=512),
                in_=wmod.rearrange("(a p) j -> p a j", p=128)[:, hc:hc + 4,
                                                             j * 512:(j + 1) * 512]),
                wait=w, inc=wld[I % 2], k=16)
            wc = [(wld[I % 2], 16 * (I // 2 + 1))]
            if I >= 2:
                wc.append((mmod, 4 * (I - 1)))
            P.on("scalar", lambda e, I=I: e.activation(
                out=wmb_sb[:, I % 2, :], in_=wm_sb[:, I % 2, :], func=AF.Copy),
                wait=wc, inc=cst)
        wm = [(cst, I + 1), (ss, SS_SILU)]
        if hc == 0 and j >= 2:
            wm.append((vmd, j - 1))
        if j in (4, 5, 10, 11):
            def fmodf(e, j=j, hc=hc, I=I, q=q):
                e.matmul(out=ps_mod[j % 2][0:1, 0:256], lhsT=scf2_sb[:, hc, :],
                         rhs=wm_sb[:, I % 2, q * 512:q * 512 + 256],
                         start=(hc == 0), stop=(hc == CCH - 1))
                return e.matmul(out=ps_modB[j % 2][0:1, :], lhsT=scf2_sb[:, hc, :],
                                rhs=wm_sb[:, I % 2, q * 512 + 256:q * 512 + 512],
                                start=(hc == 0), stop=(hc == CCH - 1))
            P.on("tensor", fmodf, wait=wm, inc=mmod)
        else:
            P.on("tensor", lambda e, j=j, hc=hc, I=I, q=q: e.matmul(
                out=ps_mod[j % 2][:], lhsT=sc_sb[:, hc, :],
                rhs=wmb_sb[:, I % 2, q * 512:(q + 1) * 512],
                start=(hc == 0), stop=(hc == CCH - 1)), wait=wm, inc=mmod)
        if hc == CCH - 1:
            wv = [(mmod, 8 * (j + 1))] + ([(bb, 16 * j)] if j > 0 else [])
            if j in (4, 5, 10, 11):
                def fev(e, j=j):
                    e.tensor_copy(out=evrow_sb[0:1, 0:256], in_=ps_mod[j % 2][0:1, 0:256])
                    return e.tensor_copy(out=evrow_sb[0:1, 256:512],
                                         in_=ps_modB[j % 2][0:1, :])
                P.on("vector", fev, wait=wv, inc=vmd)
            else:
                P.on("vector", lambda e, j=j: e.tensor_copy(out=evrow_sb[0:1, :],
                                                            in_=ps_mod[j % 2][:]),
                     wait=wv, inc=vmd)
            P.on("gpsimd", lambda e, j=j: e.dma_start(
                out=modscr.rearrange("a b -> (a b)").unsqueeze(0)[:, j * 512:(j + 1) * 512],
                in_=evrow_sb[0:1, :]), wait=[(vmd, j + 1)], inc=bb, k=16)
    BB_MOD = bb.n
    P.on("gpsimd", lambda e: e.dma_start(out=m48_sb[:], in_=modscr),
         wait=[(bb, BB_MOD)], inc=bb, k=16)
    BB_M48 = bb.n
    P.on("tensor", lambda e: e.transpose(out=ps_tr[:], in_=m48_sb[:, 0:64],
                                         identity=eye_sb[0:48, 0:48]),
         wait=[(bb, BB_M48), (ld, LD_EYE)], inc=mmod)
    P.on("vector", lambda e: e.tensor_copy(out=mod_sb[0:64, :], in_=ps_tr[0:64, :]),
         wait=[(mmod, 97)], inc=vmd)
    P.on("tensor", lambda e: e.transpose(out=ps_tr[:], in_=m48_sb[:, 64:128],
                                         identity=eye_sb[0:48, 0:48]),
         wait=[(vmd, 13)], inc=mmod)
    P.on("vector", lambda e: e.tensor_copy(out=mod_sb[64:128, :], in_=ps_tr[0:64, :]),
         wait=[(mmod, 98)], inc=vmd)
    P.on("vector", lambda e: e.tensor_scalar(out=A_sb[:], in0=mod_sb[:, 8:16],
                                             scalar1=1.0, scalar2=None, op0=ALU.add),
         wait=[(vmd, 14)], inc=vmd)
    VMD_DONE = vmd.n
    P.on("vector", lambda e: e.tensor_copy(out=mask_sb[:], in_=maskf_sb[:]),
         wait=[(ld, LD_MASK)], inc=vv)
    VV_MASK = vv.n

    onesb_sb = nc.alloc_sbuf_tensor("onesb_sb", [128, 1], BF16)
    eps_sb = nc.alloc_sbuf_tensor("eps_sb", [128, 1], F32)
    P.on("vector", lambda e: e.memset(onesb_sb[:], 1.0), inc=vv)
    P.on("vector", lambda e: e.memset(eps_sb[:], EPS), inc=vv)
    VV_ONES = vv.n
    tiles = [(st * 512, 512) for st in range(4)] + [(2048, 2)]
    VV_CAST = {}
    SS_SQ = {}
    for st, (t0, tw) in enumerate(tiles):
        wv = [(ld, LD_X)]
        if st > 0:
            wv += [(ss, SS_SQ[st - 1]), (mst, st)]
        P.on("vector", lambda e, t0=t0, tw=tw: e.tensor_copy(
            out=xb_sb[:, :, 0:tw], in_=x_sb[:, :, t0:t0 + tw]), wait=wv, inc=vv)
        VV_CAST[st] = vv.n
        ws = [(vv, VV_CAST[st])]
        if st > 0:
            ws.append((mst, st))
        P.on("scalar", lambda e, tw=tw: e.activation(
            out=x2b_sb[:, :, 0:tw], in_=xb_sb[:, :, 0:tw], func=AF.Square),
            wait=ws, inc=ss)
        SS_SQ[st] = ss.n
        def fstat(e, tw=tw):
            for a in range(CCH):
                e.matmul(out=ps_sum[0:1, 0:tw], lhsT=onesb_sb[:], rhs=xb_sb[:, a, 0:tw],
                         start=(a == 0), stop=(a == CCH - 1))
            ins = None
            for a in range(CCH):
                ins = e.matmul(out=ps_ssq[0:1, 0:tw], lhsT=onesb_sb[:],
                               rhs=x2b_sb[:, a, 0:tw], start=(a == 0), stop=(a == CCH - 1))
            return ins
        wt = [(vv, VV_CAST[st]), (ss, SS_SQ[st]), (vv, VV_ONES)]
        if st > 0:
            wt.append((sst, 2 * st))
        P.on("tensor", fstat, wait=wt, inc=mst)
        P.on("scalar", lambda e, t0=t0, tw=tw: e.activation(
            out=stat_sb[0:1, 0, t0:t0 + tw], in_=ps_sum[0:1, 0:tw], func=AF.Copy),
            wait=[(mst, st + 1)], inc=sst)
        P.on("scalar", lambda e, t0=t0, tw=tw: e.activation(
            out=stat_sb[0:1, 1, t0:t0 + tw], in_=ps_ssq[0:1, 0:tw], func=AF.Copy),
            wait=[], inc=sst)
    assert mst.n == 5 and sst.n == 10

    P.on("scalar", lambda e: e.activation(out=stat_sb[0:1, 0, :], in_=stat_sb[0:1, 0, :],
                                          func=AF.Copy, scale=1.0 / C),
         wait=[(sst, 10)], inc=ss)
    P.on("scalar", lambda e: e.activation(out=ht1_sb[0:1, :], in_=stat_sb[0:1, 0, :],
                                          func=AF.Square), wait=[], inc=ss)
    SS_MSQ = ss.n
    P.on("vector", lambda e: e.scalar_tensor_tensor(
        out=stat_sb[0:1, 1, :], in0=stat_sb[0:1, 1, :], scalar=1.0 / C,
        in1=ht1_sb[0:1, :], op0=ALU.mult, op1=ALU.subtract),
        wait=[(ss, SS_MSQ)], inc=rr)
    RR_VAR = rr.n
    P.on("scalar", lambda e: e.activation(out=stat_sb[0:1, 1, :], in_=stat_sb[0:1, 1, :],
                                          func=AF.Ln, bias=eps_sb[0:1, :]),
         wait=[(rr, RR_VAR), (vv, VV_ONES)], inc=ss)
    P.on("scalar", lambda e: e.activation(out=stat_sb[0:1, 1, :], in_=stat_sb[0:1, 1, :],
                                          func=AF.Exp, scale=-0.5), wait=[], inc=ss)
    SS_RSTD = ss.n
    P.on("gpsimd", lambda e: e.tensor_mul(out=ht1_sb[0:1, :], in0=stat_sb[0:1, 0, :],
                                          in1=stat_sb[0:1, 1, :]),
         wait=[(ss, SS_RSTD)], inc=rr)
    RR_MH = rr.n
    P.on("gpsimd", lambda e: e.dma_start(out=rowscr[0:1, :], in_=stat_sb[0:1, 1, :]),
         wait=[], inc=bb, k=16)
    P.on("gpsimd", lambda e: e.dma_start(out=rowscr[1:2, :], in_=ht1_sb[0:1, :]),
         wait=[], inc=bb, k=16)
    BB_W = bb.n
    P.on("gpsimd", lambda e: e.dma_start(
        out=brow_sb[:, 0, :], in_=rowscr[0:1, :].partition_broadcast(128).squeeze(1)),
        wait=[(bb, BB_W)], inc=bb, k=16)
    P.on("gpsimd", lambda e: e.dma_start(
        out=brow_sb[:, 1, :], in_=rowscr[1:2, :].partition_broadcast(128).squeeze(1)),
        wait=[], inc=bb, k=16)
    BB_ROWS = bb.n

    for c in range(CCH):
        def fh(e, c=c):
            e.tensor_mul(out=ht1_sb[:], in0=x_sb[:, c, :], in1=brow_sb[:, 0, :])
            e.tensor_sub(out=ht2_sb[:], in0=ht1_sb[:], in1=brow_sb[:, 1, :])
            e.tensor_scalar(out=h_sb[:, c, :], in0=ht2_sb[:],
                            scalar1=A_sb[:, c:c + 1], scalar2=mod_sb[:, c:c + 1],
                            op0=ALU.mult, op1=ALU.add)
            e.tensor_mul(out=h_sb[:, c, 0:1], in0=h_sb[:, c, 0:1], in1=mask_sb[:, 0:1])
            return e.tensor_mul(out=h_sb[:, c, WH - 1:WH], in0=h_sb[:, c, WH - 1:WH],
                                in1=mask_sb[:, 1:2])
        wv = [(bb, BB_ROWS), (rr, RR_MH), (vmd, VMD_DONE), (vv, VV_MASK)] \
            if c == 0 else []
        P.on("vector", fh, wait=wv, inc=hrd)
    assert hrd.n == CCH

    P.on("gpsimd", lambda e: e.dma_start(out=fm(h_o), in_=h_sb[:]),
         wait=[(hrd, CCH)], inc=xx, k=16)
    P.on("gpsimd", lambda e: e.dma_start(out=mod_o, in_=mod_sb[:]),
         wait=[(vmd, VMD_DONE)], inc=xx, k=16)

    for c in range(4):
        hp = h_sb[:, c, 0:TC] if c < 2 else h_sb[:, c, 2:WH]
        wd = [(hrd, CCH)] + ([(urd, 4 * c)] if c > 0 else [])
        P.on("vector", lambda e, c=c, hp=hp: e.tensor_sub(
            out=dscr_sb, in0=h_sb[:, c, 1:TC + 1], in1=hp), wait=wd, inc=urd)
        U_D = urd.n
        for ip in range(3):
            eng = "vector"
            P.on(eng, lambda e, c=c, ip=ip, hp=hp: e.scalar_tensor_tensor(
                out=u_sb[:, ip, c, :], in0=dscr_sb, scalar=mix_sb[:, ip, c:c + 1],
                in1=hp, op0=ALU.mult, op1=ALU.add),
                wait=[(urd, U_D), (ld, LD_MIX)], inc=urd)
    assert urd.n == 16
    P.on("gpsimd", lambda e: e.dma_start(
        out=u_o.rearrange("i (a p) t -> p i a t", p=128), in_=u_sb),
        wait=[(urd, 16)], inc=xx, k=16)

    for t9 in range(9):
        w = [(dqp, t9 - 1)] if t9 >= 2 else []
        if t9 < 3:
            src = wup_sh.rearrange("(a p) c -> p a c", p=128)[:, t9, :]
        elif t9 < 6:
            src = wgate_sh.rearrange("(a p) c -> p a c", p=128)[:, t9 - 3, :]
        else:
            src = wdown_sh[:, (t9 - 6) * 1024:(t9 - 5) * 1024]
        P.on("sync", lambda e, src=src, t9=t9: e.dma_start(
            out=dqw_sb[:, t9 % 2, :], in_=src), wait=w, inc=dql[t9 % 2], k=16)
        def fdq1(e, t9=t9):
            e.tensor_reduce(out=dqi_sb[:],
                            in_=dqw_sb[:, t9 % 2, :].rearrange("p (b k) -> p b k", k=32),
                            axis=AX, op=ALU.max, apply_absolute_value=True)
            e.tensor_scalar(out=dqs_sb[:], in0=dqi_sb[:],
                            scalar1=1.0 / 7.0, scalar2=1e-12,
                            op0=ALU.mult, op1=ALU.max)
            return e.reciprocal(out=dqi_sb[:], in_=dqs_sb[:])
        wq1 = [(dql[t9 % 2], 16 * (t9 // 2 + 1))] + ([(dqp, t9)] if t9 >= 1 else [])
        P.on("vector", fdq1, wait=wq1, inc=dqv)
        def fdq2(e, t9=t9):
            b1 = dqi_sb.ap().unsqueeze(2).broadcast_to([128, 32, 32])
            w3 = dqw_sb[:, t9 % 2, :].rearrange("p (b k) -> p b k", k=32)
            e.tensor_tensor(out=w3, in0=w3, in1=b1, op=ALU.mult)
            e.tensor_scalar(out=dqw_sb[:, t9 % 2, :], in0=dqw_sb[:, t9 % 2, :],
                            scalar1=MAGIC, scalar2=None, op0=ALU.add)
            e.tensor_scalar(out=dqw_sb[:, t9 % 2, :], in0=dqw_sb[:, t9 % 2, :],
                            scalar1=MAGIC, scalar2=None, op0=ALU.subtract)
            b2 = dqs_sb.ap().unsqueeze(2).broadcast_to([128, 32, 32])
            return e.tensor_tensor(
                out=dqo_sb[:, t9 % 2, :].rearrange("p (b k) -> p b k", k=32),
                in0=w3, in1=b2, op=ALU.mult)
        P.on("vector", fdq2,
             wait=[(dqv, t9 + 1)] + ([(xdq, 16 * (t9 - 1))] if t9 >= 2 else []),
             inc=dqp)
        if t9 < 3:
            P.on("gpsimd", lambda e, t9=t9: e.dma_start(
                out=wupd_o.rearrange("(a p) c -> p a c", p=128)[:, t9, :],
                in_=dqo_sb[:, t9 % 2, :]), wait=[(dqp, t9 + 1)], inc=xdq, k=16)
        elif t9 < 6:
            P.on("gpsimd", lambda e, t9=t9: e.dma_start(
                out=wgated_o.rearrange("(a p) c -> p a c", p=128)[:, t9 - 3, :],
                in_=dqo_sb[:, t9 % 2, :]), wait=[(dqp, t9 + 1)], inc=xdq, k=16)
        else:
            P.on("gpsimd", lambda e, t9=t9: e.dma_start(
                out=wdownd_o[:, (t9 - 6) * 1024:(t9 - 5) * 1024],
                in_=dqo_sb[:, t9 % 2, :]), wait=[(dqp, t9 + 1)], inc=xdq, k=16)

    P.on("sync", lambda e: None, wait=[(xx, xx.n), (xdq, xdq.n)])
    P.emit()
    return nc



def build_lb():
    nc = bass.Bass("TRN2", target_bir_lowering=False)
    h_i = nc.dram_tensor("h_i", [C, WH], BF16, kind="ExternalInput").ap()
    u_i = nc.dram_tensor("u_i", [3, C // 2, TC], BF16, kind="ExternalInput").ap()
    wkT = nc.dram_tensor("wkT", [C, C], F32, kind="ExternalInput").ap()
    wvT = nc.dram_tensor("wvT", [C, C], F32, kind="ExternalInput").ap()
    wrT = nc.dram_tensor("wrT", [C, C], F32, kind="ExternalInput").ap()
    sdec = nc.dram_tensor("sdec", [128, CCH], F32, kind="ExternalInput").ap()

    ek_o = nc.dram_tensor("ek_o", [C, TC], F32, kind="ExternalOutput").ap()
    ekv_o = nc.dram_tensor("ekv_o", [C, TC], F32, kind="ExternalOutput").ap()
    sr_o = nc.dram_tensor("sr_o", [C, TC], F32, kind="ExternalOutput").ap()
    afin_o = nc.dram_tensor("afin_o", [128, CCH], F32, kind="ExternalOutput").ap()
    bfin_o = nc.dram_tensor("bfin_o", [128, CCH], F32, kind="ExternalOutput").ap()

    h_sb = nc.alloc_sbuf_tensor("h_sb", [128, CCH, WH], BF16)
    u_sb = nc.alloc_sbuf_tensor("u_sb", [128, 3, 4, TC], BF16)
    wsf_sb = nc.alloc_sbuf_tensor("wsf_sb", [128, 2, 3, CCH, 128], F32)
    wsb_sb = nc.alloc_sbuf_tensor("wsb_sb", [128, 2, 3, CCH, 128], BF16)
    sd_sb = nc.alloc_sbuf_tensor("sd_sb", [128, CCH], F32)
    ew_sb = nc.alloc_sbuf_tensor("ew_sb", [128, CCH], F32)
    tmp8_sb = nc.alloc_sbuf_tensor("tmp8_sb", [128, CCH], F32)
    ek_sb = nc.alloc_sbuf_tensor("ek_sb", [128, TC], F32)
    v_sb = nc.alloc_sbuf_tensor("v_sb", [128, TC], F32)
    ekv_sb = nc.alloc_sbuf_tensor("ekv_sb", [128, TC], F32)
    sr_sb = nc.alloc_sbuf_tensor("sr_sb", [128, TC], F32)
    ewt_sb = nc.alloc_sbuf_tensor("ewt_sb", [128, TC], F32)
    asc_sb = nc.alloc_sbuf_tensor("asc_sb", [128, TC], F32)
    bsc_sb = nc.alloc_sbuf_tensor("bsc_sb", [128, TC], F32)
    afin_sb = nc.alloc_sbuf_tensor("afin_sb", [128, CCH], F32)
    bfin_sb = nc.alloc_sbuf_tensor("bfin_sb", [128, CCH], F32)

    ps_kvr = [nc.alloc_psum_tensor(f"ps_kvr{i}", [128, 512], F32) for i in range(4)]

    P = Prog(nc)
    ld = P.sem("ld")
    awl = [P.sem(f"awl{i}") for i in range(2)]
    wcs = P.sem("wcs")
    vv = P.sem("vv")
    mkv = P.sem("mkv")
    evc = P.sem("evc")
    vkv = P.sem("vkv")
    xx = P.sem("xx")
    xek = P.sem("xek")
    xekv = P.sem("xekv")
    xsr = P.sem("xsr")

    P.on("sync", lambda e: e.dma_start(out=h_sb[:], in_=fm(h_i)), inc=ld, k=16)
    P.on("sync", lambda e: e.dma_start(
        out=u_sb[:], in_=u_i.rearrange("i (a p) t -> p i a t", p=128)), inc=ld, k=16)
    P.on("sync", lambda e: e.dma_start(out=sd_sb[:], in_=sdec), inc=ld, k=16)
    LD_H = LD_U = LD_SD = 48

    def few(e):
        e.tensor_scalar(out=ew_sb[:], in0=sd_sb[:], scalar1=1.0 / T, scalar2=None,
                        op0=ALU.mult)
        return exp_poly(e, ew_sb[:], ew_sb[:], tmp8_sb[:])
    P.on("vector", few, wait=[(ld, LD_SD)], inc=vv)
    VV_EW = vv.n

    def emit_wslice(oc):
        for i, t in enumerate((wkT, wvT, wrT)):
            w = []
            if oc >= 2:
                w += [(wcs, 3 * (oc - 1)), (mkv, 12 * (oc - 1))]
            P.on("sync", lambda e, t=t, oc=oc, i=i: e.dma_start(
                out=wsf_sb[:, oc % 2, i, :, :],
                in_=fm(t)[:, :, oc * 128:(oc + 1) * 128]), wait=w, inc=awl[oc % 2], k=16)
        wc = [(awl[oc % 2], 48 * (oc // 2 + 1))]
        if oc >= 2:
            wc.append((mkv, 12 * (oc - 1)))
        P.on("scalar", lambda e, oc=oc: e.activation(
            out=wsb_sb.ap()[:, oc % 2].rearrange("p i a c -> p (i a c)"),
            in_=wsf_sb.ap()[:, oc % 2].rearrange("p i a c -> p (i a c)"),
            func=AF.Copy), wait=wc, inc=wcs, k=3)

    emit_wslice(0)
    emit_wslice(1)

    for oc in range(CCH):
        for ip in range(3):
            for tt in range(NTT):
                g = oc * 12 + ip * 4 + tt
                w = [(ld, LD_U), (wcs, 3 * (oc + 1))]
                if g >= 4:
                    w.append((evc, g - 3))
                def fkvr(e, oc=oc, ip=ip, tt=tt, g=g):
                    bank = ps_kvr[g % 4]
                    ins = None
                    for cin in range(CCH):
                        if cin < 4:
                            rhs = u_sb[:, ip, cin, tt * 512:(tt + 1) * 512]
                        else:
                            rhs = h_sb[:, cin, 1 + tt * 512:1 + (tt + 1) * 512]
                        ins = e.matmul(out=bank[:], lhsT=wsb_sb[:, oc % 2, ip, cin, :],
                                       rhs=rhs, start=(cin == 0), stop=(cin == CCH - 1))
                    return ins
                P.on("tensor", fkvr, wait=w, inc=mkv)
                we = [(mkv, g + 1)]
                if oc > 0 and tt == 0:
                    we.append(((xek, xekv, xsr)[ip], 16 * oc))
                    if ip <= 1:
                        we.append((vkv, 6 * oc))
                def fev(e, ip=ip, tt=tt, g=g):
                    bank = ps_kvr[g % 4]
                    sl = slice(tt * 512, (tt + 1) * 512)
                    if ip == 0:
                        return e.activation(out=ek_sb[:, sl], in_=bank[:], func=AF.Exp)
                    if ip == 1:
                        return e.activation(out=v_sb[:, sl], in_=bank[:], func=AF.Copy)
                    return e.activation(out=sr_sb[:, sl], in_=bank[:], func=AF.Sigmoid)
                P.on("scalar", fev, wait=we, inc=evc)
        if oc + 2 < CCH:
            emit_wslice(oc + 2)
        def fewt(e, oc=oc):
            e.memset(ewt_sb[:], 1.0)
            return e.tensor_scalar(out=ewt_sb[:], in0=ewt_sb[:],
                                   scalar1=ew_sb[:, oc:oc + 1], scalar2=None,
                                   op0=ALU.mult)
        P.on("vector", fewt, wait=[(vv, VV_EW), (evc, oc * 12 + 4)], inc=vkv)
        P.on("vector", lambda e: e.tensor_mul(out=ekv_sb[:], in0=ek_sb[:], in1=v_sb[:]),
             wait=[(evc, oc * 12 + 8)], inc=vkv)
        P.on("vector", lambda e: e.tensor_tensor_scan(
            out=asc_sb[:], data0=ewt_sb[:], data1=ekv_sb[:], initial=0.0,
            op0=ALU.mult, op1=ALU.add), wait=[], inc=vkv)
        P.on("vector", lambda e, oc=oc: e.tensor_copy(
            out=afin_sb[:, oc:oc + 1], in_=asc_sb[:, TC - 1:TC]), wait=[], inc=vkv)
        P.on("vector", lambda e: e.tensor_tensor_scan(
            out=bsc_sb[:], data0=ewt_sb[:], data1=ek_sb[:], initial=0.0,
            op0=ALU.mult, op1=ALU.add), wait=[], inc=vkv)
        P.on("vector", lambda e, oc=oc: e.tensor_copy(
            out=bfin_sb[:, oc:oc + 1], in_=bsc_sb[:, TC - 1:TC]), wait=[], inc=vkv)
        P.on("gpsimd", lambda e, oc=oc: e.dma_start(
            out=fm(ek_o)[:, oc, :], in_=ek_sb[:]), wait=[(vkv, 6 * oc + 5)], inc=xek, k=16)
        P.on("gpsimd", lambda e, oc=oc: e.dma_start(
            out=fm(ekv_o)[:, oc, :], in_=ekv_sb[:]), wait=[(vkv, 6 * oc + 3)], inc=xekv, k=16)
        P.on("gpsimd", lambda e, oc=oc: e.dma_start(
            out=fm(sr_o)[:, oc, :], in_=sr_sb[:]), wait=[(evc, oc * 12 + 12)], inc=xsr, k=16)
    assert mkv.n == 96 and evc.n == 96 and vkv.n == 48

    P.on("gpsimd", lambda e: e.dma_start(out=afin_o, in_=afin_sb[:]),
         wait=[(vkv, 48)], inc=xx, k=16)
    P.on("gpsimd", lambda e: e.dma_start(out=bfin_o, in_=bfin_sb[:]),
         wait=[(vkv, 48)], inc=xx, k=16)
    P.on("sync", lambda e: None, wait=[(xx, xx.n), (xek, xek.n), (xekv, xekv.n),
                                       (xsr, xsr.n)])
    P.emit()
    return nc



def build_l2():
    nc = bass.Bass("TRN2", target_bir_lowering=False)
    xT = nc.dram_tensor("xT", [C, WH], F32, kind="ExternalInput").ap()
    ek_i = nc.dram_tensor("ek_i", [C, TC], F32, kind="ExternalInput").ap()
    ekv_i = nc.dram_tensor("ekv_i", [C, TC], F32, kind="ExternalInput").ap()
    sr_i = nc.dram_tensor("sr_i", [C, TC], F32, kind="ExternalInput").ap()
    af_all = nc.dram_tensor("af_all", [128, CCH, 8], F32, kind="ExternalInput").ap()
    bf_all = nc.dram_tensor("bf_all", [128, CCH, 8], F32, kind="ExternalInput").ap()
    pw = nc.dram_tensor("pw", [128, 8], F32, kind="ExternalInput").ap()
    pmask = nc.dram_tensor("pmask", [128, 8], F32, kind="ExternalInput").ap()
    woT = nc.dram_tensor("woT", [C, C], F32, kind="ExternalInput").ap()
    mod_i = nc.dram_tensor("mod_i", [128, 48], F32, kind="ExternalInput").ap()
    sdec = nc.dram_tensor("sdec", [128, CCH], F32, kind="ExternalInput").ap()
    sfirst = nc.dram_tensor("sfirst", [128, CCH], F32, kind="ExternalInput").ap()

    x2_o = nc.dram_tensor("x2_o", [C, TC], F32, kind="ExternalOutput").ap()
    h2_o = nc.dram_tensor("h2_o", [C, TC], F32, kind="ExternalOutput").ap()
    mh2_o = nc.dram_tensor("mh2_o", [1, 1], F32, kind="ExternalOutput").ap()

    colscr = nc.dram_tensor("colscr", [128, 1], F32, kind="Internal").ap()

    y_sb = nc.alloc_sbuf_tensor("y_sb", [128, CCH, TC], BF16)
    wob_sb = nc.alloc_sbuf_tensor("wob_sb", [128, CCH, C], BF16)
    a2_sb = nc.alloc_sbuf_tensor("a2_sb", [128, TC + 1], F32)
    b2_sb = nc.alloc_sbuf_tensor("b2_sb", [128, TC + 1], F32)
    num_sb = nc.alloc_sbuf_tensor("num_sb", [128, TC], F32)
    den_sb = nc.alloc_sbuf_tensor("den_sb", [128, TC], F32)
    ewt_sb = nc.alloc_sbuf_tensor("ewt_sb", [128, TC], F32)
    xt_sb = nc.alloc_sbuf_tensor("xt_sb", [128, 4, 512], F32)
    h2t_sb = nc.alloc_sbuf_tensor("h2t_sb", [128, 2, 512], F32)
    af_sb = nc.alloc_sbuf_tensor("af_sb", [128, 2, CCH, 8], F32)
    d3_sb = nc.alloc_sbuf_tensor("d3_sb", [128, CCH, 8], F32)
    t3_sb = nc.alloc_sbuf_tensor("t3_sb", [128, CCH, 8], F32)
    ain_sb = nc.alloc_sbuf_tensor("ain_sb", [128, 2, CCH], F32)
    pw_sb = nc.alloc_sbuf_tensor("pw_sb", [128, 2, 8], F32)
    sd_sb = nc.alloc_sbuf_tensor("sd_sb", [128, CCH], F32)
    sf_sb = nc.alloc_sbuf_tensor("sf_sb", [128, CCH], F32)
    ew_sb = nc.alloc_sbuf_tensor("ew_sb", [128, CCH], F32)
    eu_sb = nc.alloc_sbuf_tensor("eu_sb", [128, CCH], F32)
    tmp8_sb = nc.alloc_sbuf_tensor("tmp8_sb", [128, CCH], F32)
    mod_sb = nc.alloc_sbuf_tensor("mod_sb", [128, 48], F32)
    ga_sb = nc.alloc_sbuf_tensor("ga_sb", [128, CCH], F32)
    gat_sb = nc.alloc_sbuf_tensor("gat_sb", [128, 2, CCH], F32)
    B2_sb = nc.alloc_sbuf_tensor("B2_sb", [128, CCH], F32)
    run_sb = nc.alloc_sbuf_tensor("run_sb", [128, 2], F32)
    mrow_sb = nc.alloc_sbuf_tensor("mrow_sb", [1, 130], F32)
    stat2_sb = nc.alloc_sbuf_tensor("stat2_sb", [1, 3, 512], F32)
    onesb_sb = nc.alloc_sbuf_tensor("onesb_sb", [128, 1], BF16)
    ones1_sb = nc.alloc_sbuf_tensor("ones1_sb", [1, 128], F32)
    eps_sb = nc.alloc_sbuf_tensor("eps_sb", [1, 1], F32)
    ZW = Arena(nc, "zoneW", 33 * 1024)
    wof_sb = ZW.carve([CCH, C], F32)
    ZW.reset()
    x2c_sb = ZW.carve([2, CCH, 512], F32)
    ZS = Arena(nc, "zoneS", 48 * 1024)
    ek2_sb = ZS.carve([2, TC], F32)
    ekv2_sb = ZS.carve([2, TC], F32)
    sr2_sb = ZS.carve([2, TC], F32)
    ZS.reset()
    xb2_sb = ZS.carve([CCH, 512], BF16)
    x2b2_sb = ZS.carve([CCH, 512], BF16)

    ps_o = [nc.alloc_psum_tensor(f"ps_o{i}", [128, 512], F32) for i in range(4)]
    ps_sum = nc.alloc_psum_tensor("ps_sum", [1, 512], F32)
    ps_ssq = nc.alloc_psum_tensor("ps_ssq", [1, 512], F32)
    ps_bc = [nc.alloc_psum_tensor(f"ps_bc{i}", [128, 512], F32) for i in range(2)]

    P = Prog(nc)
    ld = P.sem("ld")
    awo = P.sem("awo")
    cek = P.sem("cek")
    cekv = P.sem("cekv")
    csr = P.sem("csr")
    xld = [P.sem(f"xld{i}") for i in range(4)]
    hxx = [P.sem(f"hxx{i}") for i in range(2)]
    x2x = P.sem("x2x")
    vv = P.sem("vv")
    sph = P.sem("sph")
    vph = P.sem("vph")
    mm = P.sem("mm")
    evc = P.sem("evc")
    qq = P.sem("qq")
    mst = P.sem("mst")
    bcm = P.sem("bcm")
    sst = P.sem("sst")
    bb = P.sem("bb")
    hx = P.sem("hx")
    xx = P.sem("xx")

    P.on("sync", lambda e: e.dma_start(out=af_sb[:, 0], in_=af_all), inc=ld, k=16)
    P.on("sync", lambda e: e.dma_start(out=af_sb[:, 1], in_=bf_all), inc=ld, k=16)
    P.on("sync", lambda e: e.dma_start(out=pw_sb[:, 0], in_=pw), inc=ld, k=16)
    P.on("sync", lambda e: e.dma_start(out=pw_sb[:, 1], in_=pmask), inc=ld, k=16)
    P.on("sync", lambda e: e.dma_start(out=mod_sb[:], in_=mod_i), inc=ld, k=16)
    P.on("sync", lambda e: e.dma_start(out=sd_sb[:], in_=sdec), inc=ld, k=16)
    P.on("sync", lambda e: e.dma_start(out=sf_sb[:], in_=sfirst), inc=ld, k=16)
    LD_SMALL = ld.n
    P.on("gpsimd", lambda e: e.dma_start(out=wof_sb, in_=fm(woT)), inc=awo, k=16)

    for _oc in (0, 1):
        P.on("sync", lambda e, oc=_oc: e.dma_start(
            out=ek2_sb[:, oc % 2, :], in_=fm(ek_i)[:, oc, :]), inc=cek, k=16)
        P.on("scalar", lambda e, oc=_oc: e.dma_start(
            out=ekv2_sb[:, oc % 2, :], in_=fm(ekv_i)[:, oc, :]), inc=cekv, k=16)
        P.on("scalar", lambda e, oc=_oc: e.dma_start(
            out=sr2_sb[:, oc % 2, :], in_=fm(sr_i)[:, oc, :]), inc=csr, k=16)

    def fsetup(e):
        e.memset(onesb_sb[:], 1.0)
        e.memset(ones1_sb[:], 1.0)
        e.memset(eps_sb[:], EPS)
        e.tensor_scalar(out=ew_sb[:], in0=sd_sb[:], scalar1=1.0 / T, scalar2=None,
                        op0=ALU.mult)
        exp_poly(e, ew_sb[:], ew_sb[:], tmp8_sb[:])
        e.tensor_scalar(out=eu_sb[:], in0=sf_sb[:], scalar1=1.0 / T, scalar2=None,
                        op0=ALU.mult)
        exp_poly(e, eu_sb[:], eu_sb[:], tmp8_sb[:])
        e.tensor_scalar(out=B2_sb[:], in0=mod_sb[:, 32:40], scalar1=1.0, scalar2=None,
                        op0=ALU.add)
        return e.tensor_tensor(
            out=t3_sb[:], in0=sd_sb.ap().unsqueeze(2).broadcast_to([128, CCH, 8]),
            in1=pw_sb.ap()[:, 0, :].unsqueeze(1).broadcast_to([128, CCH, 8]),
            op=ALU.mult)
    P.on("vector", fsetup, wait=[(ld, LD_SMALL)], inc=vv)
    VV_T3 = vv.n
    P.on("scalar", lambda e: e.activation(out=d3_sb[:], in_=t3_sb[:], func=AF.Exp),
         wait=[(vv, VV_T3)], inc=sph)
    P.on("scalar", lambda e: e.activation(out=gat_sb[:, 0], in_=mod_sb[:, 16:24],
                                          func=AF.Abs), wait=[(ld, LD_SMALL)], inc=sph)
    P.on("scalar", lambda e: e.sign(out=gat_sb[:, 1], in_=mod_sb[:, 16:24]),
         wait=[], inc=sph)
    SPH_GA = sph.n

    def fprefix(e):
        e.tensor_tensor(out=d3_sb[:], in0=d3_sb[:],
                        in1=pw_sb.ap()[:, 1, :].unsqueeze(1).broadcast_to([128, CCH, 8]),
                        op=ALU.mult)
        e.tensor_tensor(out=t3_sb[:], in0=af_sb[:, 0], in1=d3_sb[:], op=ALU.mult)
        e.tensor_reduce(out=ain_sb[:, 0], in_=t3_sb[:], axis=AX, op=ALU.add)
        e.tensor_tensor(out=t3_sb[:], in0=af_sb[:, 1], in1=d3_sb[:], op=ALU.mult)
        e.tensor_reduce(out=ain_sb[:, 1], in_=t3_sb[:], axis=AX, op=ALU.add)
        e.tensor_scalar(out=gat_sb[:, 0], in0=gat_sb[:, 0], scalar1=0.25,
                        scalar2=None, op0=ALU.max)
        e.tensor_tensor(out=ga_sb[:], in0=gat_sb[:, 1], in1=gat_sb[:, 0],
                        op=ALU.mult)
        return e.memset(run_sb[:], 0.0)
    P.on("vector", fprefix, wait=[(sph, SPH_GA)], inc=vv)
    VV_PREFIX = vv.n

    P.on("scalar", lambda e: e.activation(
        out=wob_sb.ap().rearrange("p a b -> p (a b)"),
        in_=wof_sb.rearrange("p a b -> p (a b)"), func=AF.Copy),
        wait=[(awo, 16)], inc=qq)
    QQ_WO = qq.n

    AM, BM, YM = {}, {}, {}

    def emit_loads(oc):
        wek = [(vph, BM[oc - 2])] if oc >= 2 else []
        wsr = [(vph, YM[oc - 2])] if oc >= 2 else []
        P.on("sync", lambda e, oc=oc: e.dma_start(
            out=ek2_sb[:, oc % 2, :], in_=fm(ek_i)[:, oc, :]), wait=wek, inc=cek, k=16)
        P.on("scalar", lambda e, oc=oc: e.dma_start(
            out=ekv2_sb[:, oc % 2, :], in_=fm(ekv_i)[:, oc, :]), wait=wek,
            inc=cekv, k=16)
        P.on("scalar", lambda e, oc=oc: e.dma_start(
            out=sr2_sb[:, oc % 2, :], in_=fm(sr_i)[:, oc, :]), wait=wsr, inc=csr, k=16)

    def emit_A(oc):
        def fA(e, oc=oc):
            sl = oc % 2
            e.memset(ewt_sb[:], 1.0)
            e.tensor_scalar(out=ewt_sb[:], in0=ewt_sb[:], scalar1=ew_sb[:, oc:oc + 1],
                            scalar2=None, op0=ALU.mult)
            e.tensor_tensor_scan(out=a2_sb[:, 1:TC + 1], data0=ewt_sb[:],
                                 data1=ekv2_sb[:, sl, :],
                                 initial=ain_sb[:, 0, oc:oc + 1],
                                 op0=ALU.mult, op1=ALU.add)
            e.tensor_tensor_scan(out=b2_sb[:, 1:TC + 1], data0=ewt_sb[:],
                                 data1=ek2_sb[:, sl, :],
                                 initial=ain_sb[:, 1, oc:oc + 1],
                                 op0=ALU.mult, op1=ALU.add)
            e.tensor_copy(out=a2_sb[:, 0:1], in_=ain_sb[:, 0, oc:oc + 1])
            return e.tensor_copy(out=b2_sb[:, 0:1], in_=ain_sb[:, 1, oc:oc + 1])
        P.on("vector", fA, wait=[(cek, 16 * (oc + 1)), (cekv, 16 * (oc + 1)),
                                 (vv, VV_PREFIX)], inc=vph)
        AM[oc] = vph.n

    def emit_B(oc):
        def fB(e, oc=oc):
            sl = oc % 2
            e.scalar_tensor_tensor(out=num_sb[:], in0=ekv2_sb[:, sl, :],
                                   scalar=eu_sb[:, oc:oc + 1], in1=a2_sb[:, 0:TC],
                                   op0=ALU.mult, op1=ALU.add)
            return e.scalar_tensor_tensor(out=den_sb[:], in0=ek2_sb[:, sl, :],
                                          scalar=eu_sb[:, oc:oc + 1],
                                          in1=b2_sb[:, 0:TC],
                                          op0=ALU.mult, op1=ALU.add)
        w = [(sph, SPH_GA + oc)] if oc >= 1 else []
        P.on("vector", fB, wait=w, inc=vph)
        BM[oc] = vph.n
        def frden(e):
            e.activation(out=den_sb[:], in_=den_sb[:], func=AF.Ln)
            return e.activation(out=den_sb[:], in_=den_sb[:], func=AF.Exp, scale=-1.0)
        P.on("scalar", frden, wait=[(vph, BM[oc])], inc=sph)

    def emit_Y(oc):
        def fy(e, oc=oc):
            e.tensor_mul(out=num_sb[:], in0=num_sb[:], in1=den_sb[:])
            return e.tensor_mul(out=y_sb[:, oc, :], in0=sr2_sb[:, oc % 2, :],
                                in1=num_sb[:])
        P.on("vector", fy, wait=[(sph, SPH_GA + oc + 1), (csr, 16 * (oc + 1))],
             inc=vph)
        YM[oc] = vph.n

    emit_A(0)
    emit_B(0)
    for oc in range(1, CCH):
        emit_A(oc)
        emit_Y(oc - 1)
        if oc + 1 < CCH:
            emit_loads(oc + 1)
        emit_B(oc)
    emit_Y(CCH - 1)
    assert vph.n == 3 * CCH

    scq = P.sem("scq")

    def emit_omm(tt):
        sl = tt % 2
        for oc2 in range(CCH):
            g = tt * CCH + oc2
            w = [(evc, g - 3)] if g >= 4 else []
            P.on("sync", lambda e, tt=tt, oc2=oc2, g=g: e.dma_start(
                out=xt_sb[:, g % 4, :],
                in_=fm(xT)[:, oc2, 1 + tt * 512:1 + (tt + 1) * 512]),
                wait=w, inc=xld[g % 4], k=16)
            wm_ = [(qq, QQ_WO), (xld[g % 4], 16 * (g // 4 + 1))]
            if g >= 4:
                wm_.append((evc, g - 3))
            def fomm(e, tt=tt, oc2=oc2, g=g):
                bank = ps_o[g % 4]
                ins = None
                for cin in range(CCH):
                    if g == 0:
                        e.wait_ge(vph.h, YM[cin])
                    ins = e.matmul(out=bank[:],
                                   lhsT=wob_sb[:, cin, oc2 * 128:(oc2 + 1) * 128],
                                   rhs=y_sb[:, cin, tt * 512:(tt + 1) * 512],
                                   start=(cin == 0), stop=(cin == CCH - 1))
                return ins
            P.on("tensor", fomm, wait=wm_, inc=mm)
            wev = [(mm, g + 1)]
            if oc2 == 0 and tt >= 2:
                wev += [(x2x, 16 * (tt - 1)), (scq, 2 * (tt - 1))]
            P.on("vector", lambda e, sl=sl, oc2=oc2, g=g: e.scalar_tensor_tensor(
                out=x2c_sb[:, sl, oc2, :], in0=ps_o[g % 4][:],
                scalar=ga_sb[:, oc2:oc2 + 1],
                in1=xt_sb[:, g % 4, :], op0=ALU.mult, op1=ALU.add),
                wait=wev, inc=evc)
        P.on("gpsimd", lambda e, tt=tt, sl=sl: e.dma_start(
            out=fm(x2_o)[:, :, tt * 512:(tt + 1) * 512], in_=x2c_sb[:, sl]),
            wait=[(evc, (tt + 1) * CCH)], inc=x2x, k=16)

    def emit_castsq(tt):
        sl = tt % 2
        P.on("scalar", lambda e, sl=sl: e.activation(
            out=xb2_sb.rearrange("p a t -> p (a t)"),
            in_=x2c_sb[:, sl].rearrange("p a t -> p (a t)"), func=AF.Copy),
            wait=[(evc, (tt + 1) * CCH)] + ([(mst, tt)] if tt > 0 else []), inc=scq)
        P.on("scalar", lambda e: e.activation(
            out=x2b2_sb.rearrange("p a t -> p (a t)"),
            in_=xb2_sb.rearrange("p a t -> p (a t)"), func=AF.Square), inc=scq)

    def emit_stats(tt):
        def fstat(e):
            for a in range(CCH):
                e.matmul(out=ps_sum[0:1, :], lhsT=onesb_sb[:], rhs=xb2_sb[:, a, :],
                         start=(a == 0), stop=(a == CCH - 1))
            ins = None
            for a in range(CCH):
                ins = e.matmul(out=ps_ssq[0:1, :], lhsT=onesb_sb[:],
                               rhs=x2b2_sb[:, a, :],
                               start=(a == 0), stop=(a == CCH - 1))
            return ins
        wst = [(scq, 2 * tt + 2)]
        if tt > 0:
            wst.append((sst, 3 * tt - 2))
        P.on("tensor", fstat, wait=wst, inc=mst)

    def emit_ln2(tt):
        sl = tt % 2
        def flad1(e):
            e.tensor_scalar(out=stat2_sb[0:1, 0, :], in0=ps_sum[0:1, :],
                            scalar1=1.0 / C, scalar2=None, op0=ALU.mult)
            e.tensor_scalar(out=stat2_sb[0:1, 1, :], in0=ps_ssq[0:1, :],
                            scalar1=1.0 / C, scalar2=None, op0=ALU.mult)
            e.tensor_mul(out=stat2_sb[0:1, 2, :], in0=stat2_sb[0:1, 0, :],
                         in1=stat2_sb[0:1, 0, :])
            return e.tensor_sub(out=stat2_sb[0:1, 1, :], in0=stat2_sb[0:1, 1, :],
                                in1=stat2_sb[0:1, 2, :])
        P.on("vector", flad1, wait=[(mst, tt + 1)], inc=sst)
        def flad2(e):
            e.activation(out=stat2_sb[0:1, 1, :], in_=stat2_sb[0:1, 1, :], func=AF.Ln,
                         bias=eps_sb[0:1, :])
            return e.activation(out=stat2_sb[0:1, 1, :], in_=stat2_sb[0:1, 1, :],
                                func=AF.Exp, scale=-0.5)
        P.on("scalar", flad2, wait=[(sst, 3 * tt + 1)], inc=sst)
        def flad3(e):
            return e.tensor_mul(out=stat2_sb[0:1, 0, :], in0=stat2_sb[0:1, 0, :],
                                in1=stat2_sb[0:1, 1, :])
        P.on("vector", flad3, wait=[(sst, 3 * tt + 2)], inc=sst)
        def fbc(e):
            e.matmul(out=ps_bc[0][:, 0:256], lhsT=ones1_sb[:],
                     rhs=stat2_sb[0:1, 1, 0:256], start=True, stop=True)
            e.matmul(out=ps_bc[0][:, 256:512], lhsT=ones1_sb[:],
                     rhs=stat2_sb[0:1, 1, 256:512], start=True, stop=True)
            e.matmul(out=ps_bc[1][:, 0:256], lhsT=ones1_sb[:],
                     rhs=stat2_sb[0:1, 0, 0:256], start=True, stop=True)
            return e.matmul(out=ps_bc[1][:, 256:512], lhsT=ones1_sb[:],
                            rhs=stat2_sb[0:1, 0, 256:512], start=True, stop=True)
        wbc = [(sst, 3 * tt + 3)]
        if tt > 0:
            wbc.append((hx, tt * CCH))
        P.on("tensor", fbc, wait=wbc, inc=bcm)
        for oc2 in range(CCH):
            def fh2(e, oc2=oc2, sl=sl):
                s2 = oc2 % 2
                e.tensor_mul(out=h2t_sb[:, s2, :], in0=x2c_sb[:, sl, oc2, :],
                             in1=ps_bc[0][:])
                e.tensor_sub(out=h2t_sb[:, s2, :], in0=h2t_sb[:, s2, :],
                             in1=ps_bc[1][:])
                e.tensor_scalar(out=h2t_sb[:, s2, :], in0=h2t_sb[:, s2, :],
                                scalar1=B2_sb[:, oc2:oc2 + 1],
                                scalar2=mod_sb[:, 24 + oc2:25 + oc2],
                                op0=ALU.mult, op1=ALU.add)
                e.tensor_reduce(out=run_sb[:, 1:2], in_=h2t_sb[:, s2, :], axis=AX,
                                op=ALU.max, apply_absolute_value=True)
                return e.tensor_max(out=run_sb[:, 0:1], in0=run_sb[:, 0:1],
                                    in1=run_sb[:, 1:2])
            gh = tt * CCH + oc2
            wv = [(bcm, tt + 1)]
            if gh >= 2:
                wv.append((hxx[gh % 2], 16 * ((gh - 2) // 2 + 1)))
            P.on("vector", fh2, wait=wv, inc=hx)
            P.on("scalar", lambda e, tt=tt, oc2=oc2: e.dma_start(
                out=fm(h2_o)[:, oc2, tt * 512:(tt + 1) * 512],
                in_=h2t_sb[:, oc2 % 2, :]),
                wait=[(hx, tt * CCH + oc2 + 1)], inc=hxx[(tt * CCH + oc2) % 2], k=16)

    for tt in range(NTT):
        emit_omm(tt)
        if tt >= 1:
            emit_stats(tt - 1)
        emit_castsq(tt)
        if tt >= 1:
            emit_ln2(tt - 1)
    emit_stats(NTT - 1)
    emit_ln2(NTT - 1)
    assert mm.n == 32 and evc.n == 32

    emit_absmax_export(P, run_sb[:, 0:1], mrow_sb, colscr.rearrange("a b -> (a b)"),
                       mh2_o, [(hx, 4 * CCH)], bb, vv, xx)

    P.on("sync", lambda e: None,
         wait=[(xx, xx.n), (x2x, x2x.n), (hxx[0], hxx[0].n), (hxx[1], hxx[1].n)])
    P.emit()
    return nc



def emit_scale_prep(P, maxa_sb, srow_sb, ones1_sb, ps_sc, scol_sb, icol_sb, qmax,
                    wait, vv, bb):
    def fs(e):
        e.reduce_max(out=srow_sb[0:1, 0:1], in_=maxa_sb[0:1, :], axis=AX)
        e.tensor_scalar(out=srow_sb[0:1, 1:2], in0=srow_sb[0:1, 0:1],
                        scalar1=1.0 / qmax, scalar2=1e-12, op0=ALU.mult, op1=ALU.max)
        return e.reciprocal(out=srow_sb[0:1, 2:3], in_=srow_sb[0:1, 1:2])
    P.on("vector", fs, wait=wait, inc=vv)
    V1 = vv.n
    P.on("tensor", lambda e: e.matmul(out=ps_sc[:], lhsT=ones1_sb[:],
                                      rhs=srow_sb[0:1, 1:3], start=True, stop=True),
         wait=[(vv, V1)], inc=bb)
    B1 = bb.n
    def fcp(e):
        e.tensor_copy(out=scol_sb[:], in_=ps_sc[:, 0:1])
        return e.tensor_copy(out=icol_sb[:], in_=ps_sc[:, 1:2])
    P.on("vector", fcp, wait=[(bb, B1)], inc=bb)
    return bb.n


def emit_scale_prep_dram(P, maxa_sb, srow_sb, scr2, scol_sb, icol_sb, qmax,
                         wait, vv, bb):
    def fs(e):
        e.reduce_max(out=srow_sb[0:1, 0:1], in_=maxa_sb[0:1, :], axis=AX)
        e.tensor_scalar(out=srow_sb[0:1, 1:2], in0=srow_sb[0:1, 0:1],
                        scalar1=1.0 / qmax, scalar2=1e-12, op0=ALU.mult, op1=ALU.max)
        return e.reciprocal(out=srow_sb[0:1, 2:3], in_=srow_sb[0:1, 1:2])
    P.on("vector", fs, wait=wait, inc=vv)
    V1 = vv.n
    P.on("gpsimd", lambda e: e.dma_start(out=scr2.unsqueeze(0), in_=srow_sb[0:1, 1:3]),
         wait=[(vv, V1)], inc=bb, k=16)
    B1 = bb.n
    P.on("gpsimd", lambda e: e.dma_start(
        out=scol_sb[:], in_=scr2[0:1].unsqueeze(0).partition_broadcast(128).squeeze(1)),
        wait=[(bb, B1)], inc=bb, k=16)
    P.on("gpsimd", lambda e: e.dma_start(
        out=icol_sb[:], in_=scr2[1:2].unsqueeze(0).partition_broadcast(128).squeeze(1)),
        wait=[], inc=bb, k=16)
    return bb.n



def build_l3():
    nc = bass.Bass("TRN2", target_bir_lowering=False)
    h2_i = nc.dram_tensor("h2_i", [C, TC], F32, kind="ExternalInput").ap()
    mh2a = nc.dram_tensor("mh2a", [1, 8], F32, kind="ExternalInput").ap()
    wupT_d = nc.dram_tensor("wupT_d", [C, NI], BF16, kind="ExternalInput").ap()
    wgateT_d = nc.dram_tensor("wgateT_d", [C, NI], BF16, kind="ExternalInput").ap()
    gate_o = nc.dram_tensor("gate_o", [NI, TC], F32, kind="ExternalOutput").ap()
    up_o = nc.dram_tensor("up_o", [NI, TC], F32, kind="ExternalOutput").ap()
    mg_o = nc.dram_tensor("mg_o", [1, 1], F32, kind="ExternalOutput").ap()
    mu_o = nc.dram_tensor("mu_o", [1, 1], F32, kind="ExternalOutput").ap()
    scr2 = nc.dram_tensor("scr2", [2], F32, kind="Internal").ap()
    colscr = nc.dram_tensor("colscr", [128], F32, kind="Internal").ap()

    wg_sb = nc.alloc_sbuf_tensor("wg_sb", [128, CCH, NI], BF16)
    wu_sb = nc.alloc_sbuf_tensor("wu_sb", [128, CCH, NI], BF16)
    h2q_sb = nc.alloc_sbuf_tensor("h2q_sb", [128, CCH, TC], BF16)
    h2s_sb = nc.alloc_sbuf_tensor("h2s_sb", [128, 2, CCH, 512], F32)
    ev_sb = nc.alloc_sbuf_tensor("ev_sb", [128, 4, 512], F32)
    maxa_sb = nc.alloc_sbuf_tensor("maxa_sb", [1, 8], F32)
    srow_sb = nc.alloc_sbuf_tensor("srow_sb", [1, 4], F32)
    scol_sb = nc.alloc_sbuf_tensor("scol_sb", [128, 1], F32)
    icol_sb = nc.alloc_sbuf_tensor("icol_sb", [128, 1], F32)
    rung_sb = nc.alloc_sbuf_tensor("rung_sb", [128, 2], F32)
    runu_sb = nc.alloc_sbuf_tensor("runu_sb", [128, 2], F32)
    mrow_sb = nc.alloc_sbuf_tensor("mrow_sb", [1, 130], F32)

    ps = [nc.alloc_psum_tensor(f"ps{i}", [128, 512], F32) for i in range(4)]
    ps_sc = nc.alloc_psum_tensor("ps_sc", [128, 2], F32)
    ones1_sb = nc.alloc_sbuf_tensor("ones1_sb", [1, 128], F32)

    P = Prog(nc)
    ld = P.sem("ld")
    wld = P.sem("wld")
    wld2 = P.sem("wld2")
    hld = [P.sem(f"hld{i}") for i in range(2)]
    xsl = [P.sem(f"xsl{i}") for i in range(4)]
    vv = P.sem("vv")
    ss = P.sem("ss")
    qq = P.sem("qq")
    mm = P.sem("mm")
    evc = P.sem("evc")
    bb = P.sem("bb")
    xx = P.sem("xx")

    P.on("sync", lambda e: e.dma_start(out=maxa_sb[0:1, :], in_=mh2a), inc=ld, k=16)
    for hf in range(2):
        P.on("sync", lambda e, hf=hf: e.dma_start(
            out=wg_sb[:, :, hf * 1536:(hf + 1) * 1536],
            in_=fm(wgateT_d)[:, :, hf * 1536:(hf + 1) * 1536]), inc=wld, k=16)
        P.on("scalar", lambda e, hf=hf: e.dma_start(
            out=wu_sb[:, :, hf * 1536:(hf + 1) * 1536],
            in_=fm(wupT_d)[:, :, hf * 1536:(hf + 1) * 1536]), inc=wld2, k=16)
    P.on("vector", lambda e: e.memset(ones1_sb[:], 1.0), inc=vv)
    BB_SC = emit_scale_prep(P, maxa_sb, srow_sb, ones1_sb, ps_sc, scol_sb, icol_sb,
                            32767.0, [(ld, 16), (vv, vv.n)], vv, bb)
    P.on("vector", lambda e: e.memset(rung_sb[:], 0.0), inc=vv)
    P.on("vector", lambda e: e.memset(runu_sb[:], 0.0), inc=vv)
    VV_MAXBASE = vv.n

    for tt in range(NTT):
        w = [(qq, 2 * (tt - 1))] if tt >= 2 else []
        P.on("sync", lambda e, tt=tt: e.dma_start(
            out=h2s_sb[:, tt % 2], in_=fm(h2_i)[:, :, tt * 512:(tt + 1) * 512]),
            wait=w, inc=hld[tt % 2], k=16)
        def fqdq(e, tt=tt):
            e.tensor_scalar(out=h2s_sb[:, tt % 2], in0=h2s_sb[:, tt % 2],
                            scalar1=icol_sb[:], scalar2=MAGIC,
                            op0=ALU.mult, op1=ALU.add)
            return e.tensor_scalar(out=h2q_sb[:, :, tt * 512:(tt + 1) * 512],
                                   in0=h2s_sb[:, tt % 2], scalar1=MAGIC,
                                   scalar2=scol_sb[:], op0=ALU.subtract, op1=ALU.mult)
        P.on("vector", fqdq, wait=[(hld[tt % 2], 16 * (tt // 2 + 1)), (bb, BB_SC)], inc=qq, k=2)
    QQ_ALL = qq.n

    NG = 0
    for wi, (wsb, out_d, run) in enumerate(((wg_sb, gate_o, rung_sb),
                                            (wu_sb, up_o, runu_sb))):
        for ic in range(ICH):
            for tt in range(NTT):
                g = NG
                NG += 1
                wm_ = [(qq, QQ_ALL), ((wld, 32) if wi == 0 else (wld2, 32))]
                if g >= 4:
                    wm_.append((evc, g - 3))
                def fmm(e, wsb=wsb, ic=ic, tt=tt, g=g):
                    bank = ps[g % 4]
                    ins = None
                    for hc in range(CCH):
                        ins = e.matmul(out=bank[:],
                                       lhsT=wsb[:, hc, ic * 128:(ic + 1) * 128],
                                       rhs=h2q_sb[:, hc, tt * 512:(tt + 1) * 512],
                                       start=(hc == 0), stop=(hc == CCH - 1))
                    return ins
                P.on("tensor", fmm, wait=wm_, inc=mm)
                eng = "scalar"
                sl = g % 4
                wev = [(mm, g + 1)]
                if g >= 4:
                    wev.append((xsl[g % 4], 16 * ((g - 4) // 4 + 1)))
                    wev.append((vv, VV_MAXBASE + g - 3))
                P.on(eng, lambda e, g=g, sl=sl: e.activation(
                    out=ev_sb[:, sl], in_=ps[g % 4][:], func=AF.Copy),
                    wait=wev, inc=evc)
                def fmax(e, g=g, sl=sl, run=run):
                    e.tensor_reduce(out=run[:, 1:2], in_=ev_sb[:, sl], axis=AX,
                                    op=ALU.max, apply_absolute_value=True)
                    return e.tensor_max(out=run[:, 0:1], in0=run[:, 0:1],
                                        in1=run[:, 1:2])
                P.on("vector", fmax, wait=[(evc, g + 1)], inc=vv)
                P.on("gpsimd", lambda e, out_d=out_d, ic=ic, tt=tt, g=g, sl=sl: e.dma_start(
                    out=out_d.rearrange("(a p) t -> p a t", p=128)[:, ic,
                                                                  tt * 512:(tt + 1) * 512],
                    in_=ev_sb[:, sl]), wait=[(evc, g + 1)], inc=xsl[g % 4], k=16)
    assert NG == 192

    emit_absmax_export(P, rung_sb[:, 0:1], mrow_sb, colscr, mg_o,
                       [(vv, vv.n)], bb, vv, xx)
    emit_absmax_export(P, runu_sb[:, 0:1], mrow_sb, colscr, mu_o,
                       [(vv, vv.n)], bb, vv, xx)
    P.on("sync", lambda e: None,
         wait=[(xx, xx.n)] + [(x, x.n) for x in xsl])
    P.emit()
    return nc



def build_l4():
    nc = bass.Bass("TRN2", target_bir_lowering=False)
    gate_i = nc.dram_tensor("gate_i", [NI, TC], F32, kind="ExternalInput").ap()
    mga = nc.dram_tensor("mga", [1, 8], F32, kind="ExternalInput").ap()
    g2_o = nc.dram_tensor("g2_o", [NI, TC], F32, kind="ExternalOutput").ap()
    mg2_o = nc.dram_tensor("mg2_o", [1, 1], F32, kind="ExternalOutput").ap()
    scr2 = nc.dram_tensor("scr2", [2], F32, kind="Internal").ap()
    colscr = nc.dram_tensor("colscr", [128], F32, kind="Internal").ap()

    gt_sb = nc.alloc_sbuf_tensor("gt_sb", [128, 4, TC], F32)
    gq_sb = nc.alloc_sbuf_tensor("gq_sb", [128, 4, TC], F32)
    sg_sb = nc.alloc_sbuf_tensor("sg_sb", [128, 4, TC], F32)
    g2_sb = nc.alloc_sbuf_tensor("g2_sb", [128, 4, TC], F32)
    maxa_sb = nc.alloc_sbuf_tensor("maxa_sb", [1, 8], F32)
    srow_sb = nc.alloc_sbuf_tensor("srow_sb", [1, 4], F32)
    scol_sb = nc.alloc_sbuf_tensor("scol_sb", [128, 1], F32)
    icol_sb = nc.alloc_sbuf_tensor("icol_sb", [128, 1], F32)
    run_sb = nc.alloc_sbuf_tensor("run_sb", [128, 2], F32)
    mrow_sb = nc.alloc_sbuf_tensor("mrow_sb", [1, 130], F32)
    ones1_sb = nc.alloc_sbuf_tensor("ones1_sb", [1, 128], F32)
    ps_sc = nc.alloc_psum_tensor("ps_sc", [128, 2], F32)

    P = Prog(nc)
    ld = P.sem("ld")
    gld = [P.sem(f"gld{i}") for i in range(4)]
    xsl = [P.sem(f"xsl{i}") for i in range(4)]
    vv = P.sem("vv")
    ss = P.sem("ss")
    pp = P.sem("pp")
    qq = P.sem("qq")
    bb = P.sem("bb")
    xx = P.sem("xx")

    P.on("sync", lambda e: e.dma_start(out=maxa_sb[0:1, :], in_=mga), inc=ld, k=16)
    P.on("vector", lambda e: e.memset(ones1_sb[:], 1.0), inc=vv)
    BB_SC = emit_scale_prep(P, maxa_sb, srow_sb, ones1_sb, ps_sc, scol_sb, icol_sb,
                            32767.0, [(ld, 16), (vv, vv.n)], vv, bb)
    P.on("vector", lambda e: e.memset(run_sb[:], 0.0), inc=vv)

    NS = 4
    ACT, BQ, CS, DQ = {}, {}, {}, {}

    def stA(g):
        sl = g % NS
        w = [(pp, g - NS + 1)] if g >= NS else []
        P.on("sync", lambda e, g=g, sl=sl: e.dma_start(
            out=gt_sb[:, sl, :],
            in_=gate_i.rearrange("(a p) t -> p a t", p=128)[:, g, :]),
            wait=w, inc=gld[sl], k=16)
        P.on("scalar", lambda e, g=g, sl=sl: e.activation(
            out=gq_sb[:, sl, :], in_=gt_sb[:, sl, :], func=AF.Copy, scale=icol_sb[:],
            bias=MAGIC),
            wait=[(gld[sl], 16 * (g // NS + 1)), (bb, BB_SC)]
                 + ([(pp, g - NS + 1)] if g >= NS else []), inc=ss)
        ACT[g] = ss.n

    def stB(g):
        sl = g % NS
        P.on("vector", lambda e, sl=sl: e.tensor_scalar(
            out=gq_sb[:, sl, :], in0=gq_sb[:, sl, :], scalar1=MAGIC, scalar2=scol_sb[:],
            op0=ALU.subtract, op1=ALU.mult), wait=[(ss, ACT[g])], inc=vv)
        BQ[g] = vv.n

    def stC(g):
        sl = g % NS
        P.on("scalar", lambda e, sl=sl: e.activation(out=sg_sb[:, sl, :],
                                                     in_=gq_sb[:, sl, :],
                                                     func=AF.Sigmoid),
             wait=[(vv, BQ[g])], inc=ss)
        CS[g] = ss.n
        P.on("gpsimd", lambda e, sl=sl: e.tensor_scalar(
            out=sg_sb[:, sl, :], in0=sg_sb[:, sl, :], scalar1=65536.0,
            scalar2=MAGIC, op0=ALU.mult, op1=ALU.add), wait=[(ss, CS[g])], inc=qq)
        DQ[g] = qq.n

    def stE(g):
        sl = g % NS
        def fg2(e, g=g, sl=sl):
            e.tensor_scalar(out=sg_sb[:, sl, :], in0=sg_sb[:, sl, :], scalar1=MAGIC,
                            scalar2=65535.0, op0=ALU.subtract, op1=ALU.min)
            e.scalar_tensor_tensor(out=g2_sb[:, sl, :], in0=sg_sb[:, sl, :],
                                   scalar=1.0 / 65536.0, in1=gq_sb[:, sl, :],
                                   op0=ALU.mult, op1=ALU.mult)
            e.tensor_reduce(out=run_sb[:, 1:2], in_=g2_sb[:, sl, :], axis=AX,
                            op=ALU.max, apply_absolute_value=True)
            return e.tensor_max(out=run_sb[:, 0:1], in0=run_sb[:, 0:1],
                                in1=run_sb[:, 1:2])
        wg2 = [(qq, DQ[g])]
        if g >= NS:
            wg2.append((xsl[sl], 16 * ((g - NS) // NS + 1)))
        P.on("vector", fg2, wait=wg2, inc=pp)
        P.on("gpsimd", lambda e, g=g, sl=sl: e.dma_start(
            out=g2_o.rearrange("(a p) t -> p a t", p=128)[:, g, :],
            in_=g2_sb[:, sl, :]), wait=[(pp, g + 1)], inc=xsl[sl], k=16)

    for g in range(ICH + 2):
        if g < ICH:
            stA(g)
            stB(g)
        if 1 <= g <= ICH:
            stC(g - 1)
        if g >= 2:
            stE(g - 2)

    emit_absmax_export(P, run_sb[:, 0:1], mrow_sb, colscr, mg2_o,
                       [(pp, ICH)], bb, vv, xx)
    P.on("sync", lambda e: None,
         wait=[(xx, xx.n)] + [(x, x.n) for x in xsl])
    P.emit()
    return nc



def build_l5():
    nc = bass.Bass("TRN2", target_bir_lowering=False)
    g2_i = nc.dram_tensor("g2_i", [NI, TC], F32, kind="ExternalInput").ap()
    up_i = nc.dram_tensor("up_i", [NI, TC], F32, kind="ExternalInput").ap()
    mg2a = nc.dram_tensor("mg2a", [1, 8], F32, kind="ExternalInput").ap()
    mua = nc.dram_tensor("mua", [1, 8], F32, kind="ExternalInput").ap()
    op_o = nc.dram_tensor("op_o", [NI, TC], F32, kind="ExternalOutput").ap()
    mo_o = nc.dram_tensor("mo_o", [1, 1], F32, kind="ExternalOutput").ap()
    scr2 = nc.dram_tensor("scr2", [2], F32, kind="Internal").ap()
    scr2b = nc.dram_tensor("scr2b", [2], F32, kind="Internal").ap()
    colscr = nc.dram_tensor("colscr", [128], F32, kind="Internal").ap()

    g2t_sb = nc.alloc_sbuf_tensor("g2t_sb", [128, 3, TC], F32)
    upt_sb = nc.alloc_sbuf_tensor("upt_sb", [128, 3, TC], F32)
    q1_sb = nc.alloc_sbuf_tensor("q1_sb", [128, 3, TC], F32)
    q2_sb = nc.alloc_sbuf_tensor("q2_sb", [128, 3, TC], F32)
    op_sb = nc.alloc_sbuf_tensor("op_sb", [128, 3, TC], F32)
    maxa_sb = nc.alloc_sbuf_tensor("maxa_sb", [1, 2, 8], F32)
    srow_sb = nc.alloc_sbuf_tensor("srow_sb", [1, 4], F32)
    srow2_sb = nc.alloc_sbuf_tensor("srow2_sb", [1, 4], F32)
    scol_sb = nc.alloc_sbuf_tensor("scol_sb", [128, 1], F32)
    icol_sb = nc.alloc_sbuf_tensor("icol_sb", [128, 1], F32)
    scol2_sb = nc.alloc_sbuf_tensor("scol2_sb", [128, 1], F32)
    icol2_sb = nc.alloc_sbuf_tensor("icol2_sb", [128, 1], F32)
    run_sb = nc.alloc_sbuf_tensor("run_sb", [128, 2], F32)
    mrow_sb = nc.alloc_sbuf_tensor("mrow_sb", [1, 130], F32)
    ones1_sb = nc.alloc_sbuf_tensor("ones1_sb", [1, 128], F32)
    ps_sca = nc.alloc_psum_tensor("ps_sca", [128, 2], F32)
    ps_scb = nc.alloc_psum_tensor("ps_scb", [128, 2], F32)

    P = Prog(nc)
    ld = P.sem("ld")
    gld = [P.sem(f"gld{i}") for i in range(3)]
    uld = [P.sem(f"uld{i}") for i in range(3)]
    xsl = [P.sem(f"xsl{i}") for i in range(3)]
    vv = P.sem("vv")
    ss = P.sem("ss")
    pp = P.sem("pp")
    bb = P.sem("bb")
    xx = P.sem("xx")

    P.on("sync", lambda e: e.dma_start(out=maxa_sb[:, 0, :], in_=mg2a), inc=ld, k=16)
    P.on("sync", lambda e: e.dma_start(out=maxa_sb[:, 1, :], in_=mua), inc=ld, k=16)
    P.on("vector", lambda e: e.memset(ones1_sb[:], 1.0), inc=vv)
    B1 = emit_scale_prep(P, maxa_sb[:, 0, :], srow_sb, ones1_sb, ps_sca, scol_sb,
                         icol_sb, 32767.0, [(ld, 32), (vv, vv.n)], vv, bb)
    B2 = emit_scale_prep(P, maxa_sb[:, 1, :], srow2_sb, ones1_sb, ps_scb, scol2_sb,
                         icol2_sb, 32767.0, [(ld, 32)], vv, bb)
    P.on("vector", lambda e: e.memset(run_sb[:], 0.0), inc=vv)

    for g in range(ICH):
        sl = g % 3
        w = [(pp, g - 2)] if g >= 3 else []
        P.on("sync", lambda e, g=g, sl=sl: e.dma_start(
            out=g2t_sb[:, sl, :],
            in_=g2_i.rearrange("(a p) t -> p a t", p=128)[:, g, :]),
            wait=w, inc=gld[sl], k=16)
        P.on("scalar", lambda e, g=g, sl=sl: e.dma_start(
            out=upt_sb[:, sl, :],
            in_=up_i.rearrange("(a p) t -> p a t", p=128)[:, g, :]),
            wait=w, inc=uld[sl], k=16)
        P.on("scalar", lambda e, g=g, sl=sl: e.activation(
            out=q1_sb[:, sl, :], in_=g2t_sb[:, sl, :], func=AF.Copy, scale=icol_sb[:],
            bias=MAGIC),
            wait=[(gld[sl], 16 * (g // 3 + 1)), (bb, B2)]
                 + ([(pp, g - 2)] if g >= 3 else []), inc=ss)
        P.on("scalar", lambda e, g=g, sl=sl: e.activation(
            out=q2_sb[:, sl, :], in_=upt_sb[:, sl, :], func=AF.Copy, scale=icol2_sb[:],
            bias=MAGIC), wait=[(uld[sl], 16 * (g // 3 + 1))], inc=ss)
        S2 = ss.n
        def fq(e, sl=sl):
            e.tensor_scalar(out=q1_sb[:, sl, :], in0=q1_sb[:, sl, :], scalar1=MAGIC,
                            scalar2=scol_sb[:], op0=ALU.subtract, op1=ALU.mult)
            return e.tensor_scalar(out=q2_sb[:, sl, :], in0=q2_sb[:, sl, :],
                                   scalar1=MAGIC, scalar2=scol2_sb[:],
                                   op0=ALU.subtract, op1=ALU.mult)
        P.on("vector", fq, wait=[(ss, S2)], inc=vv)
        VQ = vv.n
        def fop(e, g=g, sl=sl):
            e.tensor_mul(out=op_sb[:, sl, :], in0=q1_sb[:, sl, :], in1=q2_sb[:, sl, :])
            e.tensor_reduce(out=run_sb[:, 1:2], in_=op_sb[:, sl, :], axis=AX,
                            op=ALU.max, apply_absolute_value=True)
            return e.tensor_max(out=run_sb[:, 0:1], in0=run_sb[:, 0:1],
                                in1=run_sb[:, 1:2])
        wo = [(vv, VQ)]
        if g >= 3:
            wo.append((xsl[sl], 16 * ((g - 3) // 3 + 1)))
        P.on("vector", fop, wait=wo, inc=pp)
        P.on("gpsimd", lambda e, g=g, sl=sl: e.dma_start(
            out=op_o.rearrange("(a p) t -> p a t", p=128)[:, g, :],
            in_=op_sb[:, sl, :]), wait=[(pp, g + 1)], inc=xsl[sl], k=16)

    emit_absmax_export(P, run_sb[:, 0:1], mrow_sb, colscr, mo_o,
                       [(pp, ICH)], bb, vv, xx)
    P.on("sync", lambda e: None,
         wait=[(xx, xx.n)] + [(x, x.n) for x in xsl])
    P.emit()
    return nc



def build_l6():
    nc = bass.Bass("TRN2", target_bir_lowering=False)
    op_i = nc.dram_tensor("op_i", [NI, TC], F32, kind="ExternalInput").ap()
    moa = nc.dram_tensor("moa", [1, 8], F32, kind="ExternalInput").ap()
    x2_i = nc.dram_tensor("x2_i", [C, TC], F32, kind="ExternalInput").ap()
    wdT_d = nc.dram_tensor("wdT_d", [NI, C], BF16, kind="ExternalInput").ap()
    mod_i = nc.dram_tensor("mod_i", [128, 48], F32, kind="ExternalInput").ap()
    out_o = nc.dram_tensor("out_o", [C, TC], F32, kind="ExternalOutput").ap()
    scr2 = nc.dram_tensor("scr2", [2], F32, kind="Internal").ap()

    wd_sb = nc.alloc_sbuf_tensor("wd_sb", [128, ICH, C], BF16)
    ot_sb = nc.alloc_sbuf_tensor("ot_sb", [128, 4, 512], F32)
    ob_sb = nc.alloc_sbuf_tensor("ob_sb", [128, 4, 512], BF16)
    oq_sb = nc.alloc_sbuf_tensor("oq_sb", [128, 512], F32)
    x2t_sb = nc.alloc_sbuf_tensor("x2t_sb", [128, 2, CCH, 512], F32)
    os_sb = nc.alloc_sbuf_tensor("os_sb", [128, 2, CCH, 512], F32)
    mod_sb = nc.alloc_sbuf_tensor("mod_sb", [128, 48], F32)
    gb_sb = nc.alloc_sbuf_tensor("gb_sb", [128, 2, CCH], F32)
    maxa_sb = nc.alloc_sbuf_tensor("maxa_sb", [1, 8], F32)
    srow_sb = nc.alloc_sbuf_tensor("srow_sb", [1, 4], F32)
    scol_sb = nc.alloc_sbuf_tensor("scol_sb", [128, 1], F32)
    icol_sb = nc.alloc_sbuf_tensor("icol_sb", [128, 1], F32)

    ps = [nc.alloc_psum_tensor(f"ps{i}", [128, 512], F32) for i in range(CCH)]

    P = Prog(nc)
    ld = P.sem("ld")
    wld = P.sem("wld")
    old_ = [P.sem(f"old{i}") for i in range(4)]
    xld = [P.sem(f"xld{i}") for i in range(2)]
    xos = [P.sem(f"xos{i}") for i in range(2)]
    vv = P.sem("vv")
    ss = P.sem("ss")
    qrd = P.sem("qrd")
    mm = P.sem("mm")
    evc = P.sem("evc")
    bb = P.sem("bb")
    xx = P.sem("xx")

    P.on("sync", lambda e: e.dma_start(out=maxa_sb[0:1, :], in_=moa), inc=ld, k=16)
    P.on("sync", lambda e: e.dma_start(out=mod_sb[:], in_=mod_i), inc=ld, k=16)
    P.on("sync", lambda e: e.dma_start(
        out=wd_sb[:], in_=wdT_d.rearrange("(a p) o -> p a o", p=128)), inc=wld, k=16)
    BB_SC = emit_scale_prep_dram(P, maxa_sb, srow_sb, scr2, scol_sb, icol_sb,
                                 32767.0, [(ld, 16)], vv, bb)
    P.on("scalar", lambda e: e.activation(out=gb_sb[:, 0], in_=mod_sb[:, 40:48],
                                          func=AF.Abs), wait=[(ld, 32)], inc=ss)
    P.on("scalar", lambda e: e.sign(out=gb_sb[:, 1], in_=mod_sb[:, 40:48]),
         wait=[], inc=ss)
    SS_GB = ss.n
    def fgb(e):
        e.tensor_scalar(out=gb_sb[:, 0], in0=gb_sb[:, 0], scalar1=0.25, scalar2=None,
                        op0=ALU.max)
        return e.tensor_mul(out=gb_sb[:, 0], in0=gb_sb[:, 1], in1=gb_sb[:, 0])
    P.on("vector", fgb, wait=[(ss, SS_GB)], inc=vv)
    VV_GB = vv.n

    for tt in range(NTT):
        w = [(evc, (tt - 1) * CCH)] if tt >= 2 else []
        P.on("sync", lambda e, tt=tt: e.dma_start(
            out=x2t_sb[:, tt % 2], in_=fm(x2_i)[:, :, tt * 512:(tt + 1) * 512]),
            wait=w, inc=xld[tt % 2], k=16)
        for ic in range(ICH):
            g = tt * ICH + ic
            w = [(qrd, g - 3)] if g >= 4 else []
            P.on("scalar", lambda e, ic=ic, tt=tt, g=g: e.dma_start(
                out=ot_sb[:, g % 4, :],
                in_=op_i.rearrange("(a p) t -> p a t", p=128)[:, ic,
                                                              tt * 512:(tt + 1) * 512]),
                wait=w, inc=old_[g % 4], k=16)
            def fq(e, g=g):
                e.tensor_scalar(out=oq_sb[:], in0=ot_sb[:, g % 4, :],
                                scalar1=icol_sb[:], scalar2=MAGIC,
                                op0=ALU.mult, op1=ALU.add)
                return e.tensor_scalar(out=ob_sb[:, g % 4, :], in0=oq_sb[:],
                                       scalar1=MAGIC, scalar2=scol_sb[:],
                                       op0=ALU.subtract, op1=ALU.mult)
            P.on("vector", fq,
                 wait=[(old_[g % 4], 16 * (g // 4 + 1)), (bb, BB_SC)]
                      + ([(mm, (g - 3) * CCH)] if g >= 4 else []), inc=qrd)
            def fmm(e, ic=ic, tt=tt, g=g):
                ins = None
                for oc2 in range(CCH):
                    ins = e.matmul(out=ps[oc2][:],
                                   lhsT=wd_sb[:, ic, oc2 * 128:(oc2 + 1) * 128],
                                   rhs=ob_sb[:, g % 4, :],
                                   start=(ic == 0), stop=(ic == ICH - 1))
                return ins
            wm_ = [(qrd, g + 1), (wld, 16)]
            if ic == 0 and tt > 0:
                wm_.append((evc, tt * CCH))
            P.on("tensor", fmm, wait=wm_, inc=mm, k=CCH)
        for oc2 in range(CCH):
            P.on("vector", lambda e, oc2=oc2, tt=tt: e.scalar_tensor_tensor(
                out=os_sb[:, tt % 2, oc2, :], in0=ps[oc2][:],
                scalar=gb_sb[:, 0, oc2:oc2 + 1], in1=x2t_sb[:, tt % 2, oc2, :],
                op0=ALU.mult, op1=ALU.add),
                wait=[(mm, (tt + 1) * ICH * CCH), (xld[tt % 2], 16 * (tt // 2 + 1)), (vv, VV_GB)]
                     + ([(xos[tt % 2], 16 * (tt // 2))] if tt >= 2 else []),
                inc=evc)
        P.on("gpsimd", lambda e, tt=tt: e.dma_start(
            out=fm(out_o)[:, :, tt * 512:(tt + 1) * 512], in_=os_sb[:, tt % 2]),
            wait=[(evc, (tt + 1) * CCH)], inc=xos[tt % 2], k=16)

    P.on("sync", lambda e: None,
         wait=[(xos[0], xos[0].n), (xos[1], xos[1].n)])
    P.emit()
    return nc



def _t(a):
    return np.ascontiguousarray(np.asarray(a).T)


def _cols(v):
    return np.ascontiguousarray(np.asarray(v).reshape(CCH, 128).T)


def la_maps(inp):
    x = np.asarray(inp["L0_attn_x"])[0]
    xT = _t(x)
    eye = np.eye(64, dtype=np.float32)
    maps = []
    for r in range(NCORES):
        lo, hi = r * TC, (r + 1) * TC
        xs = np.zeros((C, WH), np.float32)
        xs[:, 1:TC + 1] = xT[:, lo:hi]
        if r > 0:
            xs[:, 0] = xT[:, lo - 1]
        if r < NCORES - 1:
            xs[:, TC + 1] = xT[:, hi]
        maskLR = np.zeros((128, 2), np.float32)
        maskLR[:, 0] = 1.0 if r > 0 else 0.0
        maskLR[:, 1] = 1.0 if r < NCORES - 1 else 0.0
        maps.append({
            "xT": xs,
            "mixk": _cols(inp["L0_attn_mix_k"]),
            "mixv": _cols(inp["L0_attn_mix_v"]),
            "mixr": _cols(inp["L0_attn_mix_r"]),
            "cvec": _cols(np.asarray(inp["L0_c"])[0]),
            "wmod": np.asarray(inp["L0_wmod"]),
            "maskLR": maskLR,
            "eye": eye,
            "wup_sh": np.asarray(inp["L0_ffn_w_up"])[r * 384:(r + 1) * 384],
            "wgate_sh": np.asarray(inp["L0_ffn_w_gate"])[r * 384:(r + 1) * 384],
            "wdown_sh": np.asarray(inp["L0_ffn_w_down"])[r * 128:(r + 1) * 128],
        })
    return maps


def lb_maps(inp, ra):
    wk, wv, wr = _t(inp["L0_attn_Wk"]), _t(inp["L0_attn_Wv"]), _t(inp["L0_attn_Wr"])
    sd = _cols(inp["L0_attn_spatial_decay"])
    return [{
        "h_i": ra[r]["h_o"], "u_i": ra[r]["u_o"],
        "wkT": wk, "wvT": wv, "wrT": wr, "sdec": sd,
    } for r in range(NCORES)]


def l2_maps(inp, ra, rb):
    af_all = np.stack([rb[j]["afin_o"] for j in range(NCORES)], axis=2)
    bf_all = np.stack([rb[j]["bfin_o"] for j in range(NCORES)], axis=2)
    af_all = np.ascontiguousarray(af_all, np.float32)
    bf_all = np.ascontiguousarray(bf_all, np.float32)
    x = np.asarray(inp["L0_attn_x"])[0]
    xT = _t(x)
    wo = _t(inp["L0_attn_Wo"])
    sd = _cols(inp["L0_attn_spatial_decay"])
    sf = _cols(inp["L0_attn_spatial_first"])
    maps = []
    for r in range(NCORES):
        lo, hi = r * TC, (r + 1) * TC
        xs = np.zeros((C, WH), np.float32)
        xs[:, 1:TC + 1] = xT[:, lo:hi]
        pwv = np.zeros((128, 8), np.float32)
        pmk = np.zeros((128, 8), np.float32)
        for j in range(NCORES):
            if j < r:
                pwv[:, j] = (r - 1 - j) * (float(TC) / T)
                pmk[:, j] = 1.0
        maps.append({
            "xT": xs,
            "ek_i": rb[r]["ek_o"], "ekv_i": rb[r]["ekv_o"], "sr_i": rb[r]["sr_o"],
            "af_all": af_all, "bf_all": bf_all,
            "pw": pwv, "pmask": pmk,
            "woT": wo, "mod_i": ra[r]["mod_o"],
            "sdec": sd, "sfirst": sf,
        })
    return maps




def l3_maps(inp, ra, r2):
    wup_d = np.concatenate([ra[j]["wupd_o"] for j in range(NCORES)], axis=0)
    wgate_d = np.concatenate([ra[j]["wgated_o"] for j in range(NCORES)], axis=0)
    wupT = np.ascontiguousarray(wup_d.T)
    wgateT = np.ascontiguousarray(wgate_d.T)
    mh2a = np.ascontiguousarray(
        np.concatenate([r2[j]["mh2_o"] for j in range(NCORES)], axis=1))
    return [{
        "h2_i": r2[r]["h2_o"], "mh2a": mh2a,
        "wupT_d": wupT, "wgateT_d": wgateT,
    } for r in range(NCORES)]


def l4_maps(r3):
    mga = np.ascontiguousarray(
        np.concatenate([r3[j]["mg_o"] for j in range(NCORES)], axis=1))
    return [{"gate_i": r3[r]["gate_o"], "mga": mga} for r in range(NCORES)]


def l5_maps(r3, r4):
    mg2a = np.ascontiguousarray(
        np.concatenate([r4[j]["mg2_o"] for j in range(NCORES)], axis=1))
    mua = np.ascontiguousarray(
        np.concatenate([r3[j]["mu_o"] for j in range(NCORES)], axis=1))
    return [{"g2_i": r4[r]["g2_o"], "up_i": r3[r]["up_o"],
             "mg2a": mg2a, "mua": mua} for r in range(NCORES)]


def l6_maps(inp, ra, r2, r5):
    wdown_d = np.concatenate([ra[j]["wdownd_o"] for j in range(NCORES)], axis=0)
    wdT = np.ascontiguousarray(wdown_d.T)
    moa = np.ascontiguousarray(
        np.concatenate([r5[j]["mo_o"] for j in range(NCORES)], axis=1))
    return [{"op_i": r5[r]["op_o"], "moa": moa, "x2_i": r2[r]["x2_o"],
             "wdT_d": wdT, "mod_i": ra[r]["mod_o"]} for r in range(NCORES)]


def run_all(inp):
    cores = list(range(NCORES))
    ra = run_bass_kernel_spmd(build_la(), la_maps(inp), core_ids=cores).results
    rb = run_bass_kernel_spmd(build_lb(), lb_maps(inp, ra), core_ids=cores).results
    r2 = run_bass_kernel_spmd(build_l2(), l2_maps(inp, ra, rb), core_ids=cores).results
    r3 = run_bass_kernel_spmd(build_l3(), l3_maps(inp, ra, r2), core_ids=cores).results
    r4 = run_bass_kernel_spmd(build_l4(), l4_maps(r3), core_ids=cores).results
    r5 = run_bass_kernel_spmd(build_l5(), l5_maps(r3, r4), core_ids=cores).results
    r6 = run_bass_kernel_spmd(build_l6(), l6_maps(inp, ra, r2, r5), core_ids=cores).results
    return ra, rb, r2, r3, r4, r5, r6


def kernel(**inp):
    ra, rb, r2, r3, r4, r5, r6 = run_all(inp)
    out = np.concatenate(
        [np.asarray(r6[r]["out_o"]).T for r in range(NCORES)], axis=0)
    return np.ascontiguousarray(out).reshape(1, T, C)
